# version 08
# baseline (speedup 1.0000x reference)
import os
import numpy as np
import concourse.bass as bass
import concourse.mybir as mybir
from concourse.bass_utils import run_bass_kernel_spmd

F32 = mybir.dt.float32
BF16 = mybir.dt.bfloat16
AF = mybir.ActivationFunctionType
ALU = mybir.AluOpType
AX = mybir.AxisListType

P = 128
S, D, I = 4096, 1024, 2816
SH = 2048
NDC = D // P
KCH = 512
NKCH = S // KCH
NQT = SH // P
ICHUNKS = [(0, 512), (512, 512), (1024, 512), (1536, 512), (2048, 512), (2560, 256)]
NIT = I // P

ENGINES = ("sync", "gpsimd", "vector", "scalar", "tensor")


class Op:
    __slots__ = ("eng", "fn", "waits", "inc_keys", "inc_vals", "drain_after",
                 "dma", "_small", "seq")

    def __init__(self, eng, fn, dma=False):
        self.eng = eng
        self.fn = fn
        self.waits = []
        self.inc_keys = set()
        self.inc_vals = {}
        self.drain_after = False
        self.dma = dma
        self._small = False
        self.seq = -1


class Builder:
    def __init__(self):
        self.ops = {e: [] for e in ENGINES}
        self.buf = {}
        self.pending_waits = {e: [] for e in ENGINES}
        self.seq = 0
        self.strict = os.environ.get("BASS_SIM_FP64") == "1"

    def semkey(self, op):
        return ("dma_" + op.eng) if op.dma else op.eng

    def _edge(self, prod, cons):
        if prod is cons:
            return
        if (prod.eng == cons.eng and not prod.dma and prod.eng != "gpsimd"
                and not self.strict):
            return
        key = self.semkey(prod)
        prod.inc_keys.add(key)
        if not prod.dma and prod._small:
            prod.drain_after = True
        cons.waits.append((key, prod))

    def op(self, eng, fn, reads=(), writes=(), dma=False, small=False):
        o = Op(eng, fn, dma=dma)
        o._small = small
        if dma:
            o.inc_keys.add(self.semkey(o))
        if small and not dma:
            o.drain_after = True
        for bnm in reads:
            st = self.buf.get(bnm)
            if st is not None and st[0] is not None:
                self._edge(st[0], o)
        for bnm in writes:
            st = self.buf.get(bnm)
            if st is not None:
                if st[0] is not None:
                    self._edge(st[0], o)
                for r in st[1]:
                    self._edge(r, o)
        for bnm in writes:
            self.buf[bnm] = [o, []]
        for bnm in reads:
            st = self.buf.setdefault(bnm, [None, []])
            st[1].append(o)
        for w in self.pending_waits[eng]:
            o.waits.append(w)
        self.pending_waits[eng] = []
        o.seq = self.seq
        self.seq += 1
        self.ops[eng].append(o)
        return o

    def barrier(self):
        lasts = []
        for e in ENGINES:
            last_d, last_c = None, None
            for o in reversed(self.ops[e]):
                if o.dma and last_d is None:
                    last_d = o
                if not o.dma and last_c is None:
                    last_c = o
                if last_d is not None and last_c is not None:
                    break
            for last in (last_d, last_c):
                if last is None:
                    continue
                key = self.semkey(last)
                last.inc_keys.add(key)
                if not last.dma and last._small:
                    last.drain_after = True
                lasts.append((e, key, last))
        for e in ENGINES:
            for e2, key, last in lasts:
                if e2 != e:
                    self.pending_waits[e].append((key, last))

    def finalize_and_emit(self, block, sems):
        strict_dma = os.environ.get("BASS_SIM_FP64") == "1"
        counts = {k: 0 for k in sems}
        for e in ENGINES:
            for o in self.ops[e]:
                for key in sorted(o.inc_keys):
                    counts[key] += 16 if key.startswith("dma_") else 1
                    o.inc_vals[key] = counts[key]
        dma_hist = {}
        if strict_dma:
            for e in ENGINES:
                for o in self.ops[e]:
                    for key, v in o.inc_vals.items():
                        if key.startswith("dma_"):
                            dma_hist.setdefault(key, []).append((o.seq, v))
            for key in dma_hist:
                dma_hist[key].sort()

        def strict_val(key, consumer_seq, base):
            import bisect
            hist = dma_hist.get(key, [])
            i = bisect.bisect_left(hist, (consumer_seq, -1))
            return hist[i - 1][1] if i > 0 else base

        def emit_stream(eng_name):
            def run(eng):
                for o in self.ops[eng_name]:
                    wmax = {}
                    for (key, prod) in o.waits:
                        v = prod.inc_vals[key]
                        if strict_dma and key.startswith("dma_"):
                            v = strict_val(key, o.seq, v)
                        wmax[key] = max(wmax.get(key, 0), v)
                    if strict_dma and o.dma:
                        k0 = "dma_" + o.eng
                        prev = o.inc_vals[k0] - 16
                        if prev > 0:
                            wmax[k0] = max(wmax.get(k0, 0), prev)
                    for key in sorted(wmax):
                        eng.wait_ge(sems[key], wmax[key])
                    instr = o.fn(eng) if o.fn is not None else None
                    if o.drain_after and not o.dma:
                        instr = eng.drain()
                    if o.inc_keys:
                        keys = sorted(o.inc_keys)
                        assert len(keys) == 1, keys
                        assert instr is not None
                        instr.then_inc(sems[keys[0]], 16 if o.dma else 1)
            return run

        block.sync(emit_stream("sync"))
        block.gpsimd(emit_stream("gpsimd"))
        block.vector(emit_stream("vector"))
        block.scalar(emit_stream("scalar"))
        block.tensor(emit_stream("tensor"))


class Carver:
    def __init__(self, tensor, total):
        self.t = tensor
        self.total = total
        self.off = 0

    def take(self, n):
        n = (n + 15) // 16 * 16
        ap = self.t[:, self.off:self.off + n]
        self.off += n
        assert self.off <= self.total, f"arena overflow {self.off} > {self.total}"
        return ap

    def reset(self):
        self.off = 0


def build_nc():
    nc = bass.Bass("TRN2", target_bir_lowering=False)

    x_full = nc.dram_tensor("x_full", [S, D], F32, kind="ExternalInput").ap()
    x_q = nc.dram_tensor("x_q", [SH, D], F32, kind="ExternalInput").ap()
    x_ffn = nc.dram_tensor("x_ffn", [SH, D], F32, kind="ExternalInput").ap()
    w_in = {}
    for nm, shp in (("wq", [D, D]), ("wk", [D, D]), ("wv", [D, D]), ("wo", [D, D]),
                    ("wg", [I, D]), ("wu", [I, D]), ("wd", [D, I])):
        w_in[nm] = nc.dram_tensor(nm, shp, F32, kind="ExternalInput").ap()
    out_d = nc.dram_tensor("out", [SH, D], F32, kind="ExternalOutput").ap()

    kt_hi = nc.dram_tensor("kt_hi", [D, S], BF16, kind="Internal").ap()
    kt_lo = nc.dram_tensor("kt_lo", [D, S], BF16, kind="Internal").ap()
    v_hi = nc.dram_tensor("v_hi", [S, D], BF16, kind="Internal").ap()
    v_lo = nc.dram_tensor("v_lo", [S, D], BF16, kind="Internal").ap()
    qt_hi = nc.dram_tensor("qt_hi", [D, SH], BF16, kind="Internal").ap()
    qt_lo = nc.dram_tensor("qt_lo", [D, SH], BF16, kind="Internal").ap()
    fin_d = nc.dram_tensor("fin_d", [SH, D], F32, kind="Internal").ap()
    wgq_hi = nc.dram_tensor("wgq_hi", [D, I], BF16, kind="Internal").ap()
    wgq_lo = nc.dram_tensor("wgq_lo", [D, I], BF16, kind="Internal").ap()
    wuq_hi = nc.dram_tensor("wuq_hi", [D, I], BF16, kind="Internal").ap()
    wuq_lo = nc.dram_tensor("wuq_lo", [D, I], BF16, kind="Internal").ap()
    wdq_hi = nc.dram_tensor("wdq_hi", [I, D], BF16, kind="Internal").ap()
    wdq_lo = nc.dram_tensor("wdq_lo", [I, D], BF16, kind="Internal").ap()

    from contextlib import ExitStack
    es = ExitStack()

    A32E = 28 * 1024 - 384
    A16E = 48 * 1024
    a32t = es.enter_context(nc.sbuf_tensor("a32", [P, A32E], F32))
    a16t = es.enter_context(nc.sbuf_tensor("a16", [P, A16E], BF16))
    ident = es.enter_context(nc.sbuf_tensor("ident", [P, P], F32))
    smalls = es.enter_context(nc.sbuf_tensor("smalls", [P, 160], F32))

    c0 = nc.const_aps.tensor(0.0, [P, 1], F32)
    c1 = nc.const_aps.tensor(1.0, [P, 1], F32)
    c_regs = [(c0, 0.0), (c1, 1.0)]
    for i, v in enumerate((0.125, 1e-5)):
        t = es.enter_context(nc.sbuf_tensor("c_%d" % i, [P, 1], F32))
        nc.const_aps.register(F32, v, t[:])
        c_regs.append((t[:], v))

    sems = {}
    for e in ENGINES:
        sems[e] = es.enter_context(nc.semaphore("s_" + e))
        sems["dma_" + e] = es.enter_context(nc.semaphore("sd_" + e))

    ps = [es.enter_context(nc.psum_tensor("ps%d" % i, [P, 512], F32))
          for i in range(8)]

    b = Builder()

    def sm(i, k=1):
        return smalls[:, i:i + k]

    r6 = sm(0)
    lnp = [[sm(2 + i) for i in range(5)] + [sm(104 + i) for i in range(5)],
           [sm(109 + i) for i in range(10)]]
    fl_cm = [sm(12), sm(13)]
    fl_cm8 = [sm(14), sm(15)]
    fl_al = [sm(16), sm(17)]
    fl_rs = [sm(18), sm(19)]
    fl_nm = [sm(100), sm(101)]
    fl_tmp = [sm(102), sm(103)]
    fl_mA = [sm(20 + t) for t in range(NQT)]
    fl_mB = [sm(36 + t) for t in range(NQT)]
    fl_lA = [sm(52 + t) for t in range(NQT)]
    fl_lB = [sm(68 + t) for t in range(NQT)]
    fl_linv = [sm(84 + t) for t in range(NQT)]

    b.op("gpsimd", lambda g: g.memset(ident[:], 0.0), writes=["identm"])

    def prolog(g):
        r = g.affine_select(out=ident[:], in_=ident[:], compare_op=ALU.not_equal,
                            fill=1.0, base=0, pattern=[[-1, P]],
                            channel_multiplier=1)
        for ap, v in c_regs:
            r = g.memset(ap, v)
        return g.memset(smalls[:], 0.0)

    b.op("gpsimd", prolog, reads=["identm"], writes=["ident", "consts", "smalls_i"])
    b.op("gpsimd", lambda g: g.memset(r6, 6.0), reads=["smalls_i"], writes=["r6raw"])
    b.op("vector", lambda v: v.reciprocal(sm(1), r6), reads=["r6raw"],
         writes=["r6t"], small=True)
    b.op("vector", lambda v: v.tensor_copy(r6, sm(1)), reads=["r6t"],
         writes=["r6"], small=True)
    b.barrier()

    def emit_ln(xb, xname, xnb, xnname, scr, scrname, pr=0):
        t_sum, t_ssq, t_m, t_e2, t_m2, t_var, t_sd, t_rstd, t_nm, t_nmr = lnp[pr]
        q = f"_{pr}"
        b.op("vector", lambda v: v.reduce_sum(t_sum, xb, axis=AX.X),
             reads=[xname], writes=["t_sum" + q], small=True)
        b.op("scalar", lambda s: s.activation(scr, xb, AF.Square, accum_out=t_ssq),
             reads=[xname], writes=[scrname, "t_ssq" + q], small=True)
        b.op("vector", lambda v: v.tensor_scalar(t_m, t_sum, 1.0 / D, None, ALU.mult),
             reads=["t_sum" + q], writes=["t_m" + q], small=True)
        b.op("vector", lambda v: v.tensor_scalar(t_e2, t_ssq, 1.0 / D, None, ALU.mult),
             reads=["t_ssq" + q], writes=["t_e2" + q], small=True)
        b.op("vector", lambda v: v.tensor_tensor(t_m2, t_m, t_m, ALU.mult),
             reads=["t_m" + q], writes=["t_m2" + q], small=True)
        b.op("vector", lambda v: v.tensor_tensor(t_var, t_e2, t_m2, ALU.subtract),
             reads=["t_e2" + q, "t_m2" + q], writes=["t_var" + q], small=True)
        b.op("scalar", lambda s: s.activation(t_sd, t_var, AF.Sqrt, bias=1e-5),
             reads=["t_var" + q], writes=["t_sd" + q], small=True)
        b.op("vector", lambda v: v.reciprocal(t_rstd, t_sd),
             reads=["t_sd" + q], writes=["t_rstd" + q], small=True)
        b.op("vector", lambda v: v.tensor_tensor(t_nm, t_m, t_rstd, ALU.mult),
             reads=["t_m" + q, "t_rstd" + q], writes=["t_nm" + q], small=True)
        b.op("vector", lambda v: v.tensor_scalar(t_nmr, t_nm, -1.0, None, ALU.mult),
             reads=["t_nm" + q], writes=["t_nmr" + q], small=True)
        b.op("scalar", lambda s: s.activation(xnb, xb, AF.Identity,
                                              bias=t_nmr, scale=t_rstd),
             reads=[xname, "t_nmr" + q, "t_rstd" + q], writes=[xnname])

    def emit_transpose_split(src, src_name, nch, dst_fn, dst_names, tag):
        for g in range(0, nch, 4):
            n = min(4, nch - g)
            bi = 6 + (g // 4) % 2
            bank = ps[bi]
            pname = f"psb{bi}"

            def mk_tp(g=g, n=n, bank=bank):
                def f(te):
                    r = None
                    for j in range(n):
                        r = te.transpose(bank[:, 128 * j:128 * (j + 1)],
                                         src[:, 128 * (g + j):128 * (g + j + 1)],
                                         ident[:])
                    return r
                return f
            b.op("tensor", mk_tp(), reads=[src_name, "ident"], writes=[pname])

            def mk_hi(g=g, n=n, bank=bank):
                def f(se):
                    r = None
                    for j in range(n):
                        hi, _ = dst_fn(g + j)
                        r = se.copy(hi, bank[:, 128 * j:128 * (j + 1)])
                    return r
                return f

            def mk_lo(g=g, n=n, bank=bank):
                def f(v):
                    r = None
                    for j in range(n):
                        hi, lo = dst_fn(g + j)
                        r = v.tensor_tensor(lo, bank[:, 128 * j:128 * (j + 1)],
                                            hi.bitcast(BF16), ALU.subtract)
                    return r
                return f
            b.op("scalar", mk_hi(), reads=[pname], writes=dst_names)
            b.op("vector", mk_lo(), reads=[pname] + dst_names, writes=dst_names)

    def emit_quant(xbuf, xq_out, F, scr, names_in, name_out, tag, scrtag=None):
        tag = scrtag or tag
        NB = F // 16
        absx, u, acc, cmp_t, w1 = (scr[k] for k in
                                   ("absx", "u", "acc", "cmp", "t1"))
        w2 = absx
        s_s, s_sc, s_thr = scr["s"], scr["sc"], scr["thr"]
        b.op("scalar", lambda s: s.activation(absx, xbuf, AF.Abs),
             reads=names_in, writes=[f"{tag}absx"])
        b.op("vector", lambda v: v.tensor_reduce(
            s_s, absx.rearrange("p (nb k) -> p nb k", k=16), axis=AX.X, op=ALU.max),
            reads=[f"{tag}absx"], writes=[f"{tag}s"], small=True)
        b.op("vector", lambda v: v.tensor_tensor(s_thr, s_s, r6.broadcast_to([P, NB]),
                                                 ALU.mult),
             reads=[f"{tag}s", "r6"], writes=[f"{tag}sct"], small=True)
        b.op("vector", lambda v: v.tensor_scalar(s_sc, s_thr, 1e-12, None, ALU.max),
             reads=[f"{tag}sct"], writes=[f"{tag}sc"], small=True)
        b.op("vector", lambda v: v.reciprocal(s_thr, s_sc),
             reads=[f"{tag}sc"], writes=[f"{tag}thr"], small=True)
        b.op("vector", lambda v: v.tensor_tensor(
            u.rearrange("p (nb k) -> p nb k", k=16),
            absx.rearrange("p (nb k) -> p nb k", k=16),
            s_thr.unsqueeze(2).broadcast_to([P, NB, 16]), ALU.mult),
            reads=[f"{tag}absx", f"{tag}thr"], writes=[f"{tag}u"])
        uu = f"{tag}u"
        b.op("vector", lambda v: v.tensor_scalar(acc, u, 0.25, None, ALU.is_gt),
             reads=[uu], writes=[f"{tag}acc"])
        b.op("vector", lambda v: v.scalar_tensor_tensor(cmp_t, u, 0.75, acc,
                                                        ALU.is_gt, ALU.add),
             reads=[uu, f"{tag}acc"], writes=[f"{tag}cmp"])
        b.op("vector", lambda v: v.scalar_tensor_tensor(acc, u, 1.25, cmp_t,
                                                        ALU.is_gt, ALU.add),
             reads=[uu, f"{tag}cmp"], writes=[f"{tag}acc"])
        b.op("vector", lambda v: v.scalar_tensor_tensor(cmp_t, u, 1.75, acc,
                                                        ALU.is_gt, ALU.add),
             reads=[uu, f"{tag}acc"], writes=[f"{tag}cmp"])
        b.op("vector", lambda v: v.tensor_scalar(w1, u, 2.5, None, ALU.is_gt),
             reads=[uu], writes=[f"{tag}w1"])
        b.op("vector", lambda v: v.scalar_tensor_tensor(w2, u, 3.5, w1,
                                                        ALU.is_gt, ALU.add),
             reads=[uu, f"{tag}w1"], writes=[f"{tag}absx"])
        b.op("vector", lambda v: v.scalar_tensor_tensor(acc, cmp_t, 0.5, w2,
                                                        ALU.mult, ALU.add),
             reads=[f"{tag}cmp", f"{tag}absx"], writes=[f"{tag}acc"])
        b.op("vector", lambda v: v.tensor_scalar(w1, u, 5.0, None, ALU.is_gt),
             reads=[uu], writes=[f"{tag}w1"])
        b.op("vector", lambda v: v.scalar_tensor_tensor(cmp_t, w1, 2.0, acc,
                                                        ALU.mult, ALU.add),
             reads=[f"{tag}w1", f"{tag}acc"], writes=[f"{tag}cmp"])
        b.op("vector", lambda v: v.tensor_tensor(
            u.rearrange("p (nb k) -> p nb k", k=16),
            cmp_t.rearrange("p (nb k) -> p nb k", k=16),
            s_sc.unsqueeze(2).broadcast_to([P, NB, 16]), ALU.mult),
            reads=[f"{tag}cmp", f"{tag}sc"], writes=[f"{tag}u"])
        b.op("vector", lambda v: v.tensor_scalar(w1, xbuf, 0.0, 2.0,
                                                 ALU.is_ge, ALU.mult),
             reads=names_in, writes=[f"{tag}w1"])
        b.op("vector", lambda v: v.scalar_tensor_tensor(xq_out, w1, -1.0, u,
                                                        ALU.add, ALU.mult),
             reads=[f"{tag}w1", f"{tag}u"], writes=[name_out])

    def emit_mm3(bank_ap, pname, pairs, reads, nfirst=True):
        def f(te):
            r = None
            n = len(pairs) * 3
            i = 0
            for (lh, ll, rh, rl) in pairs:
                for (a, c) in ((lh, rh), (lh, rl), (ll, rh)):
                    r = te.matmul(bank_ap, a, c, start=(i == 0), stop=(i == n - 1))
                    i += 1
            return r
        b.op("tensor", f, reads=list(reads), writes=[pname])

    c32 = Carver(a32t, A32E)
    c16 = Carver(a16t, A16E)
    wload = c32.take(8 * 1024)
    xtile = [c32.take(1024), c32.take(1024)]
    xntile = [c32.take(1024), c32.take(1024)]
    lnscr = [c32.take(1024), c32.take(1024)]
    wspl_hi = c16.take(8 * 1024)
    wspl_lo = c16.take(8 * 1024)
    win_hi = [c16.take(NDC * 512), c16.take(NDC * 512)]
    win_lo = [c16.take(NDC * 512), c16.take(NDC * 512)]
    st_hi = c16.take(NDC * 512)
    st_lo = c16.take(NDC * 512)

    def emit_weight_T_split(w_dram, tag):
        for i in range(8):
            b.op("sync", (lambda i=i: lambda s: s.dma_start(
                out=wload[:, 1024 * i:1024 * (i + 1)],
                in_=w_dram[128 * i:128 * (i + 1), :]))(),
                writes=[f"{tag}wl{i}"], dma=True)
        for i in range(8):
            def dst(jd, i=i):
                o = 1024 * jd + 128 * i
                return (wspl_hi[:, o:o + 128], wspl_lo[:, o:o + 128])
            emit_transpose_split(wload[:, 1024 * i:1024 * (i + 1)], f"{tag}wl{i}",
                                 8, dst, ["wspl"], f"{tag}t{i}")

    def p1_sweep(x_dram, nsup, mode, hi_dram, lo_dram, tag, hook=None):
        for sup in range(nsup):
            wpar = sup % 2
            winh, winl = win_hi[wpar], win_lo[wpar]
            wnm = "win%d" % wpar
            for tt in range(4):
                gt = sup * 4 + tt
                xb, xnb = xtile[tt % 2], xntile[tt % 2]
                xname, xnname = f"{tag}x{tt % 2}", f"{tag}xn{tt % 2}"
                b.op("sync", (lambda gt=gt, xb=xb: lambda s: s.dma_start(
                    out=xb, in_=x_dram[128 * gt:128 * (gt + 1), :]))(),
                    writes=[xname], dma=True)
                emit_ln(xb, xname, xnb, xnname, lnscr[tt % 2], "lnscr%d" % (tt % 2),
                        pr=tt % 2)

                def dst(jd, tt=tt, winh=winh, winl=winl):
                    o = 512 * jd + 128 * tt
                    return (winh[:, o:o + 128], winl[:, o:o + 128])
                emit_transpose_split(xnb, xnname, 8, dst, [wnm], f"{tag}w{gt}")
            if mode in ("K", "Q"):
                for ie in range(8):
                    bank = ps[ie % 3]
                    pname = f"psk{ie % 3}"
                    pairs = [(wspl_hi[:, 1024 * jd + 128 * ie:1024 * jd + 128 * (ie + 1)],
                              wspl_lo[:, 1024 * jd + 128 * ie:1024 * jd + 128 * (ie + 1)],
                              winh[:, 512 * jd:512 * (jd + 1)],
                              winl[:, 512 * jd:512 * (jd + 1)]) for jd in range(NDC)]
                    emit_mm3(bank[:, :], pname, pairs, ["wspl", wnm])
                    b.op("scalar", (lambda bank=bank, ie=ie: lambda s: s.copy(
                        st_hi[:, 512 * ie:512 * (ie + 1)], bank[:, :]))(),
                        reads=[pname], writes=["stg"])
                    b.op("vector", (lambda bank=bank, ie=ie: lambda v: v.tensor_tensor(
                        st_lo[:, 512 * ie:512 * (ie + 1)], bank[:, :],
                        st_hi[:, 512 * ie:512 * (ie + 1)].bitcast(BF16),
                        ALU.subtract))(),
                        reads=[pname, "stg"], writes=["stg"])
                slen = S if mode == "K" else SH
                for dram, stg in ((hi_dram, st_hi), (lo_dram, st_lo)):
                    b.op("gpsimd", (lambda dram=dram, stg=stg, sup=sup, slen=slen:
                                    lambda g: g.dma_start(
                        out=bass.AP(dram.tensor, 512 * sup,
                                    [[slen, P], [P * slen, 8], [1, 512]]),
                        in_=stg.rearrange("p (j n) -> p j n", n=512)))(),
                        reads=["stg"], writes=["std"], dma=True)
                if hook:
                    hook()
            else:
                for tt in range(4):
                    for eh in range(2):
                        bank = ps[(2 * tt + eh) % 3]
                        pname = f"psk{(2 * tt + eh) % 3}"
                        pairs = [(winh[:, 512 * jd + 128 * tt:512 * jd + 128 * (tt + 1)],
                                  winl[:, 512 * jd + 128 * tt:512 * jd + 128 * (tt + 1)],
                                  wspl_hi[:, 1024 * jd + 512 * eh:1024 * jd + 512 * (eh + 1)],
                                  wspl_lo[:, 1024 * jd + 512 * eh:1024 * jd + 512 * (eh + 1)])
                                 for jd in range(NDC)]
                        emit_mm3(bank[:, :], pname, pairs, ["wspl", wnm])
                        o = 1024 * tt + 512 * eh
                        b.op("scalar", (lambda bank=bank, o=o: lambda s: s.copy(
                            st_hi.bitcast(BF16)[:, o:o + 512], bank[:, :]))(),
                            reads=[pname], writes=["stg"])
                        b.op("vector", (lambda bank=bank, o=o: lambda v: v.tensor_tensor(
                            st_lo.bitcast(BF16)[:, o:o + 512], bank[:, :],
                            st_hi.bitcast(BF16)[:, o:o + 512], ALU.subtract))(),
                            reads=[pname, "stg"], writes=["stg"])
                for dram, stg in ((hi_dram, st_hi), (lo_dram, st_lo)):
                    b.op("gpsimd", (lambda dram=dram, stg=stg, sup=sup:
                                    lambda g: g.dma_start(
                        out=bass.AP(dram.tensor, 512 * sup * D,
                                    [[D, P], [P * D, 4], [1, 1024]]),
                        in_=stg.rearrange("p (j n) -> p j n", n=1024)))(),
                        reads=["stg"], writes=["std"], dma=True)
            if hook:
                hook()

    pw_load = c32.take(2816)
    pw_q = c32.take(2816)
    pq_scr = {k: c32.take(1408) for k in ("absx", "u", "acc", "cmp", "t1")}
    pq_sm = {k: c32.take(96) for k in ("s", "sc", "thr")}
    pw_st_hi = c16.take(NIT * 128)
    pw_st_lo = c16.take(NIT * 128)

    def qscr(F):
        NB = F // 16
        d = {k: pq_scr[k][:, 0:F] for k in pq_scr}
        d.update({k: pq_sm[k][:, 0:NB] for k in pq_sm})
        return d

    prep_steps = []

    def prep_gu_step(w_dram, out_hi_d, out_lo_d, tag, ri):
        b.op("sync", (lambda ri=ri: lambda s: s.dma_start(
            out=pw_load[:, 0:1024], in_=w_dram[128 * ri:128 * (ri + 1), :]))(),
            writes=["pwl"], dma=True)
        emit_quant(pw_load[:, 0:1024], pw_q[:, 0:1024], 1024, qscr(1024),
                   ["pwl"], "pwq", f"{tag}q", scrtag="pwscr")

        def dst(jd):
            return (pw_st_hi[:, 128 * jd:128 * (jd + 1)],
                    pw_st_lo[:, 128 * jd:128 * (jd + 1)])
        emit_transpose_split(pw_q[:, 0:1024], "pwq", 8, dst,
                             ["pwst"], f"{tag}t{ri}")
        for dram, stg in ((out_hi_d, pw_st_hi), (out_lo_d, pw_st_lo)):
            b.op("gpsimd", (lambda dram=dram, stg=stg, ri=ri: lambda g: g.dma_start(
                out=bass.AP(dram.tensor, 128 * ri,
                            [[I, P], [P * I, 8], [1, 128]]),
                in_=stg.rearrange("p (j n) -> p j n", n=128)[:, 0:8, :]))(),
                reads=["pwst"], writes=[f"{tag}d"], dma=True)

    def prep_wd_step(rd):
        b.op("sync", (lambda rd=rd: lambda s: s.dma_start(
            out=pw_load[:, 0:2816], in_=w_in["wd"][128 * rd:128 * (rd + 1), :]))(),
            writes=["pwl"], dma=True)
        for hf in range(2):
            F = 1408
            emit_quant(pw_load[:, F * hf:F * (hf + 1)], pw_q[:, F * hf:F * (hf + 1)],
                       F, qscr(F), ["pwl"], "pwq", f"pdq{hf}", scrtag="pwscr")

        def dstD(ji):
            return (pw_st_hi[:, 128 * ji:128 * (ji + 1)],
                    pw_st_lo[:, 128 * ji:128 * (ji + 1)])
        emit_transpose_split(pw_q[:, 0:2816], "pwq", NIT, dstD, ["pwst"],
                             f"pdt{rd}")
        for dram, stg in ((wdq_hi, pw_st_hi), (wdq_lo, pw_st_lo)):
            for half in range(2):
                n0 = 11 * half
                b.op("gpsimd", (lambda dram=dram, stg=stg, rd=rd, n0=n0:
                                lambda g: g.dma_start(
                    out=bass.AP(dram.tensor, 128 * n0 * D + 128 * rd,
                                [[D, P], [P * D, 11], [1, 128]]),
                    in_=stg.rearrange("p (j n) -> p j n", n=128)[:, n0:n0 + 11, :]))(),
                    reads=["pwst"], writes=["pdd"], dma=True)

    for ri in range(NIT):
        prep_steps.append((lambda ri=ri: prep_gu_step(w_in["wg"], wgq_hi, wgq_lo, "pg", ri)))
    for ri in range(NIT):
        prep_steps.append((lambda ri=ri: prep_gu_step(w_in["wu"], wuq_hi, wuq_lo, "pu", ri)))
    for rd in range(8):
        prep_steps.append((lambda rd=rd: prep_wd_step(rd)))
    prep_iter = iter(prep_steps)

    def prep_hook(n=1):
        for _ in range(n):
            st = next(prep_iter, None)
            if st is not None:
                st()

    emit_weight_T_split(w_in["wk"], "k")
    p1_sweep(x_full, 8, "K", kt_hi, kt_lo, "f", hook=prep_hook)
    b.barrier()
    emit_weight_T_split(w_in["wv"], "v")
    p1_sweep(x_full, 8, "V", v_hi, v_lo, "g", hook=prep_hook)
    b.barrier()
    emit_weight_T_split(w_in["wq"], "q")
    p1_sweep(x_q, 4, "Q", qt_hi, qt_lo, "h", hook=prep_hook)
    while True:
        st = next(prep_iter, None)
        if st is None:
            break
        st()
    b.barrier()

    c32.reset()
    c16.reset()
    acc = c32.take(NQT * 1024)
    p_sb = [c32.take(512), c32.take(512)]
    av_tmp = c32.take(1024)
    av_tmp2 = c32.take(1024)
    kch_hi = [c16.take(NDC * 512), c16.take(NDC * 512)]
    kch_lo = [c16.take(NDC * 512), c16.take(NDC * 512)]
    vch_hi = [c16.take(4 * 1024), c16.take(4 * 1024)]
    vch_lo = [c16.take(4 * 1024), c16.take(4 * 1024)]
    qtb_hi = [c16.take(NDC * 128), c16.take(NDC * 128)]
    qtb_lo = [c16.take(NDC * 128), c16.take(NDC * 128)]
    at_hi = [c16.take(4 * 128), c16.take(4 * 128)]
    at_lo = [c16.take(4 * 128), c16.take(4 * 128)]

    def mk_3d_load(dram, dst, off, nblk, blklen, rowstride, bufstride=None):
        bs = bufstride or blklen

        def f(s):
            return s.dma_start(
                out=dst.rearrange("p (j n) -> p j n", n=bs)[:, 0:nblk, 0:blklen],
                in_=bass.AP(dram.tensor, off,
                            [[rowstride, P], [P * rowstride, nblk], [1, blklen]]))
        return f

    for c in range(NKCH):
        cp = c % 2
        b.op("sync", mk_3d_load(kt_hi, kch_hi[cp], 512 * c, 8, 512, S),
             reads=["std"], writes=[f"kch{cp}"], dma=True)
        b.op("sync", mk_3d_load(kt_lo, kch_lo[cp], 512 * c, 8, 512, S),
             reads=["std"], writes=[f"kch{cp}"], dma=True)
        b.op("sync", mk_3d_load(v_hi, vch_hi[cp], 512 * c * D, 4, 1024, D),
             reads=["std"], writes=[f"vch{cp}"], dma=True)
        b.op("sync", mk_3d_load(v_lo, vch_lo[cp], 512 * c * D, 4, 1024, D),
             reads=["std"], writes=[f"vch{cp}"], dma=True)
        for t in range(NQT):
            tp = t % 2
            b.op("sync", mk_3d_load(qt_hi, qtb_hi[tp], 128 * t, 8, 128, SH),
                 reads=["std"], writes=[f"qtb{tp}"], dma=True)
            b.op("sync", mk_3d_load(qt_lo, qtb_lo[tp], 128 * t, 8, 128, SH),
                 reads=["std"], writes=[f"qtb{tp}"], dma=True)
            sbank = ps[tp]
            sname = f"pss{tp}"
            pairs = [(qtb_hi[tp][:, 128 * jd:128 * (jd + 1)],
                      qtb_lo[tp][:, 128 * jd:128 * (jd + 1)],
                      kch_hi[cp][:, 512 * jd:512 * (jd + 1)],
                      kch_lo[cp][:, 512 * jd:512 * (jd + 1)]) for jd in range(NDC)]
            emit_mm3(sbank[:, :], sname, pairs, [f"qtb{tp}", f"kch{cp}"])
            b.op("vector", (lambda sbank=sbank, tp=tp: lambda v: v.reduce_max(
                fl_cm[tp], sbank[:, :], axis=AX.X))(),
                reads=[sname], writes=[f"fl_cm{tp}"], small=True)
            b.op("vector", (lambda tp=tp: lambda v: v.tensor_scalar(
                fl_cm8[tp], fl_cm[tp], 0.125, None, ALU.mult))(),
                reads=[f"fl_cm{tp}"], writes=[f"fl_cm8{tp}"], small=True)
            src_m = fl_mB[t] if c % 2 == 1 else fl_mA[t]
            dst_m = fl_mA[t] if c % 2 == 1 else fl_mB[t]
            if c == 0:
                b.op("vector", (lambda dst_m=dst_m, tp=tp: lambda v: v.tensor_copy(
                    dst_m, fl_cm8[tp]))(),
                    reads=[f"fl_cm8{tp}"], writes=[f"fl_m{t}"], small=True)
            else:
                b.op("vector", (lambda src_m=src_m, tp=tp: lambda v: v.tensor_tensor(
                    fl_tmp[tp], src_m, fl_cm8[tp], ALU.max))(),
                    reads=[f"fl_m{t}", f"fl_cm8{tp}"], writes=[f"fl_tmp{tp}"], small=True)
                b.op("vector", (lambda dst_m=dst_m, tp=tp: lambda v: v.tensor_copy(
                    dst_m, fl_tmp[tp]))(),
                    reads=[f"fl_tmp{tp}"], writes=[f"fl_m{t}"], small=True)
            cur_m = dst_m
            b.op("vector", (lambda cur_m=cur_m, tp=tp: lambda v: v.tensor_scalar(
                fl_nm[tp], cur_m, -1.0, None, ALU.mult))(),
                reads=[f"fl_m{t}"], writes=[f"fl_nm{tp}"], small=True)
            b.op("scalar", (lambda sbank=sbank, tp=tp: lambda s: s.activation(
                p_sb[tp], sbank[:, :], AF.Exp, bias=fl_nm[tp], scale=0.125,
                accum_out=fl_rs[tp]))(),
                reads=[sname, f"fl_nm{tp}"], writes=[f"p_sb{tp}", f"fl_rs{tp}"], small=True)
            if c == 0:
                b.op("vector", (lambda t=t, tp=tp: lambda v: v.tensor_copy(
                    fl_lB[t], fl_rs[tp]))(),
                    reads=[f"fl_rs{tp}"], writes=[f"fl_l{t}"], small=True)
            else:
                src_l = fl_lB[t] if c % 2 == 1 else fl_lA[t]
                dst_l = fl_lA[t] if c % 2 == 1 else fl_lB[t]
                b.op("vector", (lambda src_m=src_m, cur_m=cur_m, tp=tp: lambda v:
                                v.tensor_tensor(fl_tmp[tp], src_m, cur_m,
                                                ALU.subtract))(),
                     reads=[f"fl_m{t}"], writes=[f"fl_tmp{tp}"], small=True)
                b.op("scalar", (lambda tp=tp: lambda s: s.activation(
                    fl_al[tp], fl_tmp[tp], AF.Exp))(),
                    reads=[f"fl_tmp{tp}"], writes=[f"fl_al{tp}"], small=True)
                b.op("vector", (lambda src_l=src_l, tp=tp: lambda v: v.tensor_tensor(
                    fl_tmp[tp], src_l, fl_al[tp], ALU.mult))(),
                    reads=[f"fl_l{t}", f"fl_al{tp}"], writes=[f"fl_tmp{tp}"], small=True)
                b.op("vector", (lambda dst_l=dst_l, tp=tp: lambda v: v.tensor_tensor(
                    dst_l, fl_tmp[tp], fl_rs[tp], ALU.add))(),
                    reads=[f"fl_tmp{tp}", f"fl_rs{tp}"], writes=[f"fl_l{t}"], small=True)
            tbank = ps[2 + tp]
            tname = f"pst{tp}"
            b.op("tensor", (lambda tp=tp, tbank=tbank: lambda te: [
                te.transpose(tbank[:, 128 * j:128 * (j + 1)],
                             p_sb[tp][:, 128 * j:128 * (j + 1)], ident[:])
                for j in range(4)][-1])(),
                reads=[f"p_sb{tp}", "ident"], writes=[tname])
            b.op("scalar", (lambda tbank=tbank, tp=tp: lambda s: s.copy(
                at_hi[tp], tbank[:, :]))(),
                reads=[tname], writes=[f"at{tp}"])
            b.op("vector", (lambda tbank=tbank, tp=tp: lambda v: v.tensor_tensor(
                at_lo[tp], tbank[:, :], at_hi[tp], ALU.subtract))(),
                reads=[tname, f"at{tp}"], writes=[f"at{tp}"])
            for eh in range(2):
                abank = ps[4 + 2 * tp + eh]
                aname = f"psa{2 * tp + eh}"
                pairs = [(at_hi[tp][:, 128 * ks:128 * (ks + 1)],
                          at_lo[tp][:, 128 * ks:128 * (ks + 1)],
                          vch_hi[cp][:, 1024 * ks + 512 * eh:1024 * ks + 512 * (eh + 1)],
                          vch_lo[cp][:, 1024 * ks + 512 * eh:1024 * ks + 512 * (eh + 1)])
                         for ks in range(4)]
                emit_mm3(abank[:, :], aname, pairs, [f"at{tp}", f"vch{cp}"])
                aslice = acc[:, 1024 * t + 512 * eh:1024 * t + 512 * (eh + 1)]
                accn = f"acc{t}_{eh}"
                if c == 0:
                    b.op("vector", (lambda abank=abank, aslice=aslice: lambda v:
                                    v.tensor_copy(aslice, abank[:, :]))(),
                         reads=[aname], writes=[accn])
                else:
                    tsl = (av_tmp if tp == 0 else av_tmp2)[:, 512 * eh:512 * (eh + 1)]
                    b.op("scalar", (lambda aslice=aslice, tsl=tsl, tp=tp: lambda s:
                                    s.activation(tsl, aslice, AF.Identity,
                                                 scale=fl_al[tp]))(),
                         reads=[accn, f"fl_al{tp}"], writes=[f"avt{tp}{eh}"])
                    b.op("vector", (lambda abank=abank, aslice=aslice, tsl=tsl:
                                    lambda v: v.tensor_tensor(
                                        aslice, tsl, abank[:, :], ALU.add))(),
                         reads=[f"avt{tp}{eh}", aname], writes=[accn])

    b.barrier()
    c16.reset()
    g_hi = c16.take(NQT * 1024)
    g_lo = c16.take(NQT * 1024)
    woT_hi = c16.take(8 * 1024)
    woT_lo = c16.take(8 * 1024)
    last_l = fl_lB if (NKCH - 1) % 2 == 0 else fl_lA
    for t in range(NQT):
        b.op("vector", (lambda t=t: lambda v: v.reciprocal(
            fl_linv[t], last_l[t]))(),
            reads=[f"fl_l{t}"], writes=[f"fl_linv{t}"], small=True)
        b.op("vector", (lambda t=t: lambda v: v.tensor_tensor(
            av_tmp, acc[:, 1024 * t:1024 * (t + 1)],
            fl_linv[t].broadcast_to([P, 1024]), ALU.mult))(),
            reads=[f"acc{t}_0", f"acc{t}_1", f"fl_linv{t}"], writes=["avn"])
        b.op("vector", (lambda t=t: lambda v: v.tensor_copy(
            g_hi[:, 1024 * t:1024 * (t + 1)], av_tmp))(),
            reads=["avn"], writes=["grhs"])
        b.op("vector", (lambda t=t: lambda v: v.tensor_tensor(
            g_lo[:, 1024 * t:1024 * (t + 1)], av_tmp,
            g_hi[:, 1024 * t:1024 * (t + 1)], ALU.subtract))(),
            reads=["avn", "grhs"], writes=["grhs"])

    b.barrier()
    c32.reset()
    wload2 = c32.take(8 * 1024)
    gtmp = c32.take(1024)
    finst = c32.take(1024)
    for i in range(8):
        b.op("sync", (lambda i=i: lambda s: s.dma_start(
            out=wload2[:, 1024 * i:1024 * (i + 1)],
            in_=w_in["wo"][128 * i:128 * (i + 1), :]))(),
            writes=[f"owl{i}"], dma=True)
    for i in range(8):
        def dst(jd, i=i):
            o = 1024 * jd + 128 * i
            return (woT_hi[:, o:o + 128], woT_lo[:, o:o + 128])
        emit_transpose_split(wload2[:, 1024 * i:1024 * (i + 1)], f"owl{i}",
                             8, dst, ["woT"], f"ot{i}")

    for r in range(2):
        for g in range(8):
            for ie in range(8):
                bank = ps[ie % 2]
                pname = f"psg{ie % 2}"
                pairs = [(woT_hi[:, 1024 * jd + 128 * ie:1024 * jd + 128 * (ie + 1)],
                          woT_lo[:, 1024 * jd + 128 * ie:1024 * jd + 128 * (ie + 1)],
                          g_hi[:, 1024 * (8 * r + jd) + 128 * g:
                               1024 * (8 * r + jd) + 128 * (g + 1)],
                          g_lo[:, 1024 * (8 * r + jd) + 128 * g:
                               1024 * (8 * r + jd) + 128 * (g + 1)])
                         for jd in range(NDC)]
                emit_mm3(bank[:, 0:128], pname, pairs, ["woT", "grhs"])
                b.op("vector", (lambda bank=bank, ie=ie: lambda v: v.tensor_copy(
                    gtmp[:, 128 * ie:128 * (ie + 1)], bank[:, 0:128]))(),
                    reads=[pname], writes=["gtmp"])
            for ie in range(0, 8, 4):
                bi = 2 + (ie // 4) % 2
                bank = ps[bi]
                pname = f"psq{bi}"
                b.op("tensor", (lambda ie=ie, bank=bank: lambda te: [
                    te.transpose(bank[:, 128 * j:128 * (j + 1)],
                                 gtmp[:, 128 * (ie + j):128 * (ie + j + 1)],
                                 ident[:])
                    for j in range(4)][-1])(),
                    reads=["gtmp", "ident"], writes=[pname])
                b.op("vector", (lambda ie=ie, bank=bank: lambda v: v.tensor_copy(
                    finst[:, 128 * ie:128 * (ie + 4)], bank[:, :]))(),
                    reads=[pname], writes=["finst"])
            gg = 8 * r + g
            b.op("gpsimd", (lambda gg=gg: lambda gp: gp.dma_start(
                out=fin_d[128 * gg:128 * (gg + 1), :], in_=finst))(),
                reads=["finst"], writes=["find"], dma=True)

    b.barrier()

    c32.reset()
    c16.reset()
    ffn_acc = c32.take(8 * 1024)
    fx = c32.take(1024)
    fxn = c32.take(1024)
    fq = c32.take(1024)
    f_lnscr = c32.take(1024)
    f_scr = {k: c32.take(1024) for k in ("absx", "u", "acc", "cmp", "t1", "t2")}
    f_sm = {k: c32.take(64) for k in ("s", "sc", "thr")}
    h_sb = c32.take(512)
    hq_sb = c32.take(512)
    silu_sb = c32.take(512)
    finld = c32.take(1024)
    res1 = c32.take(1024)
    dn_tmp = c32.take(1024)
    res2 = c32.take(1024)

    xqT_hi = c16.take(NDC * 1024)
    xqT_lo = c16.take(NDC * 1024)
    _wg1, _wg2 = c16.take(NDC * 512), c16.take(NDC * 512)
    _wu1, _wu2 = c16.take(NDC * 512), c16.take(NDC * 512)
    _wd1, _wd2 = c16.take(4 * 1024), c16.take(4 * 1024)
    wgc_hi, wgc_lo = [_wg1, _wg1], [_wg2, _wg2]
    wuc_hi, wuc_lo = [_wu1, _wu1], [_wu2, _wu2]
    wdc_hi, wdc_lo = [_wd1, _wd1], [_wd2, _wd2]
    hqT_hi = c16.take(4 * 128)
    hqT_lo = c16.take(4 * 128)

    def fscr(F):
        NB = F // 16
        d = {k: f_scr[k][:, 0:F] for k in f_scr}
        d.update({k: f_sm[k][:, 0:NB] for k in f_sm})
        return d

    for grp in range(2):
        for tt in range(8):
            gg = 8 * grp + tt
            b.op("sync", (lambda gg=gg: lambda s: s.dma_start(
                out=fx, in_=x_ffn[128 * gg:128 * (gg + 1), :]))(),
                writes=["fx"], dma=True)
            emit_ln(fx, "fx", fxn, "fxn", f_lnscr, "flnscr", pr=0)
            emit_quant(fxn, fq, 1024, fscr(1024), ["fxn"], "fq", "fq", scrtag="fscr")

            def dst(jd, tt=tt):
                o = 1024 * jd + 128 * tt
                return (xqT_hi[:, o:o + 128], xqT_lo[:, o:o + 128])
            emit_transpose_split(fq, "fq", 8, dst, ["xqT"], f"xt{gg}")

        for ci, (io, ilen) in enumerate(ICHUNKS):
            wp = 0
            nsub = ilen // 128
            b.op("sync", mk_3d_load(wgq_hi, wgc_hi[wp], io, 8, ilen, I, 512),
                 reads=["pgd"], writes=[f"wgc{wp}"], dma=True)
            b.op("sync", mk_3d_load(wgq_lo, wgc_lo[wp], io, 8, ilen, I, 512),
                 reads=["pgd"], writes=[f"wgc{wp}"], dma=True)
            b.op("sync", mk_3d_load(wuq_hi, wuc_hi[wp], io, 8, ilen, I, 512),
                 reads=["pud"], writes=[f"wuc{wp}"], dma=True)
            b.op("sync", mk_3d_load(wuq_lo, wuc_lo[wp], io, 8, ilen, I, 512),
                 reads=["pud"], writes=[f"wuc{wp}"], dma=True)
            b.op("sync", mk_3d_load(wdq_hi, wdc_hi[wp], io * D, nsub, 1024, D),
                 reads=["pdd"], writes=[f"wdc{wp}"], dma=True)
            b.op("sync", mk_3d_load(wdq_lo, wdc_lo[wp], io * D, nsub, 1024, D),
                 reads=["pdd"], writes=[f"wdc{wp}"], dma=True)
            for tt in range(8):
                gbank, ubank = ps[tt % 2], ps[2 + tt % 2]
                gname, uname = f"psG{tt % 2}", f"psU{tt % 2}"
                wsl = [(512 * jd, 512 * jd + ilen) for jd in range(NDC)]
                gpairs = [(xqT_hi[:, 1024 * jd + 128 * tt:1024 * jd + 128 * (tt + 1)],
                           xqT_lo[:, 1024 * jd + 128 * tt:1024 * jd + 128 * (tt + 1)],
                           wgc_hi[wp][:, a:z], wgc_lo[wp][:, a:z])
                          for jd, (a, z) in enumerate(wsl)]
                upairs = [(xqT_hi[:, 1024 * jd + 128 * tt:1024 * jd + 128 * (tt + 1)],
                           xqT_lo[:, 1024 * jd + 128 * tt:1024 * jd + 128 * (tt + 1)],
                           wuc_hi[wp][:, a:z], wuc_lo[wp][:, a:z])
                          for jd, (a, z) in enumerate(wsl)]
                emit_mm3(gbank[:, 0:ilen], gname, gpairs, ["xqT", f"wgc{wp}"])
                emit_mm3(ubank[:, 0:ilen], uname, upairs, ["xqT", f"wuc{wp}"])
                b.op("scalar", (lambda gbank=gbank, ilen=ilen: lambda s:
                                s.activation(silu_sb[:, 0:ilen],
                                             gbank[:, 0:ilen], AF.Silu))(),
                     reads=[gname], writes=["silu"])
                b.op("vector", (lambda ubank=ubank, ilen=ilen: lambda v:
                                v.tensor_tensor(h_sb[:, 0:ilen],
                                                silu_sb[:, 0:ilen],
                                                ubank[:, 0:ilen], ALU.mult))(),
                     reads=["silu", uname], writes=["h"])
                emit_quant(h_sb[:, 0:ilen], hq_sb[:, 0:ilen], ilen, fscr(ilen),
                           ["h"], "hq", "hq", scrtag="fscr")
                tb = ps[4 + tt % 2]
                tn = f"psH{tt % 2}"
                b.op("tensor", (lambda tb=tb, nsub=nsub: lambda te: [
                    te.transpose(tb[:, 128 * j:128 * (j + 1)],
                                 hq_sb[:, 128 * j:128 * (j + 1)], ident[:])
                    for j in range(nsub)][-1])(),
                    reads=["hq", "ident"], writes=[tn])
                b.op("scalar", (lambda tb=tb, nsub=nsub: lambda s: s.copy(
                    hqT_hi[:, 0:128 * nsub], tb[:, 0:128 * nsub]))(),
                    reads=[tn], writes=["hqT"])
                b.op("vector", (lambda tb=tb, nsub=nsub: lambda v: v.tensor_tensor(
                    hqT_lo[:, 0:128 * nsub], tb[:, 0:128 * nsub],
                    hqT_hi[:, 0:128 * nsub], ALU.subtract))(),
                    reads=[tn, "hqT"], writes=["hqT"])
                for dh in range(2):
                    dbank = ps[6 + (tt + dh) % 2]
                    dname = f"psD{(tt + dh) % 2}"
                    pairs = [(hqT_hi[:, 128 * ks:128 * (ks + 1)],
                              hqT_lo[:, 128 * ks:128 * (ks + 1)],
                              wdc_hi[wp][:, 1024 * ks + 512 * dh:
                                         1024 * ks + 512 * (dh + 1)],
                              wdc_lo[wp][:, 1024 * ks + 512 * dh:
                                         1024 * ks + 512 * (dh + 1)])
                             for ks in range(nsub)]
                    emit_mm3(dbank[:, :], dname, pairs, ["hqT", f"wdc{wp}"])
                    fslice = ffn_acc[:, 1024 * tt + 512 * dh:
                                     1024 * tt + 512 * (dh + 1)]
                    fname = f"facc{tt}_{dh}"
                    if ci == 0:
                        b.op("vector", (lambda dbank=dbank, fslice=fslice:
                                        lambda v: v.tensor_copy(fslice,
                                                                dbank[:, :]))(),
                             reads=[dname], writes=[fname])
                    else:
                        tsl = dn_tmp[:, 512 * dh:512 * (dh + 1)]
                        b.op("scalar", (lambda fslice=fslice, tsl=tsl: lambda s:
                                        s.copy(tsl, fslice))(),
                             reads=[fname], writes=[f"dnt{dh}"])
                        b.op("vector", (lambda dbank=dbank, fslice=fslice,
                                        tsl=tsl: lambda v: v.tensor_tensor(
                                            fslice, tsl, dbank[:, :], ALU.add))(),
                             reads=[f"dnt{dh}", dname], writes=[fname])
        for tt in range(8):
            gg = 8 * grp + tt
            b.op("sync", (lambda gg=gg: lambda s: s.dma_start(
                out=finld, in_=fin_d[128 * gg:128 * (gg + 1), :]))(),
                reads=["find"], writes=["finld"], dma=True)
            b.op("sync", (lambda gg=gg: lambda s: s.dma_start(
                out=fx, in_=x_ffn[128 * gg:128 * (gg + 1), :]))(),
                writes=["fx"], dma=True)
            b.op("vector", lambda v: v.tensor_tensor(res1, fx, finld, ALU.add),
                 reads=["fx", "finld"], writes=["res1"])
            b.op("vector", (lambda tt=tt: lambda v: v.tensor_tensor(
                res2, res1, ffn_acc[:, 1024 * tt:1024 * (tt + 1)], ALU.add))(),
                reads=["res1", f"facc{tt}_0", f"facc{tt}_1"], writes=["res2"])
            b.op("gpsimd", (lambda gg=gg: lambda g: g.dma_start(
                out=out_d[128 * gg:128 * (gg + 1), :], in_=res2))(),
                reads=["res2"], writes=["outd"], dma=True)
        if grp == 0:
            b.barrier()

    final_out = b.buf["outd"][0]
    key = "dma_gpsimd"
    final_out.inc_keys.add(key)
    wait_op = b.op("sync", None)
    wait_op.waits.append((key, final_out))

    with nc.Block() as block:
        b.finalize_and_emit(block, sems)
    es.close()
    return nc


_NC_CACHE = None


def kernel(L0_attn_q, L0_attn_wq, L0_attn_wk, L0_attn_wv, L0_attn_wo,
           L0_ffn_w_gate, L0_ffn_w_up, L0_ffn_w_down):
    global _NC_CACHE
    x = np.ascontiguousarray(np.asarray(L0_attn_q, dtype=np.float32))
    ws = {
        "wq": np.ascontiguousarray(np.asarray(L0_attn_wq, dtype=np.float32)),
        "wk": np.ascontiguousarray(np.asarray(L0_attn_wk, dtype=np.float32)),
        "wv": np.ascontiguousarray(np.asarray(L0_attn_wv, dtype=np.float32)),
        "wo": np.ascontiguousarray(np.asarray(L0_attn_wo, dtype=np.float32)),
        "wg": np.ascontiguousarray(np.asarray(L0_ffn_w_gate, dtype=np.float32)),
        "wu": np.ascontiguousarray(np.asarray(L0_ffn_w_up, dtype=np.float32)),
        "wd": np.ascontiguousarray(np.asarray(L0_ffn_w_down, dtype=np.float32)),
    }
    B = x.shape[0]
    if _NC_CACHE is None:
        _NC_CACHE = build_nc()
    in_maps = []
    for b_ in range(B):
        for h in range(2):
            toks = np.concatenate([np.arange(1024) * 4 + 2 * h,
                                   np.arange(1024) * 4 + 2 * h + 1])
            m = dict(ws)
            m["x_full"] = x[b_]
            m["x_q"] = np.ascontiguousarray(x[b_][2048 * h:2048 * h + 2048])
            m["x_ffn"] = np.ascontiguousarray(x[b_][toks])
            in_maps.append(m)
    res = run_bass_kernel_spmd(_NC_CACHE, in_maps, core_ids=list(range(8)))
    out = np.empty((B, S, D), dtype=np.float32)
    for b_ in range(B):
        for h in range(2):
            o = np.asarray(res.results[2 * b_ + h]["out"])
            v = o.reshape(2, 1024, D)
            ob = out[b_].reshape(1024, 4, D)
            ob[:, 2 * h, :] = v[0]
            ob[:, 2 * h + 1, :] = v[1]
    return out


# version 09
# speedup vs baseline: 1.2200x; 1.2200x over previous
import os
import numpy as np
import concourse.bass as bass
import concourse.mybir as mybir
from concourse.bass_utils import run_bass_kernel_spmd

F32 = mybir.dt.float32
BF16 = mybir.dt.bfloat16
AF = mybir.ActivationFunctionType
ALU = mybir.AluOpType
AX = mybir.AxisListType

P = 128
S, D, I = 4096, 1024, 2816
SH = 2048
NDC = D // P
KCH = 512
NKCH = S // KCH
NQT = SH // P
ICHUNKS = [(0, 512), (512, 512), (1024, 512), (1536, 512), (2048, 512), (2560, 256)]
NIT = I // P

ENGINES = ("sync", "gpsimd", "vector", "scalar", "tensor")


class Op:
    __slots__ = ("eng", "fn", "waits", "inc_keys", "inc_vals", "drain_after",
                 "dma", "_small", "seq")

    def __init__(self, eng, fn, dma=False):
        self.eng = eng
        self.fn = fn
        self.waits = []
        self.inc_keys = set()
        self.inc_vals = {}
        self.drain_after = False
        self.dma = dma
        self._small = False
        self.seq = -1


class Builder:
    def __init__(self):
        self.ops = {e: [] for e in ENGINES}
        self.buf = {}
        self.pending_waits = {e: [] for e in ENGINES}
        self.seq = 0
        self.strict = os.environ.get("BASS_SIM_FP64") == "1"

    def semkey(self, op):
        return ("dma_" + op.eng) if op.dma else op.eng

    def _edge(self, prod, cons):
        if prod is cons:
            return
        if (prod.eng == cons.eng and not prod.dma and prod.eng != "gpsimd"
                and not self.strict):
            return
        key = self.semkey(prod)
        prod.inc_keys.add(key)
        if not prod.dma and prod._small:
            prod.drain_after = True
        cons.waits.append((key, prod))

    def op(self, eng, fn, reads=(), writes=(), dma=False, small=False):
        o = Op(eng, fn, dma=dma)
        o._small = small
        if dma:
            o.inc_keys.add(self.semkey(o))
        if small and not dma:
            o.drain_after = True
        for bnm in reads:
            st = self.buf.get(bnm)
            if st is not None and st[0] is not None:
                self._edge(st[0], o)
        for bnm in writes:
            st = self.buf.get(bnm)
            if st is not None:
                if st[0] is not None:
                    self._edge(st[0], o)
                for r in st[1]:
                    self._edge(r, o)
        for bnm in writes:
            self.buf[bnm] = [o, []]
        for bnm in reads:
            st = self.buf.setdefault(bnm, [None, []])
            st[1].append(o)
        for w in self.pending_waits[eng]:
            o.waits.append(w)
        self.pending_waits[eng] = []
        o.seq = self.seq
        self.seq += 1
        self.ops[eng].append(o)
        return o

    def barrier(self):
        lasts = []
        for e in ENGINES:
            last_d, last_c = None, None
            for o in reversed(self.ops[e]):
                if o.dma and last_d is None:
                    last_d = o
                if not o.dma and last_c is None:
                    last_c = o
                if last_d is not None and last_c is not None:
                    break
            for last in (last_d, last_c):
                if last is None:
                    continue
                key = self.semkey(last)
                last.inc_keys.add(key)
                if not last.dma and last._small:
                    last.drain_after = True
                lasts.append((e, key, last))
        for e in ENGINES:
            for e2, key, last in lasts:
                if e2 != e:
                    self.pending_waits[e].append((key, last))

    def finalize_and_emit(self, block, sems):
        strict_dma = os.environ.get("BASS_SIM_FP64") == "1"
        counts = {k: 0 for k in sems}
        for e in ENGINES:
            for o in self.ops[e]:
                for key in sorted(o.inc_keys):
                    counts[key] += 16 if key.startswith("dma_") else 1
                    o.inc_vals[key] = counts[key]
        dma_hist = {}
        if strict_dma:
            for e in ENGINES:
                for o in self.ops[e]:
                    for key, v in o.inc_vals.items():
                        if key.startswith("dma_"):
                            dma_hist.setdefault(key, []).append((o.seq, v))
            for key in dma_hist:
                dma_hist[key].sort()

        def strict_val(key, consumer_seq, base):
            import bisect
            hist = dma_hist.get(key, [])
            i = bisect.bisect_left(hist, (consumer_seq, -1))
            return hist[i - 1][1] if i > 0 else base

        def emit_stream(eng_name):
            def run(eng):
                for o in self.ops[eng_name]:
                    wmax = {}
                    for (key, prod) in o.waits:
                        v = prod.inc_vals[key]
                        if strict_dma and key.startswith("dma_"):
                            v = strict_val(key, o.seq, v)
                        wmax[key] = max(wmax.get(key, 0), v)
                    if strict_dma and o.dma:
                        k0 = "dma_" + o.eng
                        prev = o.inc_vals[k0] - 16
                        if prev > 0:
                            wmax[k0] = max(wmax.get(k0, 0), prev)
                    for key in sorted(wmax):
                        eng.wait_ge(sems[key], wmax[key])
                    instr = o.fn(eng) if o.fn is not None else None
                    if o.drain_after and not o.dma:
                        instr = eng.drain()
                    if o.inc_keys:
                        keys = sorted(o.inc_keys)
                        assert len(keys) == 1, keys
                        assert instr is not None
                        instr.then_inc(sems[keys[0]], 16 if o.dma else 1)
            return run

        block.sync(emit_stream("sync"))
        block.gpsimd(emit_stream("gpsimd"))
        block.vector(emit_stream("vector"))
        block.scalar(emit_stream("scalar"))
        block.tensor(emit_stream("tensor"))


class Carver:
    def __init__(self, tensor, total):
        self.t = tensor
        self.total = total
        self.off = 0

    def take(self, n):
        n = (n + 15) // 16 * 16
        ap = self.t[:, self.off:self.off + n]
        self.off += n
        assert self.off <= self.total, f"arena overflow {self.off} > {self.total}"
        return ap

    def reset(self):
        self.off = 0


def build_nc():
    nc = bass.Bass("TRN2", target_bir_lowering=False)

    x_full = nc.dram_tensor("x_full", [S, D], F32, kind="ExternalInput").ap()
    x_q = nc.dram_tensor("x_q", [SH, D], F32, kind="ExternalInput").ap()
    x_ffn = nc.dram_tensor("x_ffn", [SH, D], F32, kind="ExternalInput").ap()
    w_in = {}
    for nm, shp in (("wq", [D, D]), ("wk", [D, D]), ("wv", [D, D]), ("wo", [D, D]),
                    ("wg", [I, D]), ("wu", [I, D]), ("wd", [D, I])):
        w_in[nm] = nc.dram_tensor(nm, shp, F32, kind="ExternalInput").ap()
    out_d = nc.dram_tensor("out", [SH, D], F32, kind="ExternalOutput").ap()

    kt_hi = nc.dram_tensor("kt_hi", [D, S], BF16, kind="Internal").ap()
    kt_lo = nc.dram_tensor("kt_lo", [D, S], BF16, kind="Internal").ap()
    v_hi = nc.dram_tensor("v_hi", [S, D], BF16, kind="Internal").ap()
    v_lo = nc.dram_tensor("v_lo", [S, D], BF16, kind="Internal").ap()
    qt_hi = nc.dram_tensor("qt_hi", [D, SH], BF16, kind="Internal").ap()
    qt_lo = nc.dram_tensor("qt_lo", [D, SH], BF16, kind="Internal").ap()
    fin_d = nc.dram_tensor("fin_d", [SH, D], F32, kind="Internal").ap()
    wgq_hi = nc.dram_tensor("wgq_hi", [D, I], BF16, kind="Internal").ap()
    wgq_lo = nc.dram_tensor("wgq_lo", [D, I], BF16, kind="Internal").ap()
    wuq_hi = nc.dram_tensor("wuq_hi", [D, I], BF16, kind="Internal").ap()
    wuq_lo = nc.dram_tensor("wuq_lo", [D, I], BF16, kind="Internal").ap()
    wdq_hi = nc.dram_tensor("wdq_hi", [I, D], BF16, kind="Internal").ap()
    wdq_lo = nc.dram_tensor("wdq_lo", [I, D], BF16, kind="Internal").ap()

    from contextlib import ExitStack
    es = ExitStack()

    A32E = 28 * 1024 - 384
    A16E = 48 * 1024
    a32t = es.enter_context(nc.sbuf_tensor("a32", [P, A32E], F32))
    a16t = es.enter_context(nc.sbuf_tensor("a16", [P, A16E], BF16))
    ident = es.enter_context(nc.sbuf_tensor("ident", [P, P], F32))
    smalls = es.enter_context(nc.sbuf_tensor("smalls", [P, 160], F32))

    c0 = nc.const_aps.tensor(0.0, [P, 1], F32)
    c1 = nc.const_aps.tensor(1.0, [P, 1], F32)
    c_regs = [(c0, 0.0), (c1, 1.0)]
    for i, v in enumerate((0.125, 1e-5)):
        t = es.enter_context(nc.sbuf_tensor("c_%d" % i, [P, 1], F32))
        nc.const_aps.register(F32, v, t[:])
        c_regs.append((t[:], v))

    sems = {}
    for e in ENGINES:
        sems[e] = es.enter_context(nc.semaphore("s_" + e))
        sems["dma_" + e] = es.enter_context(nc.semaphore("sd_" + e))

    ps = [es.enter_context(nc.psum_tensor("ps%d" % i, [P, 512], F32))
          for i in range(8)]

    b = Builder()

    def sm(i, k=1):
        return smalls[:, i:i + k]

    r6 = sm(0)
    lnp = [[sm(2 + i) for i in range(5)] + [sm(104 + i) for i in range(5)],
           [sm(109 + i) for i in range(10)]]
    fl_cm = [sm(12), sm(13)]
    fl_cm8 = [sm(14), sm(15)]
    fl_al = [sm(16), sm(17)]
    fl_rs = [sm(18), sm(19)]
    fl_nm = [sm(100), sm(101)]
    fl_tmp = [sm(102), sm(103)]
    fl_mA = [sm(20 + t) for t in range(NQT)]
    fl_mB = [sm(36 + t) for t in range(NQT)]
    fl_lA = [sm(52 + t) for t in range(NQT)]
    fl_lB = [sm(68 + t) for t in range(NQT)]
    fl_linv = [sm(84 + t) for t in range(NQT)]

    b.op("gpsimd", lambda g: g.memset(ident[:], 0.0), writes=["identm"])

    def prolog(g):
        r = g.affine_select(out=ident[:], in_=ident[:], compare_op=ALU.not_equal,
                            fill=1.0, base=0, pattern=[[-1, P]],
                            channel_multiplier=1)
        for ap, v in c_regs:
            r = g.memset(ap, v)
        return g.memset(smalls[:], 0.0)

    b.op("gpsimd", prolog, reads=["identm"], writes=["ident", "consts", "smalls_i"])
    b.op("gpsimd", lambda g: g.memset(r6, 6.0), reads=["smalls_i"], writes=["r6raw"])
    b.op("vector", lambda v: v.reciprocal(sm(1), r6), reads=["r6raw"],
         writes=["r6t"], small=True)
    b.op("vector", lambda v: v.tensor_copy(r6, sm(1)), reads=["r6t"],
         writes=["r6"], small=True)
    b.barrier()

    def emit_ln(xb, xname, xnb, xnname, scr, scrname, pr=0):
        t_sum, t_ssq, t_m, t_e2, t_m2, t_var, t_sd, t_rstd, t_nm, t_nmr = lnp[pr]
        q = f"_{pr}"
        b.op("vector", lambda v: v.reduce_sum(t_sum, xb, axis=AX.X),
             reads=[xname], writes=["t_sum" + q], small=True)
        b.op("scalar", lambda s: s.activation(scr, xb, AF.Square, accum_out=t_ssq),
             reads=[xname], writes=[scrname, "t_ssq" + q], small=True)
        b.op("vector", lambda v: v.tensor_scalar(t_m, t_sum, 1.0 / D, None, ALU.mult),
             reads=["t_sum" + q], writes=["t_m" + q], small=True)
        b.op("vector", lambda v: v.tensor_scalar(t_e2, t_ssq, 1.0 / D, None, ALU.mult),
             reads=["t_ssq" + q], writes=["t_e2" + q], small=True)
        b.op("vector", lambda v: v.tensor_tensor(t_m2, t_m, t_m, ALU.mult),
             reads=["t_m" + q], writes=["t_m2" + q], small=True)
        b.op("vector", lambda v: v.tensor_tensor(t_var, t_e2, t_m2, ALU.subtract),
             reads=["t_e2" + q, "t_m2" + q], writes=["t_var" + q], small=True)
        b.op("scalar", lambda s: s.activation(t_sd, t_var, AF.Sqrt, bias=1e-5),
             reads=["t_var" + q], writes=["t_sd" + q], small=True)
        b.op("vector", lambda v: v.reciprocal(t_rstd, t_sd),
             reads=["t_sd" + q], writes=["t_rstd" + q], small=True)
        b.op("vector", lambda v: v.tensor_tensor(t_nm, t_m, t_rstd, ALU.mult),
             reads=["t_m" + q, "t_rstd" + q], writes=["t_nm" + q], small=True)
        b.op("vector", lambda v: v.tensor_scalar(t_nmr, t_nm, -1.0, None, ALU.mult),
             reads=["t_nm" + q], writes=["t_nmr" + q], small=True)
        b.op("scalar", lambda s: s.activation(xnb, xb, AF.Identity,
                                              bias=t_nmr, scale=t_rstd),
             reads=[xname, "t_nmr" + q, "t_rstd" + q], writes=[xnname])

    def emit_transpose_split(src, src_name, nch, dst_fn, dst_names, tag):
        for g in range(0, nch, 4):
            n = min(4, nch - g)
            bi = 6 + (g // 4) % 2
            bank = ps[bi]
            pname = f"psb{bi}"

            def mk_tp(g=g, n=n, bank=bank):
                def f(te):
                    r = None
                    for j in range(n):
                        r = te.transpose(bank[:, 128 * j:128 * (j + 1)],
                                         src[:, 128 * (g + j):128 * (g + j + 1)],
                                         ident[:])
                    return r
                return f
            b.op("tensor", mk_tp(), reads=[src_name, "ident"], writes=[pname])

            def mk_hi(g=g, n=n, bank=bank):
                def f(se):
                    r = None
                    for j in range(n):
                        hi, _ = dst_fn(g + j)
                        r = se.copy(hi, bank[:, 128 * j:128 * (j + 1)])
                    return r
                return f

            def mk_lo(g=g, n=n, bank=bank):
                def f(v):
                    r = None
                    for j in range(n):
                        hi, lo = dst_fn(g + j)
                        r = v.tensor_tensor(lo, bank[:, 128 * j:128 * (j + 1)],
                                            hi.bitcast(BF16), ALU.subtract)
                    return r
                return f
            b.op("scalar", mk_hi(), reads=[pname], writes=dst_names)
            b.op("vector", mk_lo(), reads=[pname] + dst_names, writes=dst_names)

    def emit_quant(xbuf, xq_out, F, scr, names_in, name_out, tag, scrtag=None):
        tag = scrtag or tag
        NB = F // 16
        absx, u, acc, cmp_t, w1 = (scr[k] for k in
                                   ("absx", "u", "acc", "cmp", "t1"))
        w2 = absx
        s_s, s_sc, s_thr = scr["s"], scr["sc"], scr["thr"]
        b.op("scalar", lambda s: s.activation(absx, xbuf, AF.Abs),
             reads=names_in, writes=[f"{tag}absx"])
        b.op("vector", lambda v: v.tensor_reduce(
            s_s, absx.rearrange("p (nb k) -> p nb k", k=16), axis=AX.X, op=ALU.max),
            reads=[f"{tag}absx"], writes=[f"{tag}s"], small=True)
        b.op("vector", lambda v: v.tensor_tensor(s_thr, s_s, r6.broadcast_to([P, NB]),
                                                 ALU.mult),
             reads=[f"{tag}s", "r6"], writes=[f"{tag}sct"], small=True)
        b.op("vector", lambda v: v.tensor_scalar(s_sc, s_thr, 1e-12, None, ALU.max),
             reads=[f"{tag}sct"], writes=[f"{tag}sc"], small=True)
        b.op("vector", lambda v: v.reciprocal(s_thr, s_sc),
             reads=[f"{tag}sc"], writes=[f"{tag}thr"], small=True)
        b.op("vector", lambda v: v.tensor_tensor(
            u.rearrange("p (nb k) -> p nb k", k=16),
            absx.rearrange("p (nb k) -> p nb k", k=16),
            s_thr.unsqueeze(2).broadcast_to([P, NB, 16]), ALU.mult),
            reads=[f"{tag}absx", f"{tag}thr"], writes=[f"{tag}u"])
        uu = f"{tag}u"
        b.op("vector", lambda v: v.tensor_scalar(acc, u, 0.25, None, ALU.is_gt),
             reads=[uu], writes=[f"{tag}acc"])
        b.op("vector", lambda v: v.scalar_tensor_tensor(cmp_t, u, 0.75, acc,
                                                        ALU.is_gt, ALU.add),
             reads=[uu, f"{tag}acc"], writes=[f"{tag}cmp"])
        b.op("vector", lambda v: v.scalar_tensor_tensor(acc, u, 1.25, cmp_t,
                                                        ALU.is_gt, ALU.add),
             reads=[uu, f"{tag}cmp"], writes=[f"{tag}acc"])
        b.op("vector", lambda v: v.scalar_tensor_tensor(cmp_t, u, 1.75, acc,
                                                        ALU.is_gt, ALU.add),
             reads=[uu, f"{tag}acc"], writes=[f"{tag}cmp"])
        b.op("vector", lambda v: v.tensor_scalar(w1, u, 2.5, None, ALU.is_gt),
             reads=[uu], writes=[f"{tag}w1"])
        b.op("vector", lambda v: v.scalar_tensor_tensor(w2, u, 3.5, w1,
                                                        ALU.is_gt, ALU.add),
             reads=[uu, f"{tag}w1"], writes=[f"{tag}absx"])
        b.op("vector", lambda v: v.scalar_tensor_tensor(acc, cmp_t, 0.5, w2,
                                                        ALU.mult, ALU.add),
             reads=[f"{tag}cmp", f"{tag}absx"], writes=[f"{tag}acc"])
        b.op("vector", lambda v: v.tensor_scalar(w1, u, 5.0, None, ALU.is_gt),
             reads=[uu], writes=[f"{tag}w1"])
        b.op("vector", lambda v: v.scalar_tensor_tensor(cmp_t, w1, 2.0, acc,
                                                        ALU.mult, ALU.add),
             reads=[f"{tag}w1", f"{tag}acc"], writes=[f"{tag}cmp"])
        b.op("vector", lambda v: v.tensor_tensor(
            u.rearrange("p (nb k) -> p nb k", k=16),
            cmp_t.rearrange("p (nb k) -> p nb k", k=16),
            s_sc.unsqueeze(2).broadcast_to([P, NB, 16]), ALU.mult),
            reads=[f"{tag}cmp", f"{tag}sc"], writes=[f"{tag}u"])
        b.op("vector", lambda v: v.tensor_scalar(w1, xbuf, 0.0, 2.0,
                                                 ALU.is_ge, ALU.mult),
             reads=names_in, writes=[f"{tag}w1"])
        b.op("vector", lambda v: v.scalar_tensor_tensor(xq_out, w1, -1.0, u,
                                                        ALU.add, ALU.mult),
             reads=[f"{tag}w1", f"{tag}u"], writes=[name_out])

    def emit_mm3(bank_ap, pname, pairs, reads, nfirst=True):
        def f(te):
            r = None
            n = len(pairs) * 3
            i = 0
            for (lh, ll, rh, rl) in pairs:
                for (a, c) in ((lh, rh), (lh, rl), (ll, rh)):
                    r = te.matmul(bank_ap, a, c, start=(i == 0), stop=(i == n - 1))
                    i += 1
            return r
        b.op("tensor", f, reads=list(reads), writes=[pname])

    c32 = Carver(a32t, A32E)
    c16 = Carver(a16t, A16E)
    wload = c32.take(8 * 1024)
    xtile = [c32.take(1024), c32.take(1024)]
    xntile = [c32.take(1024), c32.take(1024)]
    lnscr = [c32.take(1024), c32.take(1024)]
    wspl_hi = c16.take(8 * 1024)
    wspl_lo = c16.take(8 * 1024)
    win_hi = [c16.take(NDC * 512), c16.take(NDC * 512)]
    win_lo = [c16.take(NDC * 512), c16.take(NDC * 512)]
    st_hi = c16.take(NDC * 512)
    st_lo = c16.take(NDC * 512)

    def emit_weight_T_split(w_dram, tag):
        for i in range(8):
            b.op("sync", (lambda i=i: lambda s: s.dma_start(
                out=wload[:, 1024 * i:1024 * (i + 1)],
                in_=w_dram[128 * i:128 * (i + 1), :]))(),
                writes=[f"{tag}wl{i}"], dma=True)
        for i in range(8):
            def dst(jd, i=i):
                o = 1024 * jd + 128 * i
                return (wspl_hi[:, o:o + 128], wspl_lo[:, o:o + 128])
            emit_transpose_split(wload[:, 1024 * i:1024 * (i + 1)], f"{tag}wl{i}",
                                 8, dst, ["wspl"], f"{tag}t{i}")

    def p1_sweep(x_dram, nsup, mode, hi_dram, lo_dram, tag, hook=None):
        for sup in range(nsup):
            wpar = sup % 2
            winh, winl = win_hi[wpar], win_lo[wpar]
            wnm = "win%d" % wpar
            for tt in range(4):
                gt = sup * 4 + tt
                xb, xnb = xtile[tt % 2], xntile[tt % 2]
                xname, xnname = f"{tag}x{tt % 2}", f"{tag}xn{tt % 2}"
                b.op("sync", (lambda gt=gt, xb=xb: lambda s: s.dma_start(
                    out=xb, in_=x_dram[128 * gt:128 * (gt + 1), :]))(),
                    writes=[xname], dma=True)
                emit_ln(xb, xname, xnb, xnname, lnscr[tt % 2], "lnscr%d" % (tt % 2),
                        pr=tt % 2)

                def dst(jd, tt=tt, winh=winh, winl=winl):
                    o = 512 * jd + 128 * tt
                    return (winh[:, o:o + 128], winl[:, o:o + 128])
                emit_transpose_split(xnb, xnname, 8, dst, [wnm], f"{tag}w{gt}")
            if mode in ("K", "Q"):
                for ie in range(8):
                    bank = ps[ie % 3]
                    pname = f"psk{ie % 3}"
                    pairs = [(wspl_hi[:, 1024 * jd + 128 * ie:1024 * jd + 128 * (ie + 1)],
                              wspl_lo[:, 1024 * jd + 128 * ie:1024 * jd + 128 * (ie + 1)],
                              winh[:, 512 * jd:512 * (jd + 1)],
                              winl[:, 512 * jd:512 * (jd + 1)]) for jd in range(NDC)]
                    emit_mm3(bank[:, :], pname, pairs, ["wspl", wnm])
                    b.op("scalar", (lambda bank=bank, ie=ie: lambda s: s.copy(
                        st_hi[:, 512 * ie:512 * (ie + 1)], bank[:, :]))(),
                        reads=[pname], writes=["stg"])
                    b.op("vector", (lambda bank=bank, ie=ie: lambda v: v.tensor_tensor(
                        st_lo[:, 512 * ie:512 * (ie + 1)], bank[:, :],
                        st_hi[:, 512 * ie:512 * (ie + 1)].bitcast(BF16),
                        ALU.subtract))(),
                        reads=[pname, "stg"], writes=["stg"])
                slen = S if mode == "K" else SH
                for dram, stg in ((hi_dram, st_hi), (lo_dram, st_lo)):
                    b.op("gpsimd", (lambda dram=dram, stg=stg, sup=sup, slen=slen:
                                    lambda g: g.dma_start(
                        out=bass.AP(dram.tensor, 512 * sup,
                                    [[slen, P], [P * slen, 8], [1, 512]]),
                        in_=stg.rearrange("p (j n) -> p j n", n=512)))(),
                        reads=["stg"], writes=["std"], dma=True)
                if hook:
                    hook()
            else:
                for tt in range(4):
                    for eh in range(2):
                        bank = ps[(2 * tt + eh) % 3]
                        pname = f"psk{(2 * tt + eh) % 3}"
                        pairs = [(winh[:, 512 * jd + 128 * tt:512 * jd + 128 * (tt + 1)],
                                  winl[:, 512 * jd + 128 * tt:512 * jd + 128 * (tt + 1)],
                                  wspl_hi[:, 1024 * jd + 512 * eh:1024 * jd + 512 * (eh + 1)],
                                  wspl_lo[:, 1024 * jd + 512 * eh:1024 * jd + 512 * (eh + 1)])
                                 for jd in range(NDC)]
                        emit_mm3(bank[:, :], pname, pairs, ["wspl", wnm])
                        o = 1024 * tt + 512 * eh
                        b.op("scalar", (lambda bank=bank, o=o: lambda s: s.copy(
                            st_hi.bitcast(BF16)[:, o:o + 512], bank[:, :]))(),
                            reads=[pname], writes=["stg"])
                        b.op("vector", (lambda bank=bank, o=o: lambda v: v.tensor_tensor(
                            st_lo.bitcast(BF16)[:, o:o + 512], bank[:, :],
                            st_hi.bitcast(BF16)[:, o:o + 512], ALU.subtract))(),
                            reads=[pname, "stg"], writes=["stg"])
                for dram, stg in ((hi_dram, st_hi), (lo_dram, st_lo)):
                    b.op("gpsimd", (lambda dram=dram, stg=stg, sup=sup:
                                    lambda g: g.dma_start(
                        out=bass.AP(dram.tensor, 512 * sup * D,
                                    [[D, P], [P * D, 4], [1, 1024]]),
                        in_=stg.rearrange("p (j n) -> p j n", n=1024)))(),
                        reads=["stg"], writes=["std"], dma=True)
            if hook:
                hook()

    pw_load = c32.take(2816)
    pw_q = c32.take(2816)
    pq_scr = {k: c32.take(1408) for k in ("absx", "u", "acc", "cmp", "t1")}
    pq_sm = {k: c32.take(96) for k in ("s", "sc", "thr")}
    pw_st_hi = c16.take(NIT * 128)
    pw_st_lo = c16.take(NIT * 128)

    def qscr(F):
        NB = F // 16
        d = {k: pq_scr[k][:, 0:F] for k in pq_scr}
        d.update({k: pq_sm[k][:, 0:NB] for k in pq_sm})
        return d

    prep_steps = []

    def prep_gu_step(w_dram, out_hi_d, out_lo_d, tag, ri):
        b.op("sync", (lambda ri=ri: lambda s: s.dma_start(
            out=pw_load[:, 0:1024], in_=w_dram[128 * ri:128 * (ri + 1), :]))(),
            writes=["pwl"], dma=True)
        emit_quant(pw_load[:, 0:1024], pw_q[:, 0:1024], 1024, qscr(1024),
                   ["pwl"], "pwq", f"{tag}q", scrtag="pwscr")

        def dst(jd):
            return (pw_st_hi[:, 128 * jd:128 * (jd + 1)],
                    pw_st_lo[:, 128 * jd:128 * (jd + 1)])
        emit_transpose_split(pw_q[:, 0:1024], "pwq", 8, dst,
                             ["pwst"], f"{tag}t{ri}")
        for dram, stg in ((out_hi_d, pw_st_hi), (out_lo_d, pw_st_lo)):
            b.op("gpsimd", (lambda dram=dram, stg=stg, ri=ri: lambda g: g.dma_start(
                out=bass.AP(dram.tensor, 128 * ri,
                            [[I, P], [P * I, 8], [1, 128]]),
                in_=stg.rearrange("p (j n) -> p j n", n=128)[:, 0:8, :]))(),
                reads=["pwst"], writes=[f"{tag}d"], dma=True)

    def prep_wd_step(rd):
        b.op("sync", (lambda rd=rd: lambda s: s.dma_start(
            out=pw_load[:, 0:2816], in_=w_in["wd"][128 * rd:128 * (rd + 1), :]))(),
            writes=["pwl"], dma=True)
        for hf in range(2):
            F = 1408
            emit_quant(pw_load[:, F * hf:F * (hf + 1)], pw_q[:, F * hf:F * (hf + 1)],
                       F, qscr(F), ["pwl"], "pwq", f"pdq{hf}", scrtag="pwscr")

        def dstD(ji):
            return (pw_st_hi[:, 128 * ji:128 * (ji + 1)],
                    pw_st_lo[:, 128 * ji:128 * (ji + 1)])
        emit_transpose_split(pw_q[:, 0:2816], "pwq", NIT, dstD, ["pwst"],
                             f"pdt{rd}")
        for dram, stg in ((wdq_hi, pw_st_hi), (wdq_lo, pw_st_lo)):
            for half in range(2):
                n0 = 11 * half
                b.op("gpsimd", (lambda dram=dram, stg=stg, rd=rd, n0=n0:
                                lambda g: g.dma_start(
                    out=bass.AP(dram.tensor, 128 * n0 * D + 128 * rd,
                                [[D, P], [P * D, 11], [1, 128]]),
                    in_=stg.rearrange("p (j n) -> p j n", n=128)[:, n0:n0 + 11, :]))(),
                    reads=["pwst"], writes=["pdd"], dma=True)

    for ri in range(NIT):
        prep_steps.append((lambda ri=ri: prep_gu_step(w_in["wg"], wgq_hi, wgq_lo, "pg", ri)))
    for ri in range(NIT):
        prep_steps.append((lambda ri=ri: prep_gu_step(w_in["wu"], wuq_hi, wuq_lo, "pu", ri)))
    for rd in range(8):
        prep_steps.append((lambda rd=rd: prep_wd_step(rd)))
    prep_iter = iter(prep_steps)

    def prep_hook(n=1):
        for _ in range(n):
            st = next(prep_iter, None)
            if st is not None:
                st()

    emit_weight_T_split(w_in["wk"], "k")
    p1_sweep(x_full, 8, "K", kt_hi, kt_lo, "f")
    b.barrier()
    emit_weight_T_split(w_in["wv"], "v")
    p1_sweep(x_full, 8, "V", v_hi, v_lo, "g")
    b.barrier()
    emit_weight_T_split(w_in["wq"], "q")
    p1_sweep(x_q, 4, "Q", qt_hi, qt_lo, "h")
    while True:
        st = next(prep_iter, None)
        if st is None:
            break
        st()
    b.barrier()

    c32.reset()
    c16.reset()
    acc = c32.take(NQT * 1024)
    p_sb = [c32.take(512), c32.take(512)]
    av_tmp = c32.take(1024)
    av_tmp2 = c32.take(1024)
    kch_hi = [c16.take(NDC * 512), c16.take(NDC * 512)]
    kch_lo = [c16.take(NDC * 512), c16.take(NDC * 512)]
    vch_hi = [c16.take(4 * 1024), c16.take(4 * 1024)]
    vch_lo = [c16.take(4 * 1024), c16.take(4 * 1024)]
    qtb_hi = [c16.take(NDC * 128), c16.take(NDC * 128)]
    qtb_lo = [c16.take(NDC * 128), c16.take(NDC * 128)]
    at_hi = [c16.take(4 * 128), c16.take(4 * 128)]
    at_lo = [c16.take(4 * 128), c16.take(4 * 128)]

    def mk_3d_load(dram, dst, off, nblk, blklen, rowstride, bufstride=None):
        bs = bufstride or blklen

        def f(s):
            return s.dma_start(
                out=dst.rearrange("p (j n) -> p j n", n=bs)[:, 0:nblk, 0:blklen],
                in_=bass.AP(dram.tensor, off,
                            [[rowstride, P], [P * rowstride, nblk], [1, blklen]]))
        return f

    def p2_loads(c):
        cp = c % 2
        b.op("sync", mk_3d_load(kt_hi, kch_hi[cp], 512 * c, 8, 512, S),
             reads=["std"], writes=[f"kch{cp}"], dma=True)
        b.op("sync", mk_3d_load(kt_lo, kch_lo[cp], 512 * c, 8, 512, S),
             reads=["std"], writes=[f"kch{cp}"], dma=True)
        b.op("sync", mk_3d_load(v_hi, vch_hi[cp], 512 * c * D, 4, 1024, D),
             reads=["std"], writes=[f"vch{cp}"], dma=True)
        b.op("sync", mk_3d_load(v_lo, vch_lo[cp], 512 * c * D, 4, 1024, D),
             reads=["std"], writes=[f"vch{cp}"], dma=True)

    def p2_qt_load(t):
        tp = t % 2
        b.op("sync", mk_3d_load(qt_hi, qtb_hi[tp], 128 * t, 8, 128, SH),
             reads=["std"], writes=[f"qtb{tp}"], dma=True)
        b.op("sync", mk_3d_load(qt_lo, qtb_lo[tp], 128 * t, 8, 128, SH),
             reads=["std"], writes=[f"qtb{tp}"], dma=True)

    def p2_scores(c, t):
        cp, tp = c % 2, t % 2
        sbank = ps[tp]
        sname = f"pss{tp}"
        pairs = [(qtb_hi[tp][:, 128 * jd:128 * (jd + 1)],
                  qtb_lo[tp][:, 128 * jd:128 * (jd + 1)],
                  kch_hi[cp][:, 512 * jd:512 * (jd + 1)],
                  kch_lo[cp][:, 512 * jd:512 * (jd + 1)]) for jd in range(NDC)]
        emit_mm3(sbank[:, :], sname, pairs, [f"qtb{tp}", f"kch{cp}"])

    def p2_post(c, t):
        cp, tp = c % 2, t % 2
        sbank = ps[tp]
        sname = f"pss{tp}"
        b.op("vector", (lambda sbank=sbank, tp=tp: lambda v: v.reduce_max(
            fl_cm[tp], sbank[:, :], axis=AX.X))(),
            reads=[sname], writes=[f"fl_cm{tp}"], small=True)
        b.op("vector", (lambda tp=tp: lambda v: v.tensor_scalar(
            fl_cm8[tp], fl_cm[tp], 0.125, None, ALU.mult))(),
            reads=[f"fl_cm{tp}"], writes=[f"fl_cm8{tp}"], small=True)
        src_m = fl_mB[t] if c % 2 == 1 else fl_mA[t]
        dst_m = fl_mA[t] if c % 2 == 1 else fl_mB[t]
        if c == 0:
            b.op("vector", (lambda dst_m=dst_m, tp=tp: lambda v: v.tensor_copy(
                dst_m, fl_cm8[tp]))(),
                reads=[f"fl_cm8{tp}"], writes=[f"fl_m{t}"], small=True)
        else:
            b.op("vector", (lambda src_m=src_m, tp=tp: lambda v: v.tensor_tensor(
                fl_tmp[tp], src_m, fl_cm8[tp], ALU.max))(),
                reads=[f"fl_m{t}", f"fl_cm8{tp}"], writes=[f"fl_tmp{tp}"], small=True)
            b.op("vector", (lambda dst_m=dst_m, tp=tp: lambda v: v.tensor_copy(
                dst_m, fl_tmp[tp]))(),
                reads=[f"fl_tmp{tp}"], writes=[f"fl_m{t}"], small=True)
        cur_m = dst_m
        b.op("vector", (lambda cur_m=cur_m, tp=tp: lambda v: v.tensor_scalar(
            fl_nm[tp], cur_m, -1.0, None, ALU.mult))(),
            reads=[f"fl_m{t}"], writes=[f"fl_nm{tp}"], small=True)
        b.op("scalar", (lambda sbank=sbank, tp=tp: lambda s: s.activation(
            p_sb[tp], sbank[:, :], AF.Exp, bias=fl_nm[tp], scale=0.125,
            accum_out=fl_rs[tp]))(),
            reads=[sname, f"fl_nm{tp}"], writes=[f"p_sb{tp}", f"fl_rs{tp}"], small=True)
        if c == 0:
            b.op("vector", (lambda t=t, tp=tp: lambda v: v.tensor_copy(
                fl_lB[t], fl_rs[tp]))(),
                reads=[f"fl_rs{tp}"], writes=[f"fl_l{t}"], small=True)
        else:
            src_l = fl_lB[t] if c % 2 == 1 else fl_lA[t]
            dst_l = fl_lA[t] if c % 2 == 1 else fl_lB[t]
            b.op("vector", (lambda src_m=src_m, cur_m=cur_m, tp=tp: lambda v:
                            v.tensor_tensor(fl_tmp[tp], src_m, cur_m,
                                            ALU.subtract))(),
                 reads=[f"fl_m{t}"], writes=[f"fl_tmp{tp}"], small=True)
            b.op("scalar", (lambda tp=tp: lambda s: s.activation(
                fl_al[tp], fl_tmp[tp], AF.Exp))(),
                reads=[f"fl_tmp{tp}"], writes=[f"fl_al{tp}"], small=True)
            b.op("vector", (lambda src_l=src_l, tp=tp: lambda v: v.tensor_tensor(
                fl_tmp[tp], src_l, fl_al[tp], ALU.mult))(),
                reads=[f"fl_l{t}", f"fl_al{tp}"], writes=[f"fl_tmp{tp}"], small=True)
            b.op("vector", (lambda dst_l=dst_l, tp=tp: lambda v: v.tensor_tensor(
                dst_l, fl_tmp[tp], fl_rs[tp], ALU.add))(),
                reads=[f"fl_tmp{tp}", f"fl_rs{tp}"], writes=[f"fl_l{t}"], small=True)
        tbank = ps[2 + tp]
        tname = f"pst{tp}"
        b.op("tensor", (lambda tp=tp, tbank=tbank: lambda te: [
            te.transpose(tbank[:, 128 * j:128 * (j + 1)],
                         p_sb[tp][:, 128 * j:128 * (j + 1)], ident[:])
            for j in range(4)][-1])(),
            reads=[f"p_sb{tp}", "ident"], writes=[tname])
        b.op("scalar", (lambda tbank=tbank, tp=tp: lambda s: s.copy(
            at_hi[tp], tbank[:, :]))(),
            reads=[tname], writes=[f"at{tp}"])
        b.op("vector", (lambda tbank=tbank, tp=tp: lambda v: v.tensor_tensor(
            at_lo[tp], tbank[:, :], at_hi[tp], ALU.subtract))(),
            reads=[tname, f"at{tp}"], writes=[f"at{tp}"])
        for eh in range(2):
            abank = ps[4 + 2 * tp + eh]
            aname = f"psa{2 * tp + eh}"
            pairs = [(at_hi[tp][:, 128 * ks:128 * (ks + 1)],
                      at_lo[tp][:, 128 * ks:128 * (ks + 1)],
                      vch_hi[cp][:, 1024 * ks + 512 * eh:1024 * ks + 512 * (eh + 1)],
                      vch_lo[cp][:, 1024 * ks + 512 * eh:1024 * ks + 512 * (eh + 1)])
                     for ks in range(4)]
            emit_mm3(abank[:, :], aname, pairs, [f"at{tp}", f"vch{cp}"])
            aslice = acc[:, 1024 * t + 512 * eh:1024 * t + 512 * (eh + 1)]
            accn = f"acc{t}_{eh}"
            if c == 0:
                b.op("vector", (lambda abank=abank, aslice=aslice: lambda v:
                                v.tensor_copy(aslice, abank[:, :]))(),
                     reads=[aname], writes=[accn])
            else:
                tsl = (av_tmp if tp == 0 else av_tmp2)[:, 512 * eh:512 * (eh + 1)]
                b.op("scalar", (lambda aslice=aslice, tsl=tsl, tp=tp: lambda s:
                                s.activation(tsl, aslice, AF.Identity,
                                             scale=fl_al[tp]))(),
                     reads=[accn, f"fl_al{tp}"], writes=[f"avt{tp}{eh}"])
                b.op("vector", (lambda abank=abank, aslice=aslice, tsl=tsl:
                                lambda v: v.tensor_tensor(
                                    aslice, tsl, abank[:, :], ALU.add))(),
                     reads=[f"avt{tp}{eh}", aname], writes=[accn])

    p2_loads(0)
    p2_qt_load(0)
    for c in range(NKCH):
        for t in range(NQT):
            if t + 1 < NQT:
                p2_qt_load(t + 1)
            elif c + 1 < NKCH:
                p2_qt_load(0)
            p2_scores(c, t)
            if t == 8 and c + 1 < NKCH:
                p2_loads(c + 1)
            if t > 0:
                p2_post(c, t - 1)
        p2_post(c, NQT - 1)

    b.barrier()
    c16.reset()
    g_hi = c16.take(NQT * 1024)
    g_lo = c16.take(NQT * 1024)
    woT_hi = c16.take(8 * 1024)
    woT_lo = c16.take(8 * 1024)
    last_l = fl_lB if (NKCH - 1) % 2 == 0 else fl_lA
    for t in range(NQT):
        b.op("vector", (lambda t=t: lambda v: v.reciprocal(
            fl_linv[t], last_l[t]))(),
            reads=[f"fl_l{t}"], writes=[f"fl_linv{t}"], small=True)
        b.op("vector", (lambda t=t: lambda v: v.tensor_tensor(
            av_tmp, acc[:, 1024 * t:1024 * (t + 1)],
            fl_linv[t].broadcast_to([P, 1024]), ALU.mult))(),
            reads=[f"acc{t}_0", f"acc{t}_1", f"fl_linv{t}"], writes=["avn"])
        b.op("vector", (lambda t=t: lambda v: v.tensor_copy(
            g_hi[:, 1024 * t:1024 * (t + 1)], av_tmp))(),
            reads=["avn"], writes=["grhs"])
        b.op("vector", (lambda t=t: lambda v: v.tensor_tensor(
            g_lo[:, 1024 * t:1024 * (t + 1)], av_tmp,
            g_hi[:, 1024 * t:1024 * (t + 1)], ALU.subtract))(),
            reads=["avn", "grhs"], writes=["grhs"])

    b.barrier()
    c32.reset()
    wload2 = c32.take(8 * 1024)
    gtmp = c32.take(1024)
    finst = c32.take(1024)
    for i in range(8):
        b.op("sync", (lambda i=i: lambda s: s.dma_start(
            out=wload2[:, 1024 * i:1024 * (i + 1)],
            in_=w_in["wo"][128 * i:128 * (i + 1), :]))(),
            writes=[f"owl{i}"], dma=True)
    for i in range(8):
        def dst(jd, i=i):
            o = 1024 * jd + 128 * i
            return (woT_hi[:, o:o + 128], woT_lo[:, o:o + 128])
        emit_transpose_split(wload2[:, 1024 * i:1024 * (i + 1)], f"owl{i}",
                             8, dst, ["woT"], f"ot{i}")

    for r in range(2):
        for g in range(8):
            for ie in range(8):
                bank = ps[ie % 2]
                pname = f"psg{ie % 2}"
                pairs = [(woT_hi[:, 1024 * jd + 128 * ie:1024 * jd + 128 * (ie + 1)],
                          woT_lo[:, 1024 * jd + 128 * ie:1024 * jd + 128 * (ie + 1)],
                          g_hi[:, 1024 * (8 * r + jd) + 128 * g:
                               1024 * (8 * r + jd) + 128 * (g + 1)],
                          g_lo[:, 1024 * (8 * r + jd) + 128 * g:
                               1024 * (8 * r + jd) + 128 * (g + 1)])
                         for jd in range(NDC)]
                emit_mm3(bank[:, 0:128], pname, pairs, ["woT", "grhs"])
                b.op("vector", (lambda bank=bank, ie=ie: lambda v: v.tensor_copy(
                    gtmp[:, 128 * ie:128 * (ie + 1)], bank[:, 0:128]))(),
                    reads=[pname], writes=["gtmp"])
            for ie in range(0, 8, 4):
                bi = 2 + (ie // 4) % 2
                bank = ps[bi]
                pname = f"psq{bi}"
                b.op("tensor", (lambda ie=ie, bank=bank: lambda te: [
                    te.transpose(bank[:, 128 * j:128 * (j + 1)],
                                 gtmp[:, 128 * (ie + j):128 * (ie + j + 1)],
                                 ident[:])
                    for j in range(4)][-1])(),
                    reads=["gtmp", "ident"], writes=[pname])
                b.op("vector", (lambda ie=ie, bank=bank: lambda v: v.tensor_copy(
                    finst[:, 128 * ie:128 * (ie + 4)], bank[:, :]))(),
                    reads=[pname], writes=["finst"])
            gg = 8 * r + g
            b.op("gpsimd", (lambda gg=gg: lambda gp: gp.dma_start(
                out=fin_d[128 * gg:128 * (gg + 1), :], in_=finst))(),
                reads=["finst"], writes=["find"], dma=True)

    b.barrier()

    c32.reset()
    c16.reset()
    ffn_acc = c32.take(8 * 1024)
    fx = c32.take(1024)
    fxn = c32.take(1024)
    fq = c32.take(1024)
    f_lnscr = c32.take(1024)
    f_scr = {k: c32.take(1024) for k in ("absx", "u", "acc", "cmp", "t1", "t2")}
    f_sm = {k: c32.take(64) for k in ("s", "sc", "thr")}
    h_sb = c32.take(512)
    hq_sb = c32.take(512)
    silu_sb = c32.take(512)
    finld = c32.take(1024)
    res1 = c32.take(1024)
    dn_tmp = c32.take(1024)
    res2 = c32.take(1024)

    xqT_hi = c16.take(NDC * 1024)
    xqT_lo = c16.take(NDC * 1024)
    _wg1, _wg2 = c16.take(NDC * 512), c16.take(NDC * 512)
    _wu1, _wu2 = c16.take(NDC * 512), c16.take(NDC * 512)
    _wd1, _wd2 = c16.take(4 * 1024), c16.take(4 * 1024)
    wgc_hi, wgc_lo = [_wg1, _wg1], [_wg2, _wg2]
    wuc_hi, wuc_lo = [_wu1, _wu1], [_wu2, _wu2]
    wdc_hi, wdc_lo = [_wd1, _wd1], [_wd2, _wd2]
    hqT_hi = c16.take(4 * 128)
    hqT_lo = c16.take(4 * 128)

    def fscr(F):
        NB = F // 16
        d = {k: f_scr[k][:, 0:F] for k in f_scr}
        d.update({k: f_sm[k][:, 0:NB] for k in f_sm})
        return d

    for grp in range(2):
        for tt in range(8):
            gg = 8 * grp + tt
            b.op("sync", (lambda gg=gg: lambda s: s.dma_start(
                out=fx, in_=x_ffn[128 * gg:128 * (gg + 1), :]))(),
                writes=["fx"], dma=True)
            emit_ln(fx, "fx", fxn, "fxn", f_lnscr, "flnscr", pr=0)
            emit_quant(fxn, fq, 1024, fscr(1024), ["fxn"], "fq", "fq", scrtag="fscr")

            def dst(jd, tt=tt):
                o = 1024 * jd + 128 * tt
                return (xqT_hi[:, o:o + 128], xqT_lo[:, o:o + 128])
            emit_transpose_split(fq, "fq", 8, dst, ["xqT"], f"xt{gg}")

        def p3_gu(ci, io, ilen, tt):
            wp = 0
            gbank, ubank = ps[tt % 2], ps[2 + tt % 2]
            gname, uname = f"psG{tt % 2}", f"psU{tt % 2}"
            wsl = [(512 * jd, 512 * jd + ilen) for jd in range(NDC)]
            gpairs = [(xqT_hi[:, 1024 * jd + 128 * tt:1024 * jd + 128 * (tt + 1)],
                       xqT_lo[:, 1024 * jd + 128 * tt:1024 * jd + 128 * (tt + 1)],
                       wgc_hi[wp][:, a:z], wgc_lo[wp][:, a:z])
                      for jd, (a, z) in enumerate(wsl)]
            upairs = [(xqT_hi[:, 1024 * jd + 128 * tt:1024 * jd + 128 * (tt + 1)],
                       xqT_lo[:, 1024 * jd + 128 * tt:1024 * jd + 128 * (tt + 1)],
                       wuc_hi[wp][:, a:z], wuc_lo[wp][:, a:z])
                      for jd, (a, z) in enumerate(wsl)]
            emit_mm3(gbank[:, 0:ilen], gname, gpairs, ["xqT", f"wgc{wp}"])
            emit_mm3(ubank[:, 0:ilen], uname, upairs, ["xqT", f"wuc{wp}"])

        def p3_post(ci, io, ilen, tt):
            wp = 0
            nsub = ilen // 128
            gbank, ubank = ps[tt % 2], ps[2 + tt % 2]
            gname, uname = f"psG{tt % 2}", f"psU{tt % 2}"
            b.op("scalar", (lambda gbank=gbank, ilen=ilen: lambda s: s.activation(
                silu_sb[:, 0:ilen], gbank[:, 0:ilen], AF.Silu))(),
                reads=[gname], writes=["silu"])
            b.op("vector", (lambda ubank=ubank, ilen=ilen: lambda v: v.tensor_tensor(
                h_sb[:, 0:ilen], silu_sb[:, 0:ilen], ubank[:, 0:ilen], ALU.mult))(),
                reads=["silu", uname], writes=["h"])
            emit_quant(h_sb[:, 0:ilen], hq_sb[:, 0:ilen], ilen, fscr(ilen),
                       ["h"], "hq", "hq", scrtag="fscr")
            tb = ps[4 + tt % 2]
            tn = f"psH{tt % 2}"
            b.op("tensor", (lambda tb=tb, nsub=nsub: lambda te: [
                te.transpose(tb[:, 128 * j:128 * (j + 1)],
                             hq_sb[:, 128 * j:128 * (j + 1)], ident[:])
                for j in range(nsub)][-1])(),
                reads=["hq", "ident"], writes=[tn])
            b.op("scalar", (lambda tb=tb, nsub=nsub: lambda s: s.copy(
                hqT_hi[:, 0:128 * nsub], tb[:, 0:128 * nsub]))(),
                reads=[tn], writes=["hqT"])
            b.op("vector", (lambda tb=tb, nsub=nsub: lambda v: v.tensor_tensor(
                hqT_lo[:, 0:128 * nsub], tb[:, 0:128 * nsub],
                hqT_hi[:, 0:128 * nsub], ALU.subtract))(),
                reads=[tn, "hqT"], writes=["hqT"])
            for dh in range(2):
                dbank = ps[6 + (tt + dh) % 2]
                dname = f"psD{(tt + dh) % 2}"
                pairs = [(hqT_hi[:, 128 * ks:128 * (ks + 1)],
                          hqT_lo[:, 128 * ks:128 * (ks + 1)],
                          wdc_hi[wp][:, 1024 * ks + 512 * dh:
                                     1024 * ks + 512 * (dh + 1)],
                          wdc_lo[wp][:, 1024 * ks + 512 * dh:
                                     1024 * ks + 512 * (dh + 1)])
                         for ks in range(nsub)]
                emit_mm3(dbank[:, :], dname, pairs, ["hqT", f"wdc{wp}"])
                fslice = ffn_acc[:, 1024 * tt + 512 * dh:
                                 1024 * tt + 512 * (dh + 1)]
                fname = f"facc{tt}_{dh}"
                if ci == 0:
                    b.op("vector", (lambda dbank=dbank, fslice=fslice:
                                    lambda v: v.tensor_copy(fslice,
                                                            dbank[:, :]))(),
                         reads=[dname], writes=[fname])
                else:
                    tsl = dn_tmp[:, 512 * dh:512 * (dh + 1)]
                    b.op("scalar", (lambda fslice=fslice, tsl=tsl: lambda s:
                                    s.copy(tsl, fslice))(),
                         reads=[fname], writes=[f"dnt{dh}"])
                    b.op("vector", (lambda dbank=dbank, fslice=fslice,
                                    tsl=tsl: lambda v: v.tensor_tensor(
                                        fslice, tsl, dbank[:, :], ALU.add))(),
                         reads=[f"dnt{dh}", dname], writes=[fname])

        for ci, (io, ilen) in enumerate(ICHUNKS):
            wp = 0
            nsub = ilen // 128
            b.op("sync", mk_3d_load(wgq_hi, wgc_hi[wp], io, 8, ilen, I, 512),
                 reads=["pgd"], writes=[f"wgc{wp}"], dma=True)
            b.op("sync", mk_3d_load(wgq_lo, wgc_lo[wp], io, 8, ilen, I, 512),
                 reads=["pgd"], writes=[f"wgc{wp}"], dma=True)
            b.op("sync", mk_3d_load(wuq_hi, wuc_hi[wp], io, 8, ilen, I, 512),
                 reads=["pud"], writes=[f"wuc{wp}"], dma=True)
            b.op("sync", mk_3d_load(wuq_lo, wuc_lo[wp], io, 8, ilen, I, 512),
                 reads=["pud"], writes=[f"wuc{wp}"], dma=True)
            b.op("sync", mk_3d_load(wdq_hi, wdc_hi[wp], io * D, nsub, 1024, D),
                 reads=["pdd"], writes=[f"wdc{wp}"], dma=True)
            b.op("sync", mk_3d_load(wdq_lo, wdc_lo[wp], io * D, nsub, 1024, D),
                 reads=["pdd"], writes=[f"wdc{wp}"], dma=True)
            for tt in range(8):
                p3_gu(ci, io, ilen, tt)
                if tt > 0:
                    p3_post(ci, io, ilen, tt - 1)
            p3_post(ci, io, ilen, 7)
        for tt in range(8):
            gg = 8 * grp + tt
            b.op("sync", (lambda gg=gg: lambda s: s.dma_start(
                out=finld, in_=fin_d[128 * gg:128 * (gg + 1), :]))(),
                reads=["find"], writes=["finld"], dma=True)
            b.op("sync", (lambda gg=gg: lambda s: s.dma_start(
                out=fx, in_=x_ffn[128 * gg:128 * (gg + 1), :]))(),
                writes=["fx"], dma=True)
            b.op("vector", lambda v: v.tensor_tensor(res1, fx, finld, ALU.add),
                 reads=["fx", "finld"], writes=["res1"])
            b.op("vector", (lambda tt=tt: lambda v: v.tensor_tensor(
                res2, res1, ffn_acc[:, 1024 * tt:1024 * (tt + 1)], ALU.add))(),
                reads=["res1", f"facc{tt}_0", f"facc{tt}_1"], writes=["res2"])
            b.op("gpsimd", (lambda gg=gg: lambda g: g.dma_start(
                out=out_d[128 * gg:128 * (gg + 1), :], in_=res2))(),
                reads=["res2"], writes=["outd"], dma=True)
        if grp == 0:
            b.barrier()

    final_out = b.buf["outd"][0]
    key = "dma_gpsimd"
    final_out.inc_keys.add(key)
    wait_op = b.op("sync", None)
    wait_op.waits.append((key, final_out))

    with nc.Block() as block:
        b.finalize_and_emit(block, sems)
    es.close()
    return nc


_NC_CACHE = None


def kernel(L0_attn_q, L0_attn_wq, L0_attn_wk, L0_attn_wv, L0_attn_wo,
           L0_ffn_w_gate, L0_ffn_w_up, L0_ffn_w_down):
    global _NC_CACHE
    x = np.ascontiguousarray(np.asarray(L0_attn_q, dtype=np.float32))
    ws = {
        "wq": np.ascontiguousarray(np.asarray(L0_attn_wq, dtype=np.float32)),
        "wk": np.ascontiguousarray(np.asarray(L0_attn_wk, dtype=np.float32)),
        "wv": np.ascontiguousarray(np.asarray(L0_attn_wv, dtype=np.float32)),
        "wo": np.ascontiguousarray(np.asarray(L0_attn_wo, dtype=np.float32)),
        "wg": np.ascontiguousarray(np.asarray(L0_ffn_w_gate, dtype=np.float32)),
        "wu": np.ascontiguousarray(np.asarray(L0_ffn_w_up, dtype=np.float32)),
        "wd": np.ascontiguousarray(np.asarray(L0_ffn_w_down, dtype=np.float32)),
    }
    B = x.shape[0]
    if _NC_CACHE is None:
        _NC_CACHE = build_nc()
    in_maps = []
    for b_ in range(B):
        for h in range(2):
            toks = np.concatenate([np.arange(1024) * 4 + 2 * h,
                                   np.arange(1024) * 4 + 2 * h + 1])
            m = dict(ws)
            m["x_full"] = x[b_]
            m["x_q"] = np.ascontiguousarray(x[b_][2048 * h:2048 * h + 2048])
            m["x_ffn"] = np.ascontiguousarray(x[b_][toks])
            in_maps.append(m)
    res = run_bass_kernel_spmd(_NC_CACHE, in_maps, core_ids=list(range(8)))
    out = np.empty((B, S, D), dtype=np.float32)
    for b_ in range(B):
        for h in range(2):
            o = np.asarray(res.results[2 * b_ + h]["out"])
            v = o.reshape(2, 1024, D)
            ob = out[b_].reshape(1024, 4, D)
            ob[:, 2 * h, :] = v[0]
            ob[:, 2 * h + 1, :] = v[1]
    return out


# version 16 (final)
# speedup vs baseline: 1.2857x; 1.0539x over previous
import os
import numpy as np
import concourse.bass as bass
import concourse.mybir as mybir
from concourse.bass_utils import run_bass_kernel_spmd

F32 = mybir.dt.float32
BF16 = mybir.dt.bfloat16
AF = mybir.ActivationFunctionType
ALU = mybir.AluOpType
AX = mybir.AxisListType

P = 128
S, D, I = 4096, 1024, 2816
SH = 2048
NDC = D // P
KCH = 512
NKCH = S // KCH
NQT = SH // P
ICHUNKS = [(0, 512), (512, 512), (1024, 512), (1536, 512), (2048, 512), (2560, 256)]
NIT = I // P

ENGINES = ("sync", "gpsimd", "vector", "scalar", "tensor")


class Op:
    __slots__ = ("eng", "fn", "waits", "inc_keys", "inc_vals", "drain_after",
                 "dma", "_small", "seq")

    def __init__(self, eng, fn, dma=False):
        self.eng = eng
        self.fn = fn
        self.waits = []
        self.inc_keys = set()
        self.inc_vals = {}
        self.drain_after = False
        self.dma = dma
        self._small = False
        self.seq = -1


class Builder:
    def __init__(self):
        self.ops = {e: [] for e in ENGINES}
        self.buf = {}
        self.pending_waits = {e: [] for e in ENGINES}
        self.seq = 0
        self.strict = os.environ.get("BASS_SIM_FP64") == "1"

    def semkey(self, op):
        return ("dma_" + op.eng) if op.dma else op.eng

    def _edge(self, prod, cons):
        if prod is cons:
            return
        if (prod.eng == cons.eng and not prod.dma and prod.eng != "gpsimd"
                and not self.strict):
            return
        key = self.semkey(prod)
        prod.inc_keys.add(key)
        if not prod.dma and prod._small:
            prod.drain_after = True
        cons.waits.append((key, prod))

    def op(self, eng, fn, reads=(), writes=(), dma=False, small=False):
        o = Op(eng, fn, dma=dma)
        o._small = small
        if dma:
            o.inc_keys.add(self.semkey(o))
        if small and not dma:
            o.drain_after = True
        for bnm in reads:
            st = self.buf.get(bnm)
            if st is not None and st[0] is not None:
                self._edge(st[0], o)
        for bnm in writes:
            st = self.buf.get(bnm)
            if st is not None:
                if st[0] is not None:
                    self._edge(st[0], o)
                for r in st[1]:
                    self._edge(r, o)
        for bnm in writes:
            self.buf[bnm] = [o, []]
        for bnm in reads:
            st = self.buf.setdefault(bnm, [None, []])
            st[1].append(o)
        for w in self.pending_waits[eng]:
            o.waits.append(w)
        self.pending_waits[eng] = []
        o.seq = self.seq
        self.seq += 1
        self.ops[eng].append(o)
        return o

    def barrier(self):
        lasts = []
        for e in ENGINES:
            last_d, last_c = None, None
            for o in reversed(self.ops[e]):
                if o.dma and last_d is None:
                    last_d = o
                if not o.dma and last_c is None:
                    last_c = o
                if last_d is not None and last_c is not None:
                    break
            for last in (last_d, last_c):
                if last is None:
                    continue
                key = self.semkey(last)
                last.inc_keys.add(key)
                if not last.dma and last._small:
                    last.drain_after = True
                lasts.append((e, key, last))
        for e in ENGINES:
            for e2, key, last in lasts:
                if e2 != e:
                    self.pending_waits[e].append((key, last))

    def finalize_and_emit(self, block, sems):
        strict_dma = os.environ.get("BASS_SIM_FP64") == "1"
        counts = {k: 0 for k in sems}
        for e in ENGINES:
            for o in self.ops[e]:
                for key in sorted(o.inc_keys):
                    counts[key] += 16 if key.startswith("dma_") else 1
                    o.inc_vals[key] = counts[key]
        dma_hist = {}
        if strict_dma:
            for e in ENGINES:
                for o in self.ops[e]:
                    for key, v in o.inc_vals.items():
                        if key.startswith("dma_"):
                            dma_hist.setdefault(key, []).append((o.seq, v))
            for key in dma_hist:
                dma_hist[key].sort()

        def strict_val(key, consumer_seq, base):
            import bisect
            hist = dma_hist.get(key, [])
            i = bisect.bisect_left(hist, (consumer_seq, -1))
            return hist[i - 1][1] if i > 0 else base

        def emit_stream(eng_name):
            def run(eng):
                for o in self.ops[eng_name]:
                    wmax = {}
                    for (key, prod) in o.waits:
                        v = prod.inc_vals[key]
                        if strict_dma and key.startswith("dma_"):
                            v = strict_val(key, o.seq, v)
                        wmax[key] = max(wmax.get(key, 0), v)
                    if strict_dma and o.dma:
                        k0 = "dma_" + o.eng
                        prev = o.inc_vals[k0] - 16
                        if prev > 0:
                            wmax[k0] = max(wmax.get(k0, 0), prev)
                    for key in sorted(wmax):
                        eng.wait_ge(sems[key], wmax[key])
                    instr = o.fn(eng) if o.fn is not None else None
                    if o.drain_after and not o.dma:
                        instr = eng.drain()
                    if o.inc_keys:
                        keys = sorted(o.inc_keys)
                        assert len(keys) == 1, keys
                        assert instr is not None
                        instr.then_inc(sems[keys[0]], 16 if o.dma else 1)
            return run

        block.sync(emit_stream("sync"))
        block.gpsimd(emit_stream("gpsimd"))
        block.vector(emit_stream("vector"))
        block.scalar(emit_stream("scalar"))
        block.tensor(emit_stream("tensor"))


class Carver:
    def __init__(self, tensor, total):
        self.t = tensor
        self.total = total
        self.off = 0

    def take(self, n):
        n = (n + 15) // 16 * 16
        ap = self.t[:, self.off:self.off + n]
        self.off += n
        assert self.off <= self.total, f"arena overflow {self.off} > {self.total}"
        return ap

    def reset(self):
        self.off = 0


def build_nc():
    nc = bass.Bass("TRN2", target_bir_lowering=False)

    x_full = nc.dram_tensor("x_full", [S, D], F32, kind="ExternalInput").ap()
    x_q = nc.dram_tensor("x_q", [SH, D], F32, kind="ExternalInput").ap()
    x_ffn = nc.dram_tensor("x_ffn", [SH, D], F32, kind="ExternalInput").ap()
    w_in = {}
    for nm, shp in (("wq", [D, D]), ("wk", [D, D]), ("wv", [D, D]), ("wo", [D, D]),
                    ("wg", [I, D]), ("wu", [I, D]), ("wd", [D, I])):
        w_in[nm] = nc.dram_tensor(nm, shp, F32, kind="ExternalInput").ap()
    out_d = nc.dram_tensor("out", [SH, D], F32, kind="ExternalOutput").ap()

    kt_hi = nc.dram_tensor("kt_hi", [D, S], BF16, kind="Internal").ap()
    kt_lo = nc.dram_tensor("kt_lo", [D, S], BF16, kind="Internal").ap()
    v_hi = nc.dram_tensor("v_hi", [S, D], BF16, kind="Internal").ap()
    v_lo = nc.dram_tensor("v_lo", [S, D], BF16, kind="Internal").ap()
    qt_hi = nc.dram_tensor("qt_hi", [D, SH], BF16, kind="Internal").ap()
    qt_lo = nc.dram_tensor("qt_lo", [D, SH], BF16, kind="Internal").ap()
    fin_d = nc.dram_tensor("fin_d", [SH, D], F32, kind="Internal").ap()
    wgq_hi = nc.dram_tensor("wgq_hi", [D, I], BF16, kind="Internal").ap()
    wgq_lo = nc.dram_tensor("wgq_lo", [D, I], BF16, kind="Internal").ap()
    wuq_hi = nc.dram_tensor("wuq_hi", [D, I], BF16, kind="Internal").ap()
    wuq_lo = nc.dram_tensor("wuq_lo", [D, I], BF16, kind="Internal").ap()
    wdq_hi = nc.dram_tensor("wdq_hi", [I, D], BF16, kind="Internal").ap()
    wdq_lo = nc.dram_tensor("wdq_lo", [I, D], BF16, kind="Internal").ap()

    from contextlib import ExitStack
    es = ExitStack()

    A32E = 28 * 1024 - 384
    A16E = 48 * 1024
    a32t = es.enter_context(nc.sbuf_tensor("a32", [P, A32E], F32))
    a16t = es.enter_context(nc.sbuf_tensor("a16", [P, A16E], BF16))
    ident = es.enter_context(nc.sbuf_tensor("ident", [P, P], F32))
    smalls = es.enter_context(nc.sbuf_tensor("smalls", [P, 176], F32))

    c0 = nc.const_aps.tensor(0.0, [P, 1], F32)
    c1 = nc.const_aps.tensor(1.0, [P, 1], F32)
    c_regs = [(c0, 0.0), (c1, 1.0)]
    for i, v in enumerate((0.125, 1e-5)):
        t = es.enter_context(nc.sbuf_tensor("c_%d" % i, [P, 1], F32))
        nc.const_aps.register(F32, v, t[:])
        c_regs.append((t[:], v))

    sems = {}
    for e in ENGINES:
        sems[e] = es.enter_context(nc.semaphore("s_" + e))
        sems["dma_" + e] = es.enter_context(nc.semaphore("sd_" + e))

    ps = [es.enter_context(nc.psum_tensor("ps%d" % i, [P, 512], F32))
          for i in range(8)]

    b = Builder()

    def sm(i, k=1):
        return smalls[:, i:i + k]

    r6 = sm(0)
    lnp = [[sm(2 + i) for i in range(5)] + [sm(104 + i) for i in range(5)],
           [sm(109 + i) for i in range(10)]]
    lb = [[sm(120 + 8 * pr + 4 * i, 4) for i in range(2)] for pr in range(2)]
    lb2 = [[sm(136 + 12 * pr + 4 * i, 4) for i in range(3)] for pr in range(2)]
    fl_cm = [sm(12), sm(13)]
    fl_cm8 = [sm(14), sm(15)]
    fl_al = [sm(16), sm(17)]
    fl_rs = [sm(18), sm(19)]
    fl_nm = [sm(100), sm(101)]
    fl_tmp = [sm(102), sm(103)]
    fl_mA = [sm(20 + t) for t in range(NQT)]
    fl_mB = [sm(36 + t) for t in range(NQT)]
    fl_lA = [sm(52 + t) for t in range(NQT)]
    fl_lB = [sm(68 + t) for t in range(NQT)]
    fl_linv = [sm(84 + t) for t in range(NQT)]

    b.op("gpsimd", lambda g: g.memset(ident[:], 0.0), writes=["identm"])

    def prolog(g):
        r = g.affine_select(out=ident[:], in_=ident[:], compare_op=ALU.not_equal,
                            fill=1.0, base=0, pattern=[[-1, P]],
                            channel_multiplier=1)
        for ap, v in c_regs:
            r = g.memset(ap, v)
        return g.memset(smalls[:], 0.0)

    b.op("gpsimd", prolog, reads=["identm"], writes=["ident", "consts", "smalls_i"])
    b.op("gpsimd", lambda g: g.memset(r6, 6.0), reads=["smalls_i"], writes=["r6raw"])
    b.op("vector", lambda v: v.reciprocal(sm(1), r6), reads=["r6raw"],
         writes=["r6t"], small=True)
    b.op("vector", lambda v: v.tensor_copy(r6, sm(1)), reads=["r6t"],
         writes=["r6"], small=True)
    b.barrier()

    def emit_ln(xb, xname, xnb, xnname, scr, scrname, pr=0):
        t_sum, t_ssq, t_m, t_e2, t_m2, t_var, t_sd, t_rstd, t_nm, t_nmr = lnp[pr]
        q = f"_{pr}"
        b.op("vector", lambda v: v.reduce_sum(t_sum, xb, axis=AX.X),
             reads=[xname], writes=["t_sum" + q], small=True)
        b.op("scalar", lambda s: s.activation(scr, xb, AF.Square, accum_out=t_ssq),
             reads=[xname], writes=[scrname, "t_ssq" + q], small=True)
        b.op("vector", lambda v: v.tensor_scalar(t_m, t_sum, 1.0 / D, None, ALU.mult),
             reads=["t_sum" + q], writes=["t_m" + q], small=True)
        b.op("vector", lambda v: v.tensor_scalar(t_e2, t_ssq, 1.0 / D, None, ALU.mult),
             reads=["t_ssq" + q], writes=["t_e2" + q], small=True)
        b.op("vector", lambda v: v.tensor_tensor(t_m2, t_m, t_m, ALU.mult),
             reads=["t_m" + q], writes=["t_m2" + q], small=True)
        b.op("vector", lambda v: v.tensor_tensor(t_var, t_e2, t_m2, ALU.subtract),
             reads=["t_e2" + q, "t_m2" + q], writes=["t_var" + q], small=True)
        b.op("scalar", lambda s: s.activation(t_sd, t_var, AF.Sqrt, bias=1e-5),
             reads=["t_var" + q], writes=["t_sd" + q], small=True)
        b.op("vector", lambda v: v.reciprocal(t_rstd, t_sd),
             reads=["t_sd" + q], writes=["t_rstd" + q], small=True)
        b.op("vector", lambda v: v.tensor_tensor(t_nm, t_m, t_rstd, ALU.mult),
             reads=["t_m" + q, "t_rstd" + q], writes=["t_nm" + q], small=True)
        b.op("vector", lambda v: v.tensor_scalar(t_nmr, t_nm, -1.0, None, ALU.mult),
             reads=["t_nm" + q], writes=["t_nmr" + q], small=True)
        b.op("scalar", lambda s: s.activation(xnb, xb, AF.Identity,
                                              bias=t_nmr, scale=t_rstd),
             reads=[xname, "t_nmr" + q, "t_rstd" + q], writes=[xnname])

    def emit_transpose_split(src, src_name, nch, dst_fn, dst_names, tag):
        for g in range(0, nch, 4):
            n = min(4, nch - g)
            bi = 6 + (g // 4) % 2
            bank = ps[bi]
            pname = f"psb{bi}"

            def mk_tp(g=g, n=n, bank=bank):
                def f(te):
                    r = None
                    for j in range(n):
                        r = te.transpose(bank[:, 128 * j:128 * (j + 1)],
                                         src[:, 128 * (g + j):128 * (g + j + 1)],
                                         ident[:])
                    return r
                return f
            b.op("tensor", mk_tp(), reads=[src_name, "ident"], writes=[pname])

            def mk_hi(g=g, n=n, bank=bank):
                def f(se):
                    r = None
                    for j in range(n):
                        hi, _ = dst_fn(g + j)
                        r = se.copy(hi, bank[:, 128 * j:128 * (j + 1)])
                    return r
                return f

            def mk_lo(g=g, n=n, bank=bank):
                def f(v):
                    r = None
                    for j in range(n):
                        hi, lo = dst_fn(g + j)
                        r = v.tensor_tensor(lo, bank[:, 128 * j:128 * (j + 1)],
                                            hi.bitcast(BF16), ALU.subtract)
                    return r
                return f
            b.op("scalar", mk_hi(), reads=[pname], writes=dst_names)
            b.op("vector", mk_lo(), reads=[pname] + dst_names, writes=dst_names)

    def emit_quant(xbuf, xq_out, F, scr, names_in, name_out, tag, scrtag=None):
        tag = scrtag or tag
        NB = F // 16
        absx, u, acc, cmp_t, w1 = (scr[k] for k in
                                   ("absx", "u", "acc", "cmp", "t1"))
        w2 = absx
        s_s, s_sc, s_thr = scr["s"], scr["sc"], scr["thr"]
        b.op("scalar", lambda s: s.activation(absx, xbuf, AF.Abs),
             reads=names_in, writes=[f"{tag}absx"])
        b.op("vector", lambda v: v.tensor_reduce(
            s_s, absx.rearrange("p (nb k) -> p nb k", k=16), axis=AX.X, op=ALU.max),
            reads=[f"{tag}absx"], writes=[f"{tag}s"], small=True)
        b.op("vector", lambda v: v.tensor_tensor(s_thr, s_s, r6.broadcast_to([P, NB]),
                                                 ALU.mult),
             reads=[f"{tag}s", "r6"], writes=[f"{tag}sct"], small=True)
        b.op("vector", lambda v: v.tensor_scalar(s_sc, s_thr, 1e-12, None, ALU.max),
             reads=[f"{tag}sct"], writes=[f"{tag}sc"], small=True)
        b.op("vector", lambda v: v.reciprocal(s_thr, s_sc),
             reads=[f"{tag}sc"], writes=[f"{tag}thr"], small=True)
        b.op("vector", lambda v: v.tensor_tensor(
            u.rearrange("p (nb k) -> p nb k", k=16),
            absx.rearrange("p (nb k) -> p nb k", k=16),
            s_thr.unsqueeze(2).broadcast_to([P, NB, 16]), ALU.mult),
            reads=[f"{tag}absx", f"{tag}thr"], writes=[f"{tag}u"])
        uu = f"{tag}u"
        b.op("vector", lambda v: v.tensor_scalar(acc, u, 0.25, None, ALU.is_gt),
             reads=[uu], writes=[f"{tag}acc"])
        b.op("vector", lambda v: v.scalar_tensor_tensor(cmp_t, u, 0.75, acc,
                                                        ALU.is_gt, ALU.add),
             reads=[uu, f"{tag}acc"], writes=[f"{tag}cmp"])
        b.op("vector", lambda v: v.scalar_tensor_tensor(acc, u, 1.25, cmp_t,
                                                        ALU.is_gt, ALU.add),
             reads=[uu, f"{tag}cmp"], writes=[f"{tag}acc"])
        b.op("vector", lambda v: v.scalar_tensor_tensor(cmp_t, u, 1.75, acc,
                                                        ALU.is_gt, ALU.add),
             reads=[uu, f"{tag}acc"], writes=[f"{tag}cmp"])
        b.op("vector", lambda v: v.tensor_scalar(w1, u, 2.5, None, ALU.is_gt),
             reads=[uu], writes=[f"{tag}w1"])
        b.op("vector", lambda v: v.scalar_tensor_tensor(w2, u, 3.5, w1,
                                                        ALU.is_gt, ALU.add),
             reads=[uu, f"{tag}w1"], writes=[f"{tag}absx"])
        b.op("vector", lambda v: v.scalar_tensor_tensor(acc, cmp_t, 0.5, w2,
                                                        ALU.mult, ALU.add),
             reads=[f"{tag}cmp", f"{tag}absx"], writes=[f"{tag}acc"])
        b.op("vector", lambda v: v.tensor_scalar(w1, u, 5.0, None, ALU.is_gt),
             reads=[uu], writes=[f"{tag}w1"])
        b.op("vector", lambda v: v.scalar_tensor_tensor(cmp_t, w1, 2.0, acc,
                                                        ALU.mult, ALU.add),
             reads=[f"{tag}w1", f"{tag}acc"], writes=[f"{tag}cmp"])
        b.op("vector", lambda v: v.tensor_tensor(
            u.rearrange("p (nb k) -> p nb k", k=16),
            cmp_t.rearrange("p (nb k) -> p nb k", k=16),
            s_sc.unsqueeze(2).broadcast_to([P, NB, 16]), ALU.mult),
            reads=[f"{tag}cmp", f"{tag}sc"], writes=[f"{tag}u"])
        b.op("vector", lambda v: v.tensor_scalar(w1, xbuf, 0.0, 2.0,
                                                 ALU.is_ge, ALU.mult),
             reads=names_in, writes=[f"{tag}w1"])
        b.op("vector", lambda v: v.scalar_tensor_tensor(xq_out, w1, -1.0, u,
                                                        ALU.add, ALU.mult),
             reads=[f"{tag}w1", f"{tag}u"], writes=[name_out])

    def emit_mm3(bank_ap, pname, pairs, reads, nfirst=True):
        def f(te):
            r = None
            n = len(pairs) * 3
            i = 0
            for (lh, ll, rh, rl) in pairs:
                for (a, c) in ((lh, rh), (lh, rl), (ll, rh)):
                    r = te.matmul(bank_ap, a, c, start=(i == 0), stop=(i == n - 1))
                    i += 1
            return r
        b.op("tensor", f, reads=list(reads), writes=[pname])

    c32 = Carver(a32t, A32E)
    c16 = Carver(a16t, A16E)
    wload = c32.take(8 * 1024)
    xtile = [c32.take(1024), c32.take(1024)]
    xntile = [c32.take(1024), c32.take(1024)]
    lnscr = [c32.take(1024), c32.take(1024)]
    wspl_hi = c16.take(8 * 1024)
    wspl_lo = c16.take(8 * 1024)
    wsplv_hi = c16.take(8 * 1024)
    wsplv_lo = c16.take(8 * 1024)
    win_hi = [c16.take(NDC * 512)]
    win_lo = [c16.take(NDC * 512)]
    st_hi = c16.take(NDC * 512)
    st_lo = c16.take(NDC * 512)

    def emit_weight_T_split(w_dram, tag, whi=None, wlo=None, wnm="wspl"):
        whi = wspl_hi if whi is None else whi
        wlo = wspl_lo if wlo is None else wlo
        for i in range(8):
            b.op("sync", (lambda i=i: lambda s: s.dma_start(
                out=wload[:, 1024 * i:1024 * (i + 1)],
                in_=w_dram[128 * i:128 * (i + 1), :]))(),
                writes=[f"wl{i}"], dma=True)
        for i in range(8):
            def dst(jd, i=i):
                o = 1024 * jd + 128 * i
                return (whi[:, o:o + 128], wlo[:, o:o + 128])
            emit_transpose_split(wload[:, 1024 * i:1024 * (i + 1)], f"wl{i}",
                                 8, dst, [wnm], f"{tag}t{i}")

    def p1_sweep(x_dram, nsup, mode, hi_dram, lo_dram, tag, hook=None):
        xt4 = [xtile[0], xtile[1], xntile[0], xntile[1]]
        for sup in range(nsup):
            wpar = 0
            winh, winl = win_hi[wpar], win_lo[wpar]
            wnm = "win%d" % wpar
            spr = sup % 2
            for tt in range(4):
                gt = sup * 4 + tt
                xb = xt4[tt]
                xnb = lnscr[tt % 2]
                xname = f"x4_{tt}"
                xnname = "lnscr%d" % (tt % 2)
                b.op("sync", (lambda gt=gt, xb=xb: lambda s: s.dma_start(
                    out=xb, in_=x_dram[128 * gt:128 * (gt + 1), :]))(),
                    writes=[xname], dma=True)
                emit_ln(xb, xname, xnb, xnname, pw_q[:, 1024 * (tt % 2):1024 * (tt % 2 + 1)],
                        "xsc%d" % (tt % 2), pr=tt % 2)

                def dst(jd, tt=tt, winh=winh, winl=winl):
                    o = 512 * jd + 128 * tt
                    return (winh[:, o:o + 128], winl[:, o:o + 128])
                emit_transpose_split(xnb, xnname, 8, dst, [wnm], f"{tag}w{gt}")
            if mode in ("K", "Q", "KV"):
                for ie in range(8):
                    bank = ps[ie % 3]
                    pname = f"psk{ie % 3}"
                    pairs = [(wspl_hi[:, 1024 * jd + 128 * ie:1024 * jd + 128 * (ie + 1)],
                              wspl_lo[:, 1024 * jd + 128 * ie:1024 * jd + 128 * (ie + 1)],
                              winh[:, 512 * jd:512 * (jd + 1)],
                              winl[:, 512 * jd:512 * (jd + 1)]) for jd in range(NDC)]
                    emit_mm3(bank[:, :], pname, pairs, ["wspl", wnm])
                    b.op("scalar", (lambda bank=bank, ie=ie: lambda s: s.copy(
                        st_hi[:, 512 * ie:512 * (ie + 1)], bank[:, :]))(),
                        reads=[pname], writes=["stg"])
                    b.op("vector", (lambda bank=bank, ie=ie: lambda v: v.tensor_tensor(
                        st_lo[:, 512 * ie:512 * (ie + 1)], bank[:, :],
                        st_hi[:, 512 * ie:512 * (ie + 1)].bitcast(BF16),
                        ALU.subtract))(),
                        reads=[pname, "stg"], writes=["stg"])
                slen = SH if mode == "Q" else S
                for dram, stg in ((hi_dram, st_hi), (lo_dram, st_lo)):
                    b.op("gpsimd", (lambda dram=dram, stg=stg, sup=sup, slen=slen:
                                    lambda g: g.dma_start(
                        out=bass.AP(dram.tensor, 512 * sup,
                                    [[slen, P], [P * slen, 8], [1, 512]]),
                        in_=stg.rearrange("p (j n) -> p j n", n=512)))(),
                        reads=["stg"], writes=["std"], dma=True)
                if hook:
                    hook()
            if mode == "KV":
                for tt in range(4):
                    for eh in range(2):
                        bank = ps[(2 * tt + eh) % 3]
                        pname = f"psk{(2 * tt + eh) % 3}"
                        pairs = [(winh[:, 512 * jd + 128 * tt:512 * jd + 128 * (tt + 1)],
                                  winl[:, 512 * jd + 128 * tt:512 * jd + 128 * (tt + 1)],
                                  wsplv_hi[:, 1024 * jd + 512 * eh:1024 * jd + 512 * (eh + 1)],
                                  wsplv_lo[:, 1024 * jd + 512 * eh:1024 * jd + 512 * (eh + 1)])
                                 for jd in range(NDC)]
                        emit_mm3(bank[:, :], pname, pairs, ["wsplv", wnm])
                        o = 1024 * tt + 512 * eh
                        b.op("scalar", (lambda bank=bank, o=o: lambda s: s.copy(
                            st_hi.bitcast(BF16)[:, o:o + 512], bank[:, :]))(),
                            reads=[pname], writes=["stg"])
                        b.op("vector", (lambda bank=bank, o=o: lambda v: v.tensor_tensor(
                            st_lo.bitcast(BF16)[:, o:o + 512], bank[:, :],
                            st_hi.bitcast(BF16)[:, o:o + 512], ALU.subtract))(),
                            reads=[pname, "stg"], writes=["stg"])
                for dram, stg in ((v_hi, st_hi), (v_lo, st_lo)):
                    b.op("gpsimd", (lambda dram=dram, stg=stg, sup=sup:
                                    lambda g: g.dma_start(
                        out=bass.AP(dram.tensor, 512 * sup * D,
                                    [[D, P], [P * D, 4], [1, 1024]]),
                        in_=stg.rearrange("p (j n) -> p j n", n=1024)))(),
                        reads=["stg"], writes=["std"], dma=True)
            if hook:
                hook()

    pw_load = c32.take(2816)
    pw_q = c32.take(2816)
    pq_scr = {k: c32.take(1408) for k in ("absx", "u", "acc", "cmp", "t1")}
    pq_sm = {k: c32.take(96) for k in ("s", "sc", "thr")}
    pw_st_hi = win_hi[0][:, 0:NIT * 128]
    pw_st_lo = win_lo[0][:, 0:NIT * 128]

    def qscr(F):
        NB = F // 16
        d = {k: pq_scr[k][:, 0:F] for k in pq_scr}
        d.update({k: pq_sm[k][:, 0:NB] for k in pq_sm})
        return d

    prep_steps = []

    def prep_gu_step(w_dram, out_hi_d, out_lo_d, tag, ri):
        b.op("sync", (lambda ri=ri: lambda s: s.dma_start(
            out=pw_load[:, 0:1024], in_=w_dram[128 * ri:128 * (ri + 1), :]))(),
            writes=["pwl"], dma=True)
        emit_quant(pw_load[:, 0:1024], pw_q[:, 0:1024], 1024, qscr(1024),
                   ["pwl"], "pwq", f"{tag}q", scrtag="pwscr")

        def dst(jd):
            return (pw_st_hi[:, 128 * jd:128 * (jd + 1)],
                    pw_st_lo[:, 128 * jd:128 * (jd + 1)])
        emit_transpose_split(pw_q[:, 0:1024], "pwq", 8, dst,
                             ["pwst"], f"{tag}t{ri}")
        for dram, stg in ((out_hi_d, pw_st_hi), (out_lo_d, pw_st_lo)):
            b.op("gpsimd", (lambda dram=dram, stg=stg, ri=ri: lambda g: g.dma_start(
                out=bass.AP(dram.tensor, 128 * ri,
                            [[I, P], [P * I, 8], [1, 128]]),
                in_=stg.rearrange("p (j n) -> p j n", n=128)[:, 0:8, :]))(),
                reads=["pwst"], writes=[f"{tag}d"], dma=True)

    def prep_wd_step(rd):
        b.op("sync", (lambda rd=rd: lambda s: s.dma_start(
            out=pw_load[:, 0:2816], in_=w_in["wd"][128 * rd:128 * (rd + 1), :]))(),
            writes=["pwl"], dma=True)
        for hf in range(2):
            F = 1408
            emit_quant(pw_load[:, F * hf:F * (hf + 1)], pw_q[:, F * hf:F * (hf + 1)],
                       F, qscr(F), ["pwl"], "pwq", f"pdq{hf}", scrtag="pwscr")

        def dstD(ji):
            return (pw_st_hi[:, 128 * ji:128 * (ji + 1)],
                    pw_st_lo[:, 128 * ji:128 * (ji + 1)])
        emit_transpose_split(pw_q[:, 0:2816], "pwq", NIT, dstD, ["pwst"],
                             f"pdt{rd}")
        for dram, stg in ((wdq_hi, pw_st_hi), (wdq_lo, pw_st_lo)):
            for half in range(2):
                n0 = 11 * half
                b.op("gpsimd", (lambda dram=dram, stg=stg, rd=rd, n0=n0:
                                lambda g: g.dma_start(
                    out=bass.AP(dram.tensor, 128 * n0 * D + 128 * rd,
                                [[D, P], [P * D, 11], [1, 128]]),
                    in_=stg.rearrange("p (j n) -> p j n", n=128)[:, n0:n0 + 11, :]))(),
                    reads=["pwst"], writes=["pdd"], dma=True)

    for ri in range(NIT):
        prep_steps.append((lambda ri=ri: prep_gu_step(w_in["wg"], wgq_hi, wgq_lo, "pg", ri)))
    for ri in range(NIT):
        prep_steps.append((lambda ri=ri: prep_gu_step(w_in["wu"], wuq_hi, wuq_lo, "pu", ri)))
    for rd in range(8):
        prep_steps.append((lambda rd=rd: prep_wd_step(rd)))
    prep_iter = iter(prep_steps)

    def prep_hook(n=1):
        for _ in range(n):
            st = next(prep_iter, None)
            if st is not None:
                st()

    emit_weight_T_split(w_in["wk"], "k")
    emit_weight_T_split(w_in["wv"], "v", wsplv_hi, wsplv_lo, "wsplv")
    p1_sweep(x_full, 8, "KV", kt_hi, kt_lo, "f")
    b.barrier()
    emit_weight_T_split(w_in["wq"], "q")
    p1_sweep(x_q, 4, "Q", qt_hi, qt_lo, "h")
    while True:
        st = next(prep_iter, None)
        if st is None:
            break
        st()
    b.barrier()

    c32.reset()
    c16.reset()
    acc = c32.take(NQT * 1024)
    p_sb = [c32.take(512), c32.take(512)]
    av_tmp = c32.take(1024)
    av_tmp2 = c32.take(1024)
    kch_hi = [c16.take(NDC * 512), c16.take(NDC * 512)]
    kch_lo = [c16.take(NDC * 512), c16.take(NDC * 512)]
    vch_hi = [c16.take(4 * 1024), c16.take(4 * 1024)]
    vch_lo = [c16.take(4 * 1024), c16.take(4 * 1024)]
    qtb_hi = [c16.take(NDC * 128), c16.take(NDC * 128)]
    qtb_lo = [c16.take(NDC * 128), c16.take(NDC * 128)]
    at_hi = [c16.take(4 * 128), c16.take(4 * 128)]
    at_lo = [c16.take(4 * 128), c16.take(4 * 128)]

    def mk_3d_load(dram, dst, off, nblk, blklen, rowstride, bufstride=None):
        bs = bufstride or blklen

        def f(s):
            return s.dma_start(
                out=dst.rearrange("p (j n) -> p j n", n=bs)[:, 0:nblk, 0:blklen],
                in_=bass.AP(dram.tensor, off,
                            [[rowstride, P], [P * rowstride, nblk], [1, blklen]]))
        return f

    def p2_loads(c):
        cp = c % 2
        b.op("sync", mk_3d_load(kt_hi, kch_hi[cp], 512 * c, 8, 512, S),
             reads=["std"], writes=[f"kch{cp}"], dma=True)
        b.op("sync", mk_3d_load(kt_lo, kch_lo[cp], 512 * c, 8, 512, S),
             reads=["std"], writes=[f"kch{cp}"], dma=True)
        b.op("sync", mk_3d_load(v_hi, vch_hi[cp], 512 * c * D, 4, 1024, D),
             reads=["std"], writes=[f"vch{cp}"], dma=True)
        b.op("sync", mk_3d_load(v_lo, vch_lo[cp], 512 * c * D, 4, 1024, D),
             reads=["std"], writes=[f"vch{cp}"], dma=True)

    def p2_qt_load(t):
        tp = t % 2
        b.op("sync", mk_3d_load(qt_hi, qtb_hi[tp], 128 * t, 8, 128, SH),
             reads=["std"], writes=[f"qtb{tp}"], dma=True)
        b.op("sync", mk_3d_load(qt_lo, qtb_lo[tp], 128 * t, 8, 128, SH),
             reads=["std"], writes=[f"qtb{tp}"], dma=True)

    def p2_scores(c, t):
        cp, tp = c % 2, t % 2
        sbank = ps[tp]
        sname = f"pss{tp}"
        pairs = [(qtb_hi[tp][:, 128 * jd:128 * (jd + 1)],
                  qtb_lo[tp][:, 128 * jd:128 * (jd + 1)],
                  kch_hi[cp][:, 512 * jd:512 * (jd + 1)],
                  kch_lo[cp][:, 512 * jd:512 * (jd + 1)]) for jd in range(NDC)]
        emit_mm3(sbank[:, :], sname, pairs, [f"qtb{tp}", f"kch{cp}"])

    def p2_post(c, t):
        cp, tp = c % 2, t % 2
        sbank = ps[tp]
        sname = f"pss{tp}"
        b.op("vector", (lambda sbank=sbank, tp=tp: lambda v: v.reduce_max(
            fl_cm[tp], sbank[:, :], axis=AX.X))(),
            reads=[sname], writes=[f"fl_cm{tp}"], small=True)
        b.op("vector", (lambda tp=tp: lambda v: v.tensor_scalar(
            fl_cm8[tp], fl_cm[tp], 0.125, None, ALU.mult))(),
            reads=[f"fl_cm{tp}"], writes=[f"fl_cm8{tp}"], small=True)
        src_m = fl_mB[t] if c % 2 == 1 else fl_mA[t]
        dst_m = fl_mA[t] if c % 2 == 1 else fl_mB[t]
        if c == 0:
            b.op("vector", (lambda dst_m=dst_m, tp=tp: lambda v: v.tensor_copy(
                dst_m, fl_cm8[tp]))(),
                reads=[f"fl_cm8{tp}"], writes=[f"fl_m{t}"], small=True)
        else:
            b.op("vector", (lambda src_m=src_m, tp=tp: lambda v: v.tensor_tensor(
                fl_tmp[tp], src_m, fl_cm8[tp], ALU.max))(),
                reads=[f"fl_m{t}", f"fl_cm8{tp}"], writes=[f"fl_tmp{tp}"], small=True)
            b.op("vector", (lambda dst_m=dst_m, tp=tp: lambda v: v.tensor_copy(
                dst_m, fl_tmp[tp]))(),
                reads=[f"fl_tmp{tp}"], writes=[f"fl_m{t}"], small=True)
        cur_m = dst_m
        b.op("vector", (lambda cur_m=cur_m, tp=tp: lambda v: v.tensor_scalar(
            fl_nm[tp], cur_m, -1.0, None, ALU.mult))(),
            reads=[f"fl_m{t}"], writes=[f"fl_nm{tp}"], small=True)
        b.op("scalar", (lambda sbank=sbank, tp=tp: lambda s: s.activation(
            p_sb[tp], sbank[:, :], AF.Exp, bias=fl_nm[tp], scale=0.125,
            accum_out=fl_rs[tp]))(),
            reads=[sname, f"fl_nm{tp}"], writes=[f"p_sb{tp}", f"fl_rs{tp}"], small=True)
        if c == 0:
            b.op("vector", (lambda t=t, tp=tp: lambda v: v.tensor_copy(
                fl_lB[t], fl_rs[tp]))(),
                reads=[f"fl_rs{tp}"], writes=[f"fl_l{t}"], small=True)
        else:
            src_l = fl_lB[t] if c % 2 == 1 else fl_lA[t]
            dst_l = fl_lA[t] if c % 2 == 1 else fl_lB[t]
            b.op("vector", (lambda src_m=src_m, cur_m=cur_m, tp=tp: lambda v:
                            v.tensor_tensor(fl_tmp[tp], src_m, cur_m,
                                            ALU.subtract))(),
                 reads=[f"fl_m{t}"], writes=[f"fl_tmp{tp}"], small=True)
            b.op("scalar", (lambda tp=tp: lambda s: s.activation(
                fl_al[tp], fl_tmp[tp], AF.Exp))(),
                reads=[f"fl_tmp{tp}"], writes=[f"fl_al{tp}"], small=True)
            b.op("vector", (lambda src_l=src_l, tp=tp: lambda v: v.tensor_tensor(
                fl_tmp[tp], src_l, fl_al[tp], ALU.mult))(),
                reads=[f"fl_l{t}", f"fl_al{tp}"], writes=[f"fl_tmp{tp}"], small=True)
            b.op("vector", (lambda dst_l=dst_l, tp=tp: lambda v: v.tensor_tensor(
                dst_l, fl_tmp[tp], fl_rs[tp], ALU.add))(),
                reads=[f"fl_tmp{tp}", f"fl_rs{tp}"], writes=[f"fl_l{t}"], small=True)
        tbank = ps[2 + tp]
        tname = f"pst{tp}"
        b.op("tensor", (lambda tp=tp, tbank=tbank: lambda te: [
            te.transpose(tbank[:, 128 * j:128 * (j + 1)],
                         p_sb[tp][:, 128 * j:128 * (j + 1)], ident[:])
            for j in range(4)][-1])(),
            reads=[f"p_sb{tp}", "ident"], writes=[tname])
        b.op("scalar", (lambda tbank=tbank, tp=tp: lambda s: s.copy(
            at_hi[tp], tbank[:, :]))(),
            reads=[tname], writes=[f"at{tp}"])
        b.op("vector", (lambda tbank=tbank, tp=tp: lambda v: v.tensor_tensor(
            at_lo[tp], tbank[:, :], at_hi[tp], ALU.subtract))(),
            reads=[tname, f"at{tp}"], writes=[f"at{tp}"])
        for eh in range(2):
            abank = ps[4 + 2 * tp + eh]
            aname = f"psa{2 * tp + eh}"
            pairs = [(at_hi[tp][:, 128 * ks:128 * (ks + 1)],
                      at_lo[tp][:, 128 * ks:128 * (ks + 1)],
                      vch_hi[cp][:, 1024 * ks + 512 * eh:1024 * ks + 512 * (eh + 1)],
                      vch_lo[cp][:, 1024 * ks + 512 * eh:1024 * ks + 512 * (eh + 1)])
                     for ks in range(4)]
            emit_mm3(abank[:, :], aname, pairs, [f"at{tp}", f"vch{cp}"])
            aslice = acc[:, 1024 * t + 512 * eh:1024 * t + 512 * (eh + 1)]
            accn = f"acc{t}_{eh}"
            if c == 0:
                b.op("vector", (lambda abank=abank, aslice=aslice: lambda v:
                                v.tensor_copy(aslice, abank[:, :]))(),
                     reads=[aname], writes=[accn])
            else:
                tsl = (av_tmp if tp == 0 else av_tmp2)[:, 512 * eh:512 * (eh + 1)]
                b.op("scalar", (lambda aslice=aslice, tsl=tsl, tp=tp: lambda s:
                                s.activation(tsl, aslice, AF.Identity,
                                             scale=fl_al[tp]))(),
                     reads=[accn, f"fl_al{tp}"], writes=[f"avt{tp}{eh}"])
                b.op("vector", (lambda abank=abank, aslice=aslice, tsl=tsl:
                                lambda v: v.tensor_tensor(
                                    aslice, tsl, abank[:, :], ALU.add))(),
                     reads=[f"avt{tp}{eh}", aname], writes=[accn])

    p2_loads(0)
    p2_qt_load(0)
    for c in range(NKCH):
        for t in range(NQT):
            if t + 1 < NQT:
                p2_qt_load(t + 1)
            elif c + 1 < NKCH:
                p2_qt_load(0)
            p2_scores(c, t)
            if t == 8 and c + 1 < NKCH:
                p2_loads(c + 1)
            if t > 0:
                p2_post(c, t - 1)
        p2_post(c, NQT - 1)

    b.barrier()
    c16.reset()
    g_hi = c16.take(NQT * 1024)
    g_lo = c16.take(NQT * 1024)
    woT_hi = c16.take(8 * 1024)
    woT_lo = c16.take(8 * 1024)
    last_l = fl_lB if (NKCH - 1) % 2 == 0 else fl_lA
    for t in range(NQT):
        b.op("vector", (lambda t=t: lambda v: v.reciprocal(
            fl_linv[t], last_l[t]))(),
            reads=[f"fl_l{t}"], writes=[f"fl_linv{t}"], small=True)
        b.op("vector", (lambda t=t: lambda v: v.tensor_tensor(
            av_tmp, acc[:, 1024 * t:1024 * (t + 1)],
            fl_linv[t].broadcast_to([P, 1024]), ALU.mult))(),
            reads=[f"acc{t}_0", f"acc{t}_1", f"fl_linv{t}"], writes=["avn"])
        b.op("vector", (lambda t=t: lambda v: v.tensor_copy(
            g_hi[:, 1024 * t:1024 * (t + 1)], av_tmp))(),
            reads=["avn"], writes=["grhs"])
        b.op("vector", (lambda t=t: lambda v: v.tensor_tensor(
            g_lo[:, 1024 * t:1024 * (t + 1)], av_tmp,
            g_hi[:, 1024 * t:1024 * (t + 1)], ALU.subtract))(),
            reads=["avn", "grhs"], writes=["grhs"])

    b.barrier()
    c32.reset()
    wload2 = c32.take(8 * 1024)
    gtmp = c32.take(1024)
    finst = c32.take(1024)
    for i in range(8):
        b.op("sync", (lambda i=i: lambda s: s.dma_start(
            out=wload2[:, 1024 * i:1024 * (i + 1)],
            in_=w_in["wo"][128 * i:128 * (i + 1), :]))(),
            writes=[f"owl{i}"], dma=True)
    for i in range(8):
        def dst(jd, i=i):
            o = 1024 * jd + 128 * i
            return (woT_hi[:, o:o + 128], woT_lo[:, o:o + 128])
        emit_transpose_split(wload2[:, 1024 * i:1024 * (i + 1)], f"owl{i}",
                             8, dst, ["woT"], f"ot{i}")

    for r in range(2):
        for g in range(8):
            for ie in range(8):
                bank = ps[ie % 2]
                pname = f"psg{ie % 2}"
                pairs = [(woT_hi[:, 1024 * jd + 128 * ie:1024 * jd + 128 * (ie + 1)],
                          woT_lo[:, 1024 * jd + 128 * ie:1024 * jd + 128 * (ie + 1)],
                          g_hi[:, 1024 * (8 * r + jd) + 128 * g:
                               1024 * (8 * r + jd) + 128 * (g + 1)],
                          g_lo[:, 1024 * (8 * r + jd) + 128 * g:
                               1024 * (8 * r + jd) + 128 * (g + 1)])
                         for jd in range(NDC)]
                emit_mm3(bank[:, 0:128], pname, pairs, ["woT", "grhs"])
                b.op("vector", (lambda bank=bank, ie=ie: lambda v: v.tensor_copy(
                    gtmp[:, 128 * ie:128 * (ie + 1)], bank[:, 0:128]))(),
                    reads=[pname], writes=["gtmp"])
            for ie in range(0, 8, 4):
                bi = 2 + (ie // 4) % 2
                bank = ps[bi]
                pname = f"psq{bi}"
                b.op("tensor", (lambda ie=ie, bank=bank: lambda te: [
                    te.transpose(bank[:, 128 * j:128 * (j + 1)],
                                 gtmp[:, 128 * (ie + j):128 * (ie + j + 1)],
                                 ident[:])
                    for j in range(4)][-1])(),
                    reads=["gtmp", "ident"], writes=[pname])
                b.op("vector", (lambda ie=ie, bank=bank: lambda v: v.tensor_copy(
                    finst[:, 128 * ie:128 * (ie + 4)], bank[:, :]))(),
                    reads=[pname], writes=["finst"])
            gg = 8 * r + g
            b.op("gpsimd", (lambda gg=gg: lambda gp: gp.dma_start(
                out=fin_d[128 * gg:128 * (gg + 1), :], in_=finst))(),
                reads=["finst"], writes=["find"], dma=True)

    b.barrier()

    c32.reset()
    c16.reset()
    ffn_acc = c32.take(8 * 1024)
    fx = c32.take(1024)
    fxn = c32.take(1024)
    fq = c32.take(1024)
    f_lnscr = c32.take(1024)
    f_scr = {k: c32.take(1024) for k in ("absx", "u", "acc", "cmp", "t1", "t2")}
    f_sm = {k: c32.take(64) for k in ("s", "sc", "thr")}
    h_sb = c32.take(512)
    hq_sb = c32.take(512)
    silu_sb = c32.take(512)
    finld = c32.take(1024)
    res1 = c32.take(1024)
    dn_tmp = c32.take(1024)
    res2 = c32.take(1024)

    xqT_hi = c16.take(NDC * 1024)
    xqT_lo = c16.take(NDC * 1024)
    _wg1, _wg2 = c16.take(NDC * 512), c16.take(NDC * 512)
    _wu1, _wu2 = c16.take(NDC * 512), c16.take(NDC * 512)
    _wd1, _wd2 = c16.take(4 * 1024), c16.take(4 * 1024)
    wgc_hi, wgc_lo = [_wg1, _wg1], [_wg2, _wg2]
    wuc_hi, wuc_lo = [_wu1, _wu1], [_wu2, _wu2]
    wdc_hi, wdc_lo = [_wd1, _wd1], [_wd2, _wd2]
    hqT_hi = c16.take(4 * 128)
    hqT_lo = c16.take(4 * 128)

    def fscr(F):
        NB = F // 16
        d = {k: f_scr[k][:, 0:F] for k in f_scr}
        d.update({k: f_sm[k][:, 0:NB] for k in f_sm})
        return d

    for grp in range(2):
        for tt in range(8):
            gg = 8 * grp + tt
            b.op("sync", (lambda gg=gg: lambda s: s.dma_start(
                out=fx, in_=x_ffn[128 * gg:128 * (gg + 1), :]))(),
                writes=["fx"], dma=True)
            emit_ln(fx, "fx", fxn, "fxn", f_lnscr, "flnscr", pr=0)
            emit_quant(fxn, fq, 1024, fscr(1024), ["fxn"], "fq", "fq", scrtag="fscr")

            def dst(jd, tt=tt):
                o = 1024 * jd + 128 * tt
                return (xqT_hi[:, o:o + 128], xqT_lo[:, o:o + 128])
            emit_transpose_split(fq, "fq", 8, dst, ["xqT"], f"xt{gg}")

        def p3_gu(ci, io, ilen, tt):
            wp = 0
            gbank, ubank = ps[tt % 2], ps[2 + tt % 2]
            gname, uname = f"psG{tt % 2}", f"psU{tt % 2}"
            wsl = [(512 * jd, 512 * jd + ilen) for jd in range(NDC)]
            gpairs = [(xqT_hi[:, 1024 * jd + 128 * tt:1024 * jd + 128 * (tt + 1)],
                       xqT_lo[:, 1024 * jd + 128 * tt:1024 * jd + 128 * (tt + 1)],
                       wgc_hi[wp][:, a:z], wgc_lo[wp][:, a:z])
                      for jd, (a, z) in enumerate(wsl)]
            upairs = [(xqT_hi[:, 1024 * jd + 128 * tt:1024 * jd + 128 * (tt + 1)],
                       xqT_lo[:, 1024 * jd + 128 * tt:1024 * jd + 128 * (tt + 1)],
                       wuc_hi[wp][:, a:z], wuc_lo[wp][:, a:z])
                      for jd, (a, z) in enumerate(wsl)]
            emit_mm3(gbank[:, 0:ilen], gname, gpairs, ["xqT", f"wgc{wp}"])
            emit_mm3(ubank[:, 0:ilen], uname, upairs, ["xqT", f"wuc{wp}"])

        def p3_post(ci, io, ilen, tt):
            wp = 0
            nsub = ilen // 128
            gbank, ubank = ps[tt % 2], ps[2 + tt % 2]
            gname, uname = f"psG{tt % 2}", f"psU{tt % 2}"
            b.op("scalar", (lambda gbank=gbank, ilen=ilen: lambda s: s.activation(
                silu_sb[:, 0:ilen], gbank[:, 0:ilen], AF.Silu))(),
                reads=[gname], writes=["silu"])
            b.op("vector", (lambda ubank=ubank, ilen=ilen: lambda v: v.tensor_tensor(
                h_sb[:, 0:ilen], silu_sb[:, 0:ilen], ubank[:, 0:ilen], ALU.mult))(),
                reads=["silu", uname], writes=["h"])
            emit_quant(h_sb[:, 0:ilen], hq_sb[:, 0:ilen], ilen, fscr(ilen),
                       ["h"], "hq", "hq", scrtag="fscr")
            tb = ps[4 + tt % 2]
            tn = f"psH{tt % 2}"
            b.op("tensor", (lambda tb=tb, nsub=nsub: lambda te: [
                te.transpose(tb[:, 128 * j:128 * (j + 1)],
                             hq_sb[:, 128 * j:128 * (j + 1)], ident[:])
                for j in range(nsub)][-1])(),
                reads=["hq", "ident"], writes=[tn])
            b.op("scalar", (lambda tb=tb, nsub=nsub: lambda s: s.copy(
                hqT_hi[:, 0:128 * nsub], tb[:, 0:128 * nsub]))(),
                reads=[tn], writes=["hqT"])
            b.op("vector", (lambda tb=tb, nsub=nsub: lambda v: v.tensor_tensor(
                hqT_lo[:, 0:128 * nsub], tb[:, 0:128 * nsub],
                hqT_hi[:, 0:128 * nsub], ALU.subtract))(),
                reads=[tn, "hqT"], writes=["hqT"])
            for dh in range(2):
                dbank = ps[6 + (tt + dh) % 2]
                dname = f"psD{(tt + dh) % 2}"
                pairs = [(hqT_hi[:, 128 * ks:128 * (ks + 1)],
                          hqT_lo[:, 128 * ks:128 * (ks + 1)],
                          wdc_hi[wp][:, 1024 * ks + 512 * dh:
                                     1024 * ks + 512 * (dh + 1)],
                          wdc_lo[wp][:, 1024 * ks + 512 * dh:
                                     1024 * ks + 512 * (dh + 1)])
                         for ks in range(nsub)]
                emit_mm3(dbank[:, :], dname, pairs, ["hqT", f"wdc{wp}"])
                fslice = ffn_acc[:, 1024 * tt + 512 * dh:
                                 1024 * tt + 512 * (dh + 1)]
                fname = f"facc{tt}_{dh}"
                if ci == 0:
                    b.op("vector", (lambda dbank=dbank, fslice=fslice:
                                    lambda v: v.tensor_copy(fslice,
                                                            dbank[:, :]))(),
                         reads=[dname], writes=[fname])
                else:
                    tsl = dn_tmp[:, 512 * dh:512 * (dh + 1)]
                    b.op("scalar", (lambda fslice=fslice, tsl=tsl: lambda s:
                                    s.copy(tsl, fslice))(),
                         reads=[fname], writes=[f"dnt{dh}"])
                    b.op("vector", (lambda dbank=dbank, fslice=fslice,
                                    tsl=tsl: lambda v: v.tensor_tensor(
                                        fslice, tsl, dbank[:, :], ALU.add))(),
                         reads=[f"dnt{dh}", dname], writes=[fname])

        for ci, (io, ilen) in enumerate(ICHUNKS):
            wp = 0
            nsub = ilen // 128
            b.op("sync", mk_3d_load(wgq_hi, wgc_hi[wp], io, 8, ilen, I, 512),
                 reads=["pgd"], writes=[f"wgc{wp}"], dma=True)
            b.op("sync", mk_3d_load(wgq_lo, wgc_lo[wp], io, 8, ilen, I, 512),
                 reads=["pgd"], writes=[f"wgc{wp}"], dma=True)
            b.op("sync", mk_3d_load(wuq_hi, wuc_hi[wp], io, 8, ilen, I, 512),
                 reads=["pud"], writes=[f"wuc{wp}"], dma=True)
            b.op("sync", mk_3d_load(wuq_lo, wuc_lo[wp], io, 8, ilen, I, 512),
                 reads=["pud"], writes=[f"wuc{wp}"], dma=True)
            b.op("sync", mk_3d_load(wdq_hi, wdc_hi[wp], io * D, nsub, 1024, D),
                 reads=["pdd"], writes=[f"wdc{wp}"], dma=True)
            b.op("sync", mk_3d_load(wdq_lo, wdc_lo[wp], io * D, nsub, 1024, D),
                 reads=["pdd"], writes=[f"wdc{wp}"], dma=True)
            for tt in range(8):
                p3_gu(ci, io, ilen, tt)
                if tt > 0:
                    p3_post(ci, io, ilen, tt - 1)
            p3_post(ci, io, ilen, 7)
        for tt in range(8):
            gg = 8 * grp + tt
            b.op("sync", (lambda gg=gg: lambda s: s.dma_start(
                out=finld, in_=fin_d[128 * gg:128 * (gg + 1), :]))(),
                reads=["find"], writes=["finld"], dma=True)
            b.op("sync", (lambda gg=gg: lambda s: s.dma_start(
                out=fx, in_=x_ffn[128 * gg:128 * (gg + 1), :]))(),
                writes=["fx"], dma=True)
            b.op("vector", lambda v: v.tensor_tensor(res1, fx, finld, ALU.add),
                 reads=["fx", "finld"], writes=["res1"])
            b.op("vector", (lambda tt=tt: lambda v: v.tensor_tensor(
                res2, res1, ffn_acc[:, 1024 * tt:1024 * (tt + 1)], ALU.add))(),
                reads=["res1", f"facc{tt}_0", f"facc{tt}_1"], writes=["res2"])
            b.op("gpsimd", (lambda gg=gg: lambda g: g.dma_start(
                out=out_d[128 * gg:128 * (gg + 1), :], in_=res2))(),
                reads=["res2"], writes=["outd"], dma=True)
        if grp == 0:
            b.barrier()

    final_out = b.buf["outd"][0]
    key = "dma_gpsimd"
    final_out.inc_keys.add(key)
    wait_op = b.op("sync", None)
    wait_op.waits.append((key, final_out))

    with nc.Block() as block:
        b.finalize_and_emit(block, sems)
    es.close()
    return nc


_NC_CACHE = None


def kernel(L0_attn_q, L0_attn_wq, L0_attn_wk, L0_attn_wv, L0_attn_wo,
           L0_ffn_w_gate, L0_ffn_w_up, L0_ffn_w_down):
    global _NC_CACHE
    x = np.ascontiguousarray(np.asarray(L0_attn_q, dtype=np.float32))
    ws = {
        "wq": np.ascontiguousarray(np.asarray(L0_attn_wq, dtype=np.float32)),
        "wk": np.ascontiguousarray(np.asarray(L0_attn_wk, dtype=np.float32)),
        "wv": np.ascontiguousarray(np.asarray(L0_attn_wv, dtype=np.float32)),
        "wo": np.ascontiguousarray(np.asarray(L0_attn_wo, dtype=np.float32)),
        "wg": np.ascontiguousarray(np.asarray(L0_ffn_w_gate, dtype=np.float32)),
        "wu": np.ascontiguousarray(np.asarray(L0_ffn_w_up, dtype=np.float32)),
        "wd": np.ascontiguousarray(np.asarray(L0_ffn_w_down, dtype=np.float32)),
    }
    B = x.shape[0]
    if _NC_CACHE is None:
        _NC_CACHE = build_nc()
    in_maps = []
    for b_ in range(B):
        for h in range(2):
            toks = np.concatenate([np.arange(1024) * 4 + 2 * h,
                                   np.arange(1024) * 4 + 2 * h + 1])
            m = dict(ws)
            m["x_full"] = x[b_]
            m["x_q"] = np.ascontiguousarray(x[b_][2048 * h:2048 * h + 2048])
            m["x_ffn"] = np.ascontiguousarray(x[b_][toks])
            in_maps.append(m)
    res = run_bass_kernel_spmd(_NC_CACHE, in_maps, core_ids=list(range(8)))
    out = np.empty((B, S, D), dtype=np.float32)
    for b_ in range(B):
        for h in range(2):
            o = np.asarray(res.results[2 * b_ + h]["out"])
            v = o.reshape(2, 1024, D)
            ob = out[b_].reshape(1024, 4, D)
            ob[:, 2 * h, :] = v[0]
            ob[:, 2 * h + 1, :] = v[1]
    return out


# version 18
# speedup vs baseline: 1.2870x; 1.0010x over previous
import os
import numpy as np
import concourse.bass as bass
import concourse.mybir as mybir
from concourse.bass_utils import run_bass_kernel_spmd

F32 = mybir.dt.float32
BF16 = mybir.dt.bfloat16
AF = mybir.ActivationFunctionType
ALU = mybir.AluOpType
AX = mybir.AxisListType

P = 128
S, D, I = 4096, 1024, 2816
SH = 2048
NDC = D // P
KCH = 512
NKCH = S // KCH
NQT = SH // P
ICHUNKS = [(0, 512), (512, 512), (1024, 512), (1536, 512), (2048, 512), (2560, 256)]
NIT = I // P

ENGINES = ("sync", "gpsimd", "vector", "scalar", "tensor")


class Op:
    __slots__ = ("eng", "fn", "waits", "inc_keys", "inc_vals", "drain_after",
                 "dma", "_small", "seq")

    def __init__(self, eng, fn, dma=False):
        self.eng = eng
        self.fn = fn
        self.waits = []
        self.inc_keys = set()
        self.inc_vals = {}
        self.drain_after = False
        self.dma = dma
        self._small = False
        self.seq = -1


class Builder:
    def __init__(self):
        self.ops = {e: [] for e in ENGINES}
        self.buf = {}
        self.pending_waits = {e: [] for e in ENGINES}
        self.seq = 0
        self.strict = os.environ.get("BASS_SIM_FP64") == "1"

    def semkey(self, op):
        return ("dma_" + op.eng) if op.dma else op.eng

    def _edge(self, prod, cons):
        if prod is cons:
            return
        if (prod.eng == cons.eng and not prod.dma and prod.eng != "gpsimd"
                and not self.strict):
            return
        key = self.semkey(prod)
        prod.inc_keys.add(key)
        if not prod.dma and prod._small:
            prod.drain_after = True
        cons.waits.append((key, prod))

    def op(self, eng, fn, reads=(), writes=(), dma=False, small=False):
        o = Op(eng, fn, dma=dma)
        o._small = small
        if dma:
            o.inc_keys.add(self.semkey(o))
        if small and not dma:
            o.drain_after = True
        for bnm in reads:
            st = self.buf.get(bnm)
            if st is not None and st[0] is not None:
                self._edge(st[0], o)
        for bnm in writes:
            st = self.buf.get(bnm)
            if st is not None:
                if st[0] is not None:
                    self._edge(st[0], o)
                for r in st[1]:
                    self._edge(r, o)
        for bnm in writes:
            self.buf[bnm] = [o, []]
        for bnm in reads:
            st = self.buf.setdefault(bnm, [None, []])
            st[1].append(o)
        for w in self.pending_waits[eng]:
            o.waits.append(w)
        self.pending_waits[eng] = []
        o.seq = self.seq
        self.seq += 1
        self.ops[eng].append(o)
        return o

    def barrier(self):
        lasts = []
        for e in ENGINES:
            last_d, last_c = None, None
            for o in reversed(self.ops[e]):
                if o.dma and last_d is None:
                    last_d = o
                if not o.dma and last_c is None:
                    last_c = o
                if last_d is not None and last_c is not None:
                    break
            for last in (last_d, last_c):
                if last is None:
                    continue
                key = self.semkey(last)
                last.inc_keys.add(key)
                if not last.dma and last._small:
                    last.drain_after = True
                lasts.append((e, key, last))
        for e in ENGINES:
            for e2, key, last in lasts:
                if e2 != e:
                    self.pending_waits[e].append((key, last))

    def finalize_and_emit(self, block, sems):
        strict_dma = os.environ.get("BASS_SIM_FP64") == "1"
        counts = {k: 0 for k in sems}
        for e in ENGINES:
            for o in self.ops[e]:
                for key in sorted(o.inc_keys):
                    counts[key] += 16 if key.startswith("dma_") else 1
                    o.inc_vals[key] = counts[key]
        dma_hist = {}
        if strict_dma:
            for e in ENGINES:
                for o in self.ops[e]:
                    for key, v in o.inc_vals.items():
                        if key.startswith("dma_"):
                            dma_hist.setdefault(key, []).append((o.seq, v))
            for key in dma_hist:
                dma_hist[key].sort()

        def strict_val(key, consumer_seq, base):
            import bisect
            hist = dma_hist.get(key, [])
            i = bisect.bisect_left(hist, (consumer_seq, -1))
            return hist[i - 1][1] if i > 0 else base

        def emit_stream(eng_name):
            def run(eng):
                for o in self.ops[eng_name]:
                    wmax = {}
                    for (key, prod) in o.waits:
                        v = prod.inc_vals[key]
                        if strict_dma and key.startswith("dma_"):
                            v = strict_val(key, o.seq, v)
                        wmax[key] = max(wmax.get(key, 0), v)
                    if strict_dma and o.dma:
                        k0 = "dma_" + o.eng
                        prev = o.inc_vals[k0] - 16
                        if prev > 0:
                            wmax[k0] = max(wmax.get(k0, 0), prev)
                    for key in sorted(wmax):
                        eng.wait_ge(sems[key], wmax[key])
                    instr = o.fn(eng) if o.fn is not None else None
                    if o.drain_after and not o.dma:
                        instr = eng.drain()
                    if o.inc_keys:
                        keys = sorted(o.inc_keys)
                        assert len(keys) == 1, keys
                        assert instr is not None
                        instr.then_inc(sems[keys[0]], 16 if o.dma else 1)
            return run

        block.sync(emit_stream("sync"))
        block.gpsimd(emit_stream("gpsimd"))
        block.vector(emit_stream("vector"))
        block.scalar(emit_stream("scalar"))
        block.tensor(emit_stream("tensor"))


class Carver:
    def __init__(self, tensor, total):
        self.t = tensor
        self.total = total
        self.off = 0

    def take(self, n):
        n = (n + 15) // 16 * 16
        ap = self.t[:, self.off:self.off + n]
        self.off += n
        assert self.off <= self.total, f"arena overflow {self.off} > {self.total}"
        return ap

    def reset(self):
        self.off = 0


def build_nc():
    nc = bass.Bass("TRN2", target_bir_lowering=False)

    x_full = nc.dram_tensor("x_full", [S, D], F32, kind="ExternalInput").ap()
    x_q = nc.dram_tensor("x_q", [SH, D], F32, kind="ExternalInput").ap()
    x_ffn = nc.dram_tensor("x_ffn", [SH, D], F32, kind="ExternalInput").ap()
    w_in = {}
    for nm, shp in (("wq", [D, D]), ("wk", [D, D]), ("wv", [D, D]), ("wo", [D, D]),
                    ("wg", [I, D]), ("wu", [I, D]), ("wd", [D, I])):
        w_in[nm] = nc.dram_tensor(nm, shp, F32, kind="ExternalInput").ap()
    out_d = nc.dram_tensor("out", [SH, D], F32, kind="ExternalOutput").ap()

    kt_hi = nc.dram_tensor("kt_hi", [D, S], BF16, kind="Internal").ap()
    kt_lo = nc.dram_tensor("kt_lo", [D, S], BF16, kind="Internal").ap()
    v_hi = nc.dram_tensor("v_hi", [S, D], BF16, kind="Internal").ap()
    v_lo = nc.dram_tensor("v_lo", [S, D], BF16, kind="Internal").ap()
    qt_hi = nc.dram_tensor("qt_hi", [D, SH], BF16, kind="Internal").ap()
    qt_lo = nc.dram_tensor("qt_lo", [D, SH], BF16, kind="Internal").ap()
    fin_d = nc.dram_tensor("fin_d", [SH, D], F32, kind="Internal").ap()
    wgq_hi = nc.dram_tensor("wgq_hi", [D, I], BF16, kind="Internal").ap()
    wgq_lo = nc.dram_tensor("wgq_lo", [D, I], BF16, kind="Internal").ap()
    wuq_hi = nc.dram_tensor("wuq_hi", [D, I], BF16, kind="Internal").ap()
    wuq_lo = nc.dram_tensor("wuq_lo", [D, I], BF16, kind="Internal").ap()
    wdq_hi = nc.dram_tensor("wdq_hi", [I, D], BF16, kind="Internal").ap()
    wdq_lo = nc.dram_tensor("wdq_lo", [I, D], BF16, kind="Internal").ap()

    from contextlib import ExitStack
    es = ExitStack()

    A32E = 28 * 1024 - 384
    A16E = 48 * 1024
    a32t = es.enter_context(nc.sbuf_tensor("a32", [P, A32E], F32))
    a16t = es.enter_context(nc.sbuf_tensor("a16", [P, A16E], BF16))
    ident = es.enter_context(nc.sbuf_tensor("ident", [P, P], F32))
    smalls = es.enter_context(nc.sbuf_tensor("smalls", [P, 176], F32))

    c0 = nc.const_aps.tensor(0.0, [P, 1], F32)
    c1 = nc.const_aps.tensor(1.0, [P, 1], F32)
    c_regs = [(c0, 0.0), (c1, 1.0)]
    for i, v in enumerate((0.125, 1e-5)):
        t = es.enter_context(nc.sbuf_tensor("c_%d" % i, [P, 1], F32))
        nc.const_aps.register(F32, v, t[:])
        c_regs.append((t[:], v))

    sems = {}
    for e in ENGINES:
        sems[e] = es.enter_context(nc.semaphore("s_" + e))
        sems["dma_" + e] = es.enter_context(nc.semaphore("sd_" + e))

    ps = [es.enter_context(nc.psum_tensor("ps%d" % i, [P, 512], F32))
          for i in range(8)]

    b = Builder()

    def sm(i, k=1):
        return smalls[:, i:i + k]

    r6 = sm(0)
    lnp = [[sm(2 + i) for i in range(5)] + [sm(104 + i) for i in range(5)],
           [sm(109 + i) for i in range(10)]]
    lb = [[sm(120 + 8 * pr + 4 * i, 4) for i in range(2)] for pr in range(2)]
    lb2 = [[sm(136 + 12 * pr + 4 * i, 4) for i in range(3)] for pr in range(2)]
    fl_cm = [sm(12), sm(13)]
    fl_cm8 = [sm(14), sm(15)]
    fl_al = [sm(16), sm(17)]
    fl_rs = [sm(18), sm(19)]
    fl_nm = [sm(100), sm(101)]
    fl_tmp = [sm(102), sm(103)]
    fl_mA = [sm(20 + t) for t in range(NQT)]
    fl_mB = [sm(36 + t) for t in range(NQT)]
    fl_lA = [sm(52 + t) for t in range(NQT)]
    fl_lB = [sm(68 + t) for t in range(NQT)]
    fl_linv = [sm(84 + t) for t in range(NQT)]

    b.op("gpsimd", lambda g: g.memset(ident[:], 0.0), writes=["identm"])

    def prolog(g):
        r = g.affine_select(out=ident[:], in_=ident[:], compare_op=ALU.not_equal,
                            fill=1.0, base=0, pattern=[[-1, P]],
                            channel_multiplier=1)
        for ap, v in c_regs:
            r = g.memset(ap, v)
        return g.memset(smalls[:], 0.0)

    b.op("gpsimd", prolog, reads=["identm"], writes=["ident", "consts", "smalls_i"])
    b.op("gpsimd", lambda g: g.memset(r6, 6.0), reads=["smalls_i"], writes=["r6raw"])
    b.op("vector", lambda v: v.reciprocal(sm(1), r6), reads=["r6raw"],
         writes=["r6t"], small=True)
    b.op("vector", lambda v: v.tensor_copy(r6, sm(1)), reads=["r6t"],
         writes=["r6"], small=True)
    b.barrier()

    def emit_ln(xb, xname, xnb, xnname, scr, scrname, pr=0):
        t_sum, t_ssq, t_m, t_e2, t_m2, t_var, t_sd, t_rstd, t_nm, t_nmr = lnp[pr]
        q = f"_{pr}"
        b.op("vector", lambda v: v.reduce_sum(t_sum, xb, axis=AX.X),
             reads=[xname], writes=["t_sum" + q], small=True)
        b.op("scalar", lambda s: s.activation(scr, xb, AF.Square, accum_out=t_ssq),
             reads=[xname], writes=[scrname, "t_ssq" + q], small=True)
        b.op("vector", lambda v: v.tensor_scalar(t_m, t_sum, 1.0 / D, None, ALU.mult),
             reads=["t_sum" + q], writes=["t_m" + q], small=True)
        b.op("vector", lambda v: v.tensor_scalar(t_e2, t_ssq, 1.0 / D, None, ALU.mult),
             reads=["t_ssq" + q], writes=["t_e2" + q], small=True)
        b.op("vector", lambda v: v.tensor_tensor(t_m2, t_m, t_m, ALU.mult),
             reads=["t_m" + q], writes=["t_m2" + q], small=True)
        b.op("vector", lambda v: v.tensor_tensor(t_var, t_e2, t_m2, ALU.subtract),
             reads=["t_e2" + q, "t_m2" + q], writes=["t_var" + q], small=True)
        b.op("scalar", lambda s: s.activation(t_sd, t_var, AF.Sqrt, bias=1e-5),
             reads=["t_var" + q], writes=["t_sd" + q], small=True)
        b.op("vector", lambda v: v.reciprocal(t_rstd, t_sd),
             reads=["t_sd" + q], writes=["t_rstd" + q], small=True)
        b.op("vector", lambda v: v.tensor_tensor(t_nm, t_m, t_rstd, ALU.mult),
             reads=["t_m" + q, "t_rstd" + q], writes=["t_nm" + q], small=True)
        b.op("vector", lambda v: v.tensor_scalar(t_nmr, t_nm, -1.0, None, ALU.mult),
             reads=["t_nm" + q], writes=["t_nmr" + q], small=True)
        b.op("scalar", lambda s: s.activation(xnb, xb, AF.Identity,
                                              bias=t_nmr, scale=t_rstd),
             reads=[xname, "t_nmr" + q, "t_rstd" + q], writes=[xnname])

    def emit_transpose_split(src, src_name, nch, dst_fn, dst_names, tag):
        for g in range(0, nch, 4):
            n = min(4, nch - g)
            bi = 6 + (g // 4) % 2
            bank = ps[bi]
            pname = f"psb{bi}"

            def mk_tp(g=g, n=n, bank=bank):
                def f(te):
                    r = None
                    for j in range(n):
                        r = te.transpose(bank[:, 128 * j:128 * (j + 1)],
                                         src[:, 128 * (g + j):128 * (g + j + 1)],
                                         ident[:])
                    return r
                return f
            b.op("tensor", mk_tp(), reads=[src_name, "ident"], writes=[pname])

            def mk_hi(g=g, n=n, bank=bank):
                def f(se):
                    r = None
                    for j in range(n):
                        hi, _ = dst_fn(g + j)
                        r = se.copy(hi, bank[:, 128 * j:128 * (j + 1)])
                    return r
                return f

            def mk_lo(g=g, n=n, bank=bank):
                def f(v):
                    r = None
                    for j in range(n):
                        hi, lo = dst_fn(g + j)
                        r = v.tensor_tensor(lo, bank[:, 128 * j:128 * (j + 1)],
                                            hi.bitcast(BF16), ALU.subtract)
                    return r
                return f
            b.op("scalar", mk_hi(), reads=[pname], writes=dst_names)
            b.op("vector", mk_lo(), reads=[pname] + dst_names, writes=dst_names)

    def emit_quant(xbuf, xq_out, F, scr, names_in, name_out, tag, scrtag=None):
        tag = scrtag or tag
        NB = F // 16
        absx, u, acc, cmp_t, w1 = (scr[k] for k in
                                   ("absx", "u", "acc", "cmp", "t1"))
        w2 = absx
        s_s, s_sc, s_thr = scr["s"], scr["sc"], scr["thr"]
        b.op("scalar", lambda s: s.activation(absx, xbuf, AF.Abs),
             reads=names_in, writes=[f"{tag}absx"])
        b.op("vector", lambda v: v.tensor_reduce(
            s_s, absx.rearrange("p (nb k) -> p nb k", k=16), axis=AX.X, op=ALU.max),
            reads=[f"{tag}absx"], writes=[f"{tag}s"], small=True)
        b.op("vector", lambda v: v.tensor_tensor(s_thr, s_s, r6.broadcast_to([P, NB]),
                                                 ALU.mult),
             reads=[f"{tag}s", "r6"], writes=[f"{tag}sct"], small=True)
        b.op("vector", lambda v: v.tensor_scalar(s_sc, s_thr, 1e-12, None, ALU.max),
             reads=[f"{tag}sct"], writes=[f"{tag}sc"], small=True)
        b.op("vector", lambda v: v.reciprocal(s_thr, s_sc),
             reads=[f"{tag}sc"], writes=[f"{tag}thr"], small=True)
        b.op("vector", lambda v: v.tensor_tensor(
            u.rearrange("p (nb k) -> p nb k", k=16),
            absx.rearrange("p (nb k) -> p nb k", k=16),
            s_thr.unsqueeze(2).broadcast_to([P, NB, 16]), ALU.mult),
            reads=[f"{tag}absx", f"{tag}thr"], writes=[f"{tag}u"])
        uu = f"{tag}u"
        b.op("vector", lambda v: v.tensor_scalar(acc, u, 0.25, None, ALU.is_gt),
             reads=[uu], writes=[f"{tag}acc"])
        b.op("vector", lambda v: v.scalar_tensor_tensor(cmp_t, u, 0.75, acc,
                                                        ALU.is_gt, ALU.add),
             reads=[uu, f"{tag}acc"], writes=[f"{tag}cmp"])
        b.op("vector", lambda v: v.scalar_tensor_tensor(acc, u, 1.25, cmp_t,
                                                        ALU.is_gt, ALU.add),
             reads=[uu, f"{tag}cmp"], writes=[f"{tag}acc"])
        b.op("vector", lambda v: v.scalar_tensor_tensor(cmp_t, u, 1.75, acc,
                                                        ALU.is_gt, ALU.add),
             reads=[uu, f"{tag}acc"], writes=[f"{tag}cmp"])
        b.op("vector", lambda v: v.tensor_scalar(w1, u, 2.5, None, ALU.is_gt),
             reads=[uu], writes=[f"{tag}w1"])
        b.op("vector", lambda v: v.scalar_tensor_tensor(w2, u, 3.5, w1,
                                                        ALU.is_gt, ALU.add),
             reads=[uu, f"{tag}w1"], writes=[f"{tag}absx"])
        b.op("vector", lambda v: v.scalar_tensor_tensor(acc, cmp_t, 0.5, w2,
                                                        ALU.mult, ALU.add),
             reads=[f"{tag}cmp", f"{tag}absx"], writes=[f"{tag}acc"])
        b.op("vector", lambda v: v.tensor_scalar(w1, u, 5.0, None, ALU.is_gt),
             reads=[uu], writes=[f"{tag}w1"])
        b.op("vector", lambda v: v.scalar_tensor_tensor(cmp_t, w1, 2.0, acc,
                                                        ALU.mult, ALU.add),
             reads=[f"{tag}w1", f"{tag}acc"], writes=[f"{tag}cmp"])
        b.op("vector", lambda v: v.tensor_tensor(
            u.rearrange("p (nb k) -> p nb k", k=16),
            cmp_t.rearrange("p (nb k) -> p nb k", k=16),
            s_sc.unsqueeze(2).broadcast_to([P, NB, 16]), ALU.mult),
            reads=[f"{tag}cmp", f"{tag}sc"], writes=[f"{tag}u"])
        b.op("vector", lambda v: v.tensor_scalar(w1, xbuf, 0.0, 2.0,
                                                 ALU.is_ge, ALU.mult),
             reads=names_in, writes=[f"{tag}w1"])
        b.op("vector", lambda v: v.scalar_tensor_tensor(xq_out, w1, -1.0, u,
                                                        ALU.add, ALU.mult),
             reads=[f"{tag}w1", f"{tag}u"], writes=[name_out])

    def emit_mm3(bank_ap, pname, pairs, reads, nfirst=True):
        def f(te):
            r = None
            n = len(pairs) * 3
            i = 0
            for (lh, ll, rh, rl) in pairs:
                for (a, c) in ((lh, rh), (lh, rl), (ll, rh)):
                    r = te.matmul(bank_ap, a, c, start=(i == 0), stop=(i == n - 1))
                    i += 1
            return r
        b.op("tensor", f, reads=list(reads), writes=[pname])

    c32 = Carver(a32t, A32E)
    c16 = Carver(a16t, A16E)
    wload = c32.take(8 * 1024)
    xtile = [c32.take(1024), c32.take(1024)]
    xntile = [c32.take(1024), c32.take(1024)]
    lnscr = [c32.take(1024), c32.take(1024)]
    wspl_hi = c16.take(8 * 1024)
    wspl_lo = c16.take(8 * 1024)
    wsplv_hi = c16.take(8 * 1024)
    wsplv_lo = c16.take(8 * 1024)
    win_hi = [c16.take(NDC * 512)]
    win_lo = [c16.take(NDC * 512)]
    st_hi = c16.take(NDC * 512)
    st_lo = c16.take(NDC * 512)

    def emit_weight_T_split(w_dram, tag, whi=None, wlo=None, wnm="wspl"):
        whi = wspl_hi if whi is None else whi
        wlo = wspl_lo if wlo is None else wlo
        for i in range(8):
            b.op("sync", (lambda i=i: lambda s: s.dma_start(
                out=wload[:, 1024 * i:1024 * (i + 1)],
                in_=w_dram[128 * i:128 * (i + 1), :]))(),
                writes=[f"wl{i}"], dma=True)
        for i in range(8):
            def dst(jd, i=i):
                o = 1024 * jd + 128 * i
                return (whi[:, o:o + 128], wlo[:, o:o + 128])
            emit_transpose_split(wload[:, 1024 * i:1024 * (i + 1)], f"wl{i}",
                                 8, dst, [wnm], f"{tag}t{i}")

    def p1_sweep(x_dram, nsup, mode, hi_dram, lo_dram, tag, hook=None):
        xt4 = [xtile[0], xtile[1], xntile[0], xntile[1]]
        for sup in range(nsup):
            wpar = 0
            winh, winl = win_hi[wpar], win_lo[wpar]
            wnm = "win%d" % wpar
            spr = sup % 2
            for tt in range(4):
                gt = sup * 4 + tt
                xb = xt4[tt]
                xnb = lnscr[tt % 2]
                xname = f"x4_{tt}"
                xnname = "lnscr%d" % (tt % 2)
                b.op("sync", (lambda gt=gt, xb=xb: lambda s: s.dma_start(
                    out=xb, in_=x_dram[128 * gt:128 * (gt + 1), :]))(),
                    writes=[xname], dma=True)
                emit_ln(xb, xname, xnb, xnname, xnb, xnname, pr=tt % 2)

                def dst(jd, tt=tt, winh=winh, winl=winl):
                    o = 512 * jd + 128 * tt
                    return (winh[:, o:o + 128], winl[:, o:o + 128])
                emit_transpose_split(xnb, xnname, 8, dst, [wnm], f"{tag}w{gt}")
            if mode in ("K", "Q", "KV"):
                for ie in range(8):
                    bank = ps[ie % 3]
                    pname = f"psk{ie % 3}"
                    pairs = [(wspl_hi[:, 1024 * jd + 128 * ie:1024 * jd + 128 * (ie + 1)],
                              wspl_lo[:, 1024 * jd + 128 * ie:1024 * jd + 128 * (ie + 1)],
                              winh[:, 512 * jd:512 * (jd + 1)],
                              winl[:, 512 * jd:512 * (jd + 1)]) for jd in range(NDC)]
                    emit_mm3(bank[:, :], pname, pairs, ["wspl", wnm])
                    b.op("scalar", (lambda bank=bank, ie=ie: lambda s: s.copy(
                        st_hi[:, 512 * ie:512 * (ie + 1)], bank[:, :]))(),
                        reads=[pname], writes=["stg"])
                    b.op("vector", (lambda bank=bank, ie=ie: lambda v: v.tensor_tensor(
                        st_lo[:, 512 * ie:512 * (ie + 1)], bank[:, :],
                        st_hi[:, 512 * ie:512 * (ie + 1)].bitcast(BF16),
                        ALU.subtract))(),
                        reads=[pname, "stg"], writes=["stg"])
                slen = SH if mode == "Q" else S
                for dram, stg in ((hi_dram, st_hi), (lo_dram, st_lo)):
                    b.op("gpsimd", (lambda dram=dram, stg=stg, sup=sup, slen=slen:
                                    lambda g: g.dma_start(
                        out=bass.AP(dram.tensor, 512 * sup,
                                    [[slen, P], [P * slen, 8], [1, 512]]),
                        in_=stg.rearrange("p (j n) -> p j n", n=512)))(),
                        reads=["stg"], writes=["std"], dma=True)
                if hook:
                    hook()
            if mode == "KV":
                for tt in range(4):
                    for eh in range(2):
                        bank = ps[(2 * tt + eh) % 3]
                        pname = f"psk{(2 * tt + eh) % 3}"
                        pairs = [(winh[:, 512 * jd + 128 * tt:512 * jd + 128 * (tt + 1)],
                                  winl[:, 512 * jd + 128 * tt:512 * jd + 128 * (tt + 1)],
                                  wsplv_hi[:, 1024 * jd + 512 * eh:1024 * jd + 512 * (eh + 1)],
                                  wsplv_lo[:, 1024 * jd + 512 * eh:1024 * jd + 512 * (eh + 1)])
                                 for jd in range(NDC)]
                        emit_mm3(bank[:, :], pname, pairs, ["wsplv", wnm])
                        o = 1024 * tt + 512 * eh
                        b.op("scalar", (lambda bank=bank, o=o: lambda s: s.copy(
                            st_hi.bitcast(BF16)[:, o:o + 512], bank[:, :]))(),
                            reads=[pname], writes=["stg"])
                        b.op("vector", (lambda bank=bank, o=o: lambda v: v.tensor_tensor(
                            st_lo.bitcast(BF16)[:, o:o + 512], bank[:, :],
                            st_hi.bitcast(BF16)[:, o:o + 512], ALU.subtract))(),
                            reads=[pname, "stg"], writes=["stg"])
                for dram, stg in ((v_hi, st_hi), (v_lo, st_lo)):
                    b.op("gpsimd", (lambda dram=dram, stg=stg, sup=sup:
                                    lambda g: g.dma_start(
                        out=bass.AP(dram.tensor, 512 * sup * D,
                                    [[D, P], [P * D, 4], [1, 1024]]),
                        in_=stg.rearrange("p (j n) -> p j n", n=1024)))(),
                        reads=["stg"], writes=["std"], dma=True)
            if hook:
                hook()

    pw_load = c32.take(2816)
    pw_q = c32.take(2816)
    pq_scr = {k: c32.take(1408) for k in ("absx", "u", "acc", "cmp", "t1")}
    pq_sm = {k: c32.take(96) for k in ("s", "sc", "thr")}
    pw_st_hi = st_hi[:, 0:NIT * 128]
    pw_st_lo = st_lo[:, 0:NIT * 128]

    def qscr(F):
        NB = F // 16
        d = {k: pq_scr[k][:, 0:F] for k in pq_scr}
        d.update({k: pq_sm[k][:, 0:NB] for k in pq_sm})
        return d

    prep_steps = []

    def prep_gu_step(w_dram, out_hi_d, out_lo_d, tag, ri):
        b.op("sync", (lambda ri=ri: lambda s: s.dma_start(
            out=pw_load[:, 0:1024], in_=w_dram[128 * ri:128 * (ri + 1), :]))(),
            writes=["pwl"], dma=True)
        emit_quant(pw_load[:, 0:1024], pw_q[:, 0:1024], 1024, qscr(1024),
                   ["pwl"], "pwq", f"{tag}q", scrtag="pwscr")

        def dst(jd):
            return (pw_st_hi[:, 128 * jd:128 * (jd + 1)],
                    pw_st_lo[:, 128 * jd:128 * (jd + 1)])
        emit_transpose_split(pw_q[:, 0:1024], "pwq", 8, dst,
                             ["stg"], f"{tag}t{ri}")
        for dram, stg in ((out_hi_d, pw_st_hi), (out_lo_d, pw_st_lo)):
            b.op("gpsimd", (lambda dram=dram, stg=stg, ri=ri: lambda g: g.dma_start(
                out=bass.AP(dram.tensor, 128 * ri,
                            [[I, P], [P * I, 8], [1, 128]]),
                in_=stg.rearrange("p (j n) -> p j n", n=128)[:, 0:8, :]))(),
                reads=["stg"], writes=[f"{tag}d"], dma=True)

    def prep_wd_step(rd):
        b.op("sync", (lambda rd=rd: lambda s: s.dma_start(
            out=pw_load[:, 0:2816], in_=w_in["wd"][128 * rd:128 * (rd + 1), :]))(),
            writes=["pwl"], dma=True)
        for hf in range(2):
            F = 1408
            emit_quant(pw_load[:, F * hf:F * (hf + 1)], pw_q[:, F * hf:F * (hf + 1)],
                       F, qscr(F), ["pwl"], "pwq", f"pdq{hf}", scrtag="pwscr")

        def dstD(ji):
            return (pw_st_hi[:, 128 * ji:128 * (ji + 1)],
                    pw_st_lo[:, 128 * ji:128 * (ji + 1)])
        emit_transpose_split(pw_q[:, 0:2816], "pwq", NIT, dstD, ["stg"],
                             f"pdt{rd}")
        for dram, stg in ((wdq_hi, pw_st_hi), (wdq_lo, pw_st_lo)):
            for half in range(2):
                n0 = 11 * half
                b.op("gpsimd", (lambda dram=dram, stg=stg, rd=rd, n0=n0:
                                lambda g: g.dma_start(
                    out=bass.AP(dram.tensor, 128 * n0 * D + 128 * rd,
                                [[D, P], [P * D, 11], [1, 128]]),
                    in_=stg.rearrange("p (j n) -> p j n", n=128)[:, n0:n0 + 11, :]))(),
                    reads=["stg"], writes=["pdd"], dma=True)

    for ri in range(NIT):
        prep_steps.append((lambda ri=ri: prep_gu_step(w_in["wg"], wgq_hi, wgq_lo, "pg", ri)))
    for ri in range(NIT):
        prep_steps.append((lambda ri=ri: prep_gu_step(w_in["wu"], wuq_hi, wuq_lo, "pu", ri)))
    for rd in range(8):
        prep_steps.append((lambda rd=rd: prep_wd_step(rd)))
    prep_iter = iter(prep_steps)

    def prep_hook(n=1):
        for _ in range(n):
            st = next(prep_iter, None)
            if st is not None:
                st()

    emit_weight_T_split(w_in["wk"], "k")
    emit_weight_T_split(w_in["wv"], "v", wsplv_hi, wsplv_lo, "wsplv")
    p1_sweep(x_full, 8, "KV", kt_hi, kt_lo, "f", hook=lambda: prep_hook(2))
    b.barrier()
    emit_weight_T_split(w_in["wq"], "q")
    p1_sweep(x_q, 4, "Q", qt_hi, qt_lo, "h", hook=lambda: prep_hook(1))
    while True:
        st = next(prep_iter, None)
        if st is None:
            break
        st()
    b.barrier()

    c32.reset()
    c16.reset()
    acc = c32.take(NQT * 1024)
    p_sb = [c32.take(512), c32.take(512)]
    av_tmp = c32.take(1024)
    av_tmp2 = c32.take(1024)
    kch_hi = [c16.take(NDC * 512), c16.take(NDC * 512)]
    kch_lo = [c16.take(NDC * 512), c16.take(NDC * 512)]
    vch_hi = [c16.take(4 * 1024), c16.take(4 * 1024)]
    vch_lo = [c16.take(4 * 1024), c16.take(4 * 1024)]
    qtb_hi = [c16.take(NDC * 128), c16.take(NDC * 128)]
    qtb_lo = [c16.take(NDC * 128), c16.take(NDC * 128)]
    at_hi = [c16.take(4 * 128), c16.take(4 * 128)]
    at_lo = [c16.take(4 * 128), c16.take(4 * 128)]

    def mk_3d_load(dram, dst, off, nblk, blklen, rowstride, bufstride=None):
        bs = bufstride or blklen

        def f(s):
            return s.dma_start(
                out=dst.rearrange("p (j n) -> p j n", n=bs)[:, 0:nblk, 0:blklen],
                in_=bass.AP(dram.tensor, off,
                            [[rowstride, P], [P * rowstride, nblk], [1, blklen]]))
        return f

    def p2_loads(c):
        cp = c % 2
        b.op("sync", mk_3d_load(kt_hi, kch_hi[cp], 512 * c, 8, 512, S),
             reads=["std"], writes=[f"kch{cp}"], dma=True)
        b.op("sync", mk_3d_load(kt_lo, kch_lo[cp], 512 * c, 8, 512, S),
             reads=["std"], writes=[f"kch{cp}"], dma=True)
        b.op("sync", mk_3d_load(v_hi, vch_hi[cp], 512 * c * D, 4, 1024, D),
             reads=["std"], writes=[f"vch{cp}"], dma=True)
        b.op("sync", mk_3d_load(v_lo, vch_lo[cp], 512 * c * D, 4, 1024, D),
             reads=["std"], writes=[f"vch{cp}"], dma=True)

    def p2_qt_load(t):
        tp = t % 2
        b.op("sync", mk_3d_load(qt_hi, qtb_hi[tp], 128 * t, 8, 128, SH),
             reads=["std"], writes=[f"qtb{tp}"], dma=True)
        b.op("sync", mk_3d_load(qt_lo, qtb_lo[tp], 128 * t, 8, 128, SH),
             reads=["std"], writes=[f"qtb{tp}"], dma=True)

    def p2_scores(c, t):
        cp, tp = c % 2, t % 2
        sbank = ps[tp]
        sname = f"pss{tp}"
        pairs = [(qtb_hi[tp][:, 128 * jd:128 * (jd + 1)],
                  qtb_lo[tp][:, 128 * jd:128 * (jd + 1)],
                  kch_hi[cp][:, 512 * jd:512 * (jd + 1)],
                  kch_lo[cp][:, 512 * jd:512 * (jd + 1)]) for jd in range(NDC)]
        emit_mm3(sbank[:, :], sname, pairs, [f"qtb{tp}", f"kch{cp}"])

    def p2_post(c, t):
        cp, tp = c % 2, t % 2
        sbank = ps[tp]
        sname = f"pss{tp}"
        b.op("vector", (lambda sbank=sbank, tp=tp: lambda v: v.reduce_max(
            fl_cm[tp], sbank[:, :], axis=AX.X))(),
            reads=[sname], writes=[f"fl_cm{tp}"], small=True)
        b.op("vector", (lambda tp=tp: lambda v: v.tensor_scalar(
            fl_cm8[tp], fl_cm[tp], 0.125, None, ALU.mult))(),
            reads=[f"fl_cm{tp}"], writes=[f"fl_cm8{tp}"], small=True)
        src_m = fl_mB[t] if c % 2 == 1 else fl_mA[t]
        dst_m = fl_mA[t] if c % 2 == 1 else fl_mB[t]
        if c == 0:
            b.op("vector", (lambda dst_m=dst_m, tp=tp: lambda v: v.tensor_copy(
                dst_m, fl_cm8[tp]))(),
                reads=[f"fl_cm8{tp}"], writes=[f"fl_m{t}"], small=True)
        else:
            b.op("vector", (lambda src_m=src_m, tp=tp: lambda v: v.tensor_tensor(
                fl_tmp[tp], src_m, fl_cm8[tp], ALU.max))(),
                reads=[f"fl_m{t}", f"fl_cm8{tp}"], writes=[f"fl_tmp{tp}"], small=True)
            b.op("vector", (lambda dst_m=dst_m, tp=tp: lambda v: v.tensor_copy(
                dst_m, fl_tmp[tp]))(),
                reads=[f"fl_tmp{tp}"], writes=[f"fl_m{t}"], small=True)
        cur_m = dst_m
        b.op("vector", (lambda cur_m=cur_m, tp=tp: lambda v: v.tensor_scalar(
            fl_nm[tp], cur_m, -1.0, None, ALU.mult))(),
            reads=[f"fl_m{t}"], writes=[f"fl_nm{tp}"], small=True)
        b.op("scalar", (lambda sbank=sbank, tp=tp: lambda s: s.activation(
            p_sb[tp], sbank[:, :], AF.Exp, bias=fl_nm[tp], scale=0.125,
            accum_out=fl_rs[tp]))(),
            reads=[sname, f"fl_nm{tp}"], writes=[f"p_sb{tp}", f"fl_rs{tp}"], small=True)
        if c == 0:
            b.op("vector", (lambda t=t, tp=tp: lambda v: v.tensor_copy(
                fl_lB[t], fl_rs[tp]))(),
                reads=[f"fl_rs{tp}"], writes=[f"fl_l{t}"], small=True)
        else:
            src_l = fl_lB[t] if c % 2 == 1 else fl_lA[t]
            dst_l = fl_lA[t] if c % 2 == 1 else fl_lB[t]
            b.op("vector", (lambda src_m=src_m, cur_m=cur_m, tp=tp: lambda v:
                            v.tensor_tensor(fl_tmp[tp], src_m, cur_m,
                                            ALU.subtract))(),
                 reads=[f"fl_m{t}"], writes=[f"fl_tmp{tp}"], small=True)
            b.op("scalar", (lambda tp=tp: lambda s: s.activation(
                fl_al[tp], fl_tmp[tp], AF.Exp))(),
                reads=[f"fl_tmp{tp}"], writes=[f"fl_al{tp}"], small=True)
            b.op("vector", (lambda src_l=src_l, tp=tp: lambda v: v.tensor_tensor(
                fl_tmp[tp], src_l, fl_al[tp], ALU.mult))(),
                reads=[f"fl_l{t}", f"fl_al{tp}"], writes=[f"fl_tmp{tp}"], small=True)
            b.op("vector", (lambda dst_l=dst_l, tp=tp: lambda v: v.tensor_tensor(
                dst_l, fl_tmp[tp], fl_rs[tp], ALU.add))(),
                reads=[f"fl_tmp{tp}", f"fl_rs{tp}"], writes=[f"fl_l{t}"], small=True)
        tbank = ps[2 + tp]
        tname = f"pst{tp}"
        b.op("tensor", (lambda tp=tp, tbank=tbank: lambda te: [
            te.transpose(tbank[:, 128 * j:128 * (j + 1)],
                         p_sb[tp][:, 128 * j:128 * (j + 1)], ident[:])
            for j in range(4)][-1])(),
            reads=[f"p_sb{tp}", "ident"], writes=[tname])
        b.op("scalar", (lambda tbank=tbank, tp=tp: lambda s: s.copy(
            at_hi[tp], tbank[:, :]))(),
            reads=[tname], writes=[f"at{tp}"])
        b.op("vector", (lambda tbank=tbank, tp=tp: lambda v: v.tensor_tensor(
            at_lo[tp], tbank[:, :], at_hi[tp], ALU.subtract))(),
            reads=[tname, f"at{tp}"], writes=[f"at{tp}"])
        for eh in range(2):
            abank = ps[4 + 2 * tp + eh]
            aname = f"psa{2 * tp + eh}"
            pairs = [(at_hi[tp][:, 128 * ks:128 * (ks + 1)],
                      at_lo[tp][:, 128 * ks:128 * (ks + 1)],
                      vch_hi[cp][:, 1024 * ks + 512 * eh:1024 * ks + 512 * (eh + 1)],
                      vch_lo[cp][:, 1024 * ks + 512 * eh:1024 * ks + 512 * (eh + 1)])
                     for ks in range(4)]
            emit_mm3(abank[:, :], aname, pairs, [f"at{tp}", f"vch{cp}"])
            aslice = acc[:, 1024 * t + 512 * eh:1024 * t + 512 * (eh + 1)]
            accn = f"acc{t}_{eh}"
            if c == 0:
                b.op("vector", (lambda abank=abank, aslice=aslice: lambda v:
                                v.tensor_copy(aslice, abank[:, :]))(),
                     reads=[aname], writes=[accn])
            else:
                tsl = (av_tmp if tp == 0 else av_tmp2)[:, 512 * eh:512 * (eh + 1)]
                b.op("scalar", (lambda aslice=aslice, tsl=tsl, tp=tp: lambda s:
                                s.activation(tsl, aslice, AF.Identity,
                                             scale=fl_al[tp]))(),
                     reads=[accn, f"fl_al{tp}"], writes=[f"avt{tp}{eh}"])
                b.op("vector", (lambda abank=abank, aslice=aslice, tsl=tsl:
                                lambda v: v.tensor_tensor(
                                    aslice, tsl, abank[:, :], ALU.add))(),
                     reads=[f"avt{tp}{eh}", aname], writes=[accn])

    p2_loads(0)
    p2_qt_load(0)
    for c in range(NKCH):
        for t in range(NQT):
            if t + 1 < NQT:
                p2_qt_load(t + 1)
            elif c + 1 < NKCH:
                p2_qt_load(0)
            p2_scores(c, t)
            if t == 8 and c + 1 < NKCH:
                p2_loads(c + 1)
            if t > 0:
                p2_post(c, t - 1)
        p2_post(c, NQT - 1)

    b.barrier()
    c16.reset()
    g_hi = c16.take(NQT * 1024)
    g_lo = c16.take(NQT * 1024)
    woT_hi = c16.take(8 * 1024)
    woT_lo = c16.take(8 * 1024)
    last_l = fl_lB if (NKCH - 1) % 2 == 0 else fl_lA
    for t in range(NQT):
        b.op("vector", (lambda t=t: lambda v: v.reciprocal(
            fl_linv[t], last_l[t]))(),
            reads=[f"fl_l{t}"], writes=[f"fl_linv{t}"], small=True)
        b.op("vector", (lambda t=t: lambda v: v.tensor_tensor(
            av_tmp, acc[:, 1024 * t:1024 * (t + 1)],
            fl_linv[t].broadcast_to([P, 1024]), ALU.mult))(),
            reads=[f"acc{t}_0", f"acc{t}_1", f"fl_linv{t}"], writes=["avn"])
        b.op("vector", (lambda t=t: lambda v: v.tensor_copy(
            g_hi[:, 1024 * t:1024 * (t + 1)], av_tmp))(),
            reads=["avn"], writes=["grhs"])
        b.op("vector", (lambda t=t: lambda v: v.tensor_tensor(
            g_lo[:, 1024 * t:1024 * (t + 1)], av_tmp,
            g_hi[:, 1024 * t:1024 * (t + 1)], ALU.subtract))(),
            reads=["avn", "grhs"], writes=["grhs"])

    b.barrier()
    c32.reset()
    wload2 = c32.take(8 * 1024)
    gtmp = c32.take(1024)
    finst = c32.take(1024)
    for i in range(8):
        b.op("sync", (lambda i=i: lambda s: s.dma_start(
            out=wload2[:, 1024 * i:1024 * (i + 1)],
            in_=w_in["wo"][128 * i:128 * (i + 1), :]))(),
            writes=[f"owl{i}"], dma=True)
    for i in range(8):
        def dst(jd, i=i):
            o = 1024 * jd + 128 * i
            return (woT_hi[:, o:o + 128], woT_lo[:, o:o + 128])
        emit_transpose_split(wload2[:, 1024 * i:1024 * (i + 1)], f"owl{i}",
                             8, dst, ["woT"], f"ot{i}")

    for r in range(2):
        for g in range(8):
            for ie in range(8):
                bank = ps[ie % 2]
                pname = f"psg{ie % 2}"
                pairs = [(woT_hi[:, 1024 * jd + 128 * ie:1024 * jd + 128 * (ie + 1)],
                          woT_lo[:, 1024 * jd + 128 * ie:1024 * jd + 128 * (ie + 1)],
                          g_hi[:, 1024 * (8 * r + jd) + 128 * g:
                               1024 * (8 * r + jd) + 128 * (g + 1)],
                          g_lo[:, 1024 * (8 * r + jd) + 128 * g:
                               1024 * (8 * r + jd) + 128 * (g + 1)])
                         for jd in range(NDC)]
                emit_mm3(bank[:, 0:128], pname, pairs, ["woT", "grhs"])
                b.op("vector", (lambda bank=bank, ie=ie: lambda v: v.tensor_copy(
                    gtmp[:, 128 * ie:128 * (ie + 1)], bank[:, 0:128]))(),
                    reads=[pname], writes=["gtmp"])
            for ie in range(0, 8, 4):
                bi = 2 + (ie // 4) % 2
                bank = ps[bi]
                pname = f"psq{bi}"
                b.op("tensor", (lambda ie=ie, bank=bank: lambda te: [
                    te.transpose(bank[:, 128 * j:128 * (j + 1)],
                                 gtmp[:, 128 * (ie + j):128 * (ie + j + 1)],
                                 ident[:])
                    for j in range(4)][-1])(),
                    reads=["gtmp", "ident"], writes=[pname])
                b.op("vector", (lambda ie=ie, bank=bank: lambda v: v.tensor_copy(
                    finst[:, 128 * ie:128 * (ie + 4)], bank[:, :]))(),
                    reads=[pname], writes=["finst"])
            gg = 8 * r + g
            b.op("gpsimd", (lambda gg=gg: lambda gp: gp.dma_start(
                out=fin_d[128 * gg:128 * (gg + 1), :], in_=finst))(),
                reads=["finst"], writes=["find"], dma=True)

    b.barrier()

    c32.reset()
    c16.reset()
    ffn_acc = c32.take(8 * 1024)
    fx = c32.take(1024)
    fxn = c32.take(1024)
    fq = c32.take(1024)
    f_lnscr = c32.take(1024)
    f_scr = {k: c32.take(1024) for k in ("absx", "u", "acc", "cmp", "t1", "t2")}
    f_sm = {k: c32.take(64) for k in ("s", "sc", "thr")}
    h_sb = c32.take(512)
    hq_sb = c32.take(512)
    silu_sb = c32.take(512)
    finld = c32.take(1024)
    res1 = c32.take(1024)
    dn_tmp = c32.take(1024)
    res2 = c32.take(1024)

    xqT_hi = c16.take(NDC * 1024)
    xqT_lo = c16.take(NDC * 1024)
    _wg1, _wg2 = c16.take(NDC * 512), c16.take(NDC * 512)
    _wu1, _wu2 = c16.take(NDC * 512), c16.take(NDC * 512)
    _wd1, _wd2 = c16.take(4 * 1024), c16.take(4 * 1024)
    wgc_hi, wgc_lo = [_wg1, _wg1], [_wg2, _wg2]
    wuc_hi, wuc_lo = [_wu1, _wu1], [_wu2, _wu2]
    wdc_hi, wdc_lo = [_wd1, _wd1], [_wd2, _wd2]
    hqT_hi = c16.take(4 * 128)
    hqT_lo = c16.take(4 * 128)

    def fscr(F):
        NB = F // 16
        d = {k: f_scr[k][:, 0:F] for k in f_scr}
        d.update({k: f_sm[k][:, 0:NB] for k in f_sm})
        return d

    for grp in range(2):
        for tt in range(8):
            gg = 8 * grp + tt
            b.op("sync", (lambda gg=gg: lambda s: s.dma_start(
                out=fx, in_=x_ffn[128 * gg:128 * (gg + 1), :]))(),
                writes=["fx"], dma=True)
            emit_ln(fx, "fx", fxn, "fxn", f_lnscr, "flnscr", pr=0)
            emit_quant(fxn, fq, 1024, fscr(1024), ["fxn"], "fq", "fq", scrtag="fscr")

            def dst(jd, tt=tt):
                o = 1024 * jd + 128 * tt
                return (xqT_hi[:, o:o + 128], xqT_lo[:, o:o + 128])
            emit_transpose_split(fq, "fq", 8, dst, ["xqT"], f"xt{gg}")

        def p3_gu(ci, io, ilen, tt):
            wp = 0
            gbank, ubank = ps[tt % 2], ps[2 + tt % 2]
            gname, uname = f"psG{tt % 2}", f"psU{tt % 2}"
            wsl = [(512 * jd, 512 * jd + ilen) for jd in range(NDC)]
            gpairs = [(xqT_hi[:, 1024 * jd + 128 * tt:1024 * jd + 128 * (tt + 1)],
                       xqT_lo[:, 1024 * jd + 128 * tt:1024 * jd + 128 * (tt + 1)],
                       wgc_hi[wp][:, a:z], wgc_lo[wp][:, a:z])
                      for jd, (a, z) in enumerate(wsl)]
            upairs = [(xqT_hi[:, 1024 * jd + 128 * tt:1024 * jd + 128 * (tt + 1)],
                       xqT_lo[:, 1024 * jd + 128 * tt:1024 * jd + 128 * (tt + 1)],
                       wuc_hi[wp][:, a:z], wuc_lo[wp][:, a:z])
                      for jd, (a, z) in enumerate(wsl)]
            emit_mm3(gbank[:, 0:ilen], gname, gpairs, ["xqT", f"wgc{wp}"])
            emit_mm3(ubank[:, 0:ilen], uname, upairs, ["xqT", f"wuc{wp}"])

        def p3_post(ci, io, ilen, tt):
            wp = 0
            nsub = ilen // 128
            gbank, ubank = ps[tt % 2], ps[2 + tt % 2]
            gname, uname = f"psG{tt % 2}", f"psU{tt % 2}"
            b.op("scalar", (lambda gbank=gbank, ilen=ilen: lambda s: s.activation(
                silu_sb[:, 0:ilen], gbank[:, 0:ilen], AF.Silu))(),
                reads=[gname], writes=["silu"])
            b.op("vector", (lambda ubank=ubank, ilen=ilen: lambda v: v.tensor_tensor(
                h_sb[:, 0:ilen], silu_sb[:, 0:ilen], ubank[:, 0:ilen], ALU.mult))(),
                reads=["silu", uname], writes=["h"])
            emit_quant(h_sb[:, 0:ilen], hq_sb[:, 0:ilen], ilen, fscr(ilen),
                       ["h"], "hq", "hq", scrtag="fscr")
            tb = ps[4 + tt % 2]
            tn = f"psH{tt % 2}"
            b.op("tensor", (lambda tb=tb, nsub=nsub: lambda te: [
                te.transpose(tb[:, 128 * j:128 * (j + 1)],
                             hq_sb[:, 128 * j:128 * (j + 1)], ident[:])
                for j in range(nsub)][-1])(),
                reads=["hq", "ident"], writes=[tn])
            b.op("scalar", (lambda tb=tb, nsub=nsub: lambda s: s.copy(
                hqT_hi[:, 0:128 * nsub], tb[:, 0:128 * nsub]))(),
                reads=[tn], writes=["hqT"])
            b.op("vector", (lambda tb=tb, nsub=nsub: lambda v: v.tensor_tensor(
                hqT_lo[:, 0:128 * nsub], tb[:, 0:128 * nsub],
                hqT_hi[:, 0:128 * nsub], ALU.subtract))(),
                reads=[tn, "hqT"], writes=["hqT"])
            for dh in range(2):
                dbank = ps[6 + (tt + dh) % 2]
                dname = f"psD{(tt + dh) % 2}"
                pairs = [(hqT_hi[:, 128 * ks:128 * (ks + 1)],
                          hqT_lo[:, 128 * ks:128 * (ks + 1)],
                          wdc_hi[wp][:, 1024 * ks + 512 * dh:
                                     1024 * ks + 512 * (dh + 1)],
                          wdc_lo[wp][:, 1024 * ks + 512 * dh:
                                     1024 * ks + 512 * (dh + 1)])
                         for ks in range(nsub)]
                emit_mm3(dbank[:, :], dname, pairs, ["hqT", f"wdc{wp}"])
                fslice = ffn_acc[:, 1024 * tt + 512 * dh:
                                 1024 * tt + 512 * (dh + 1)]
                fname = f"facc{tt}_{dh}"
                if ci == 0:
                    b.op("vector", (lambda dbank=dbank, fslice=fslice:
                                    lambda v: v.tensor_copy(fslice,
                                                            dbank[:, :]))(),
                         reads=[dname], writes=[fname])
                else:
                    tsl = dn_tmp[:, 512 * dh:512 * (dh + 1)]
                    b.op("scalar", (lambda fslice=fslice, tsl=tsl: lambda s:
                                    s.copy(tsl, fslice))(),
                         reads=[fname], writes=[f"dnt{dh}"])
                    b.op("vector", (lambda dbank=dbank, fslice=fslice,
                                    tsl=tsl: lambda v: v.tensor_tensor(
                                        fslice, tsl, dbank[:, :], ALU.add))(),
                         reads=[f"dnt{dh}", dname], writes=[fname])

        for ci, (io, ilen) in enumerate(ICHUNKS):
            wp = 0
            nsub = ilen // 128
            b.op("sync", mk_3d_load(wgq_hi, wgc_hi[wp], io, 8, ilen, I, 512),
                 reads=["pgd"], writes=[f"wgc{wp}"], dma=True)
            b.op("sync", mk_3d_load(wgq_lo, wgc_lo[wp], io, 8, ilen, I, 512),
                 reads=["pgd"], writes=[f"wgc{wp}"], dma=True)
            b.op("sync", mk_3d_load(wuq_hi, wuc_hi[wp], io, 8, ilen, I, 512),
                 reads=["pud"], writes=[f"wuc{wp}"], dma=True)
            b.op("sync", mk_3d_load(wuq_lo, wuc_lo[wp], io, 8, ilen, I, 512),
                 reads=["pud"], writes=[f"wuc{wp}"], dma=True)
            b.op("sync", mk_3d_load(wdq_hi, wdc_hi[wp], io * D, nsub, 1024, D),
                 reads=["pdd"], writes=[f"wdc{wp}"], dma=True)
            b.op("sync", mk_3d_load(wdq_lo, wdc_lo[wp], io * D, nsub, 1024, D),
                 reads=["pdd"], writes=[f"wdc{wp}"], dma=True)
            for tt in range(8):
                p3_gu(ci, io, ilen, tt)
                if tt > 0:
                    p3_post(ci, io, ilen, tt - 1)
            p3_post(ci, io, ilen, 7)
        for tt in range(8):
            gg = 8 * grp + tt
            b.op("sync", (lambda gg=gg: lambda s: s.dma_start(
                out=finld, in_=fin_d[128 * gg:128 * (gg + 1), :]))(),
                reads=["find"], writes=["finld"], dma=True)
            b.op("sync", (lambda gg=gg: lambda s: s.dma_start(
                out=fx, in_=x_ffn[128 * gg:128 * (gg + 1), :]))(),
                writes=["fx"], dma=True)
            b.op("vector", lambda v: v.tensor_tensor(res1, fx, finld, ALU.add),
                 reads=["fx", "finld"], writes=["res1"])
            b.op("vector", (lambda tt=tt: lambda v: v.tensor_tensor(
                res2, res1, ffn_acc[:, 1024 * tt:1024 * (tt + 1)], ALU.add))(),
                reads=["res1", f"facc{tt}_0", f"facc{tt}_1"], writes=["res2"])
            b.op("gpsimd", (lambda gg=gg: lambda g: g.dma_start(
                out=out_d[128 * gg:128 * (gg + 1), :], in_=res2))(),
                reads=["res2"], writes=["outd"], dma=True)
        if grp == 0:
            b.barrier()

    final_out = b.buf["outd"][0]
    key = "dma_gpsimd"
    final_out.inc_keys.add(key)
    wait_op = b.op("sync", None)
    wait_op.waits.append((key, final_out))

    with nc.Block() as block:
        b.finalize_and_emit(block, sems)
    es.close()
    return nc


_NC_CACHE = None


def kernel(L0_attn_q, L0_attn_wq, L0_attn_wk, L0_attn_wv, L0_attn_wo,
           L0_ffn_w_gate, L0_ffn_w_up, L0_ffn_w_down):
    global _NC_CACHE
    x = np.ascontiguousarray(np.asarray(L0_attn_q, dtype=np.float32))
    ws = {
        "wq": np.ascontiguousarray(np.asarray(L0_attn_wq, dtype=np.float32)),
        "wk": np.ascontiguousarray(np.asarray(L0_attn_wk, dtype=np.float32)),
        "wv": np.ascontiguousarray(np.asarray(L0_attn_wv, dtype=np.float32)),
        "wo": np.ascontiguousarray(np.asarray(L0_attn_wo, dtype=np.float32)),
        "wg": np.ascontiguousarray(np.asarray(L0_ffn_w_gate, dtype=np.float32)),
        "wu": np.ascontiguousarray(np.asarray(L0_ffn_w_up, dtype=np.float32)),
        "wd": np.ascontiguousarray(np.asarray(L0_ffn_w_down, dtype=np.float32)),
    }
    B = x.shape[0]
    if _NC_CACHE is None:
        _NC_CACHE = build_nc()
    in_maps = []
    for b_ in range(B):
        for h in range(2):
            toks = np.concatenate([np.arange(1024) * 4 + 2 * h,
                                   np.arange(1024) * 4 + 2 * h + 1])
            m = dict(ws)
            m["x_full"] = x[b_]
            m["x_q"] = np.ascontiguousarray(x[b_][2048 * h:2048 * h + 2048])
            m["x_ffn"] = np.ascontiguousarray(x[b_][toks])
            in_maps.append(m)
    res = run_bass_kernel_spmd(_NC_CACHE, in_maps, core_ids=list(range(8)))
    out = np.empty((B, S, D), dtype=np.float32)
    for b_ in range(B):
        for h in range(2):
            o = np.asarray(res.results[2 * b_ + h]["out"])
            v = o.reshape(2, 1024, D)
            ob = out[b_].reshape(1024, 4, D)
            ob[:, 2 * h, :] = v[0]
            ob[:, 2 * h + 1, :] = v[1]
    return out
